# version 15
# baseline (speedup 1.0000x reference)
import math
from contextlib import ExitStack

import numpy as np
import ml_dtypes

import concourse.bass as bass
import concourse.mybir as mybir
from concourse.bass_utils import run_bass_kernel_spmd

F32 = mybir.dt.float32
BF16 = mybir.dt.bfloat16
AF = mybir.ActivationFunctionType
ALU = mybir.AluOpType
AX = mybir.AxisListType
BF = ml_dtypes.bfloat16

D = 2048
S = 4096
TOK = 2048
T_OWN = 16
T_ALL = 32
KB = 16
HKV = 4
DH = 128
FDD = 768
FS = 6
NHH = 8
RD = 64
LAT = 512
IM = 1024
NE = 8
EPS_LN = 1e-5
EPS_RMS = 1e-5
EPS_ATT = 1e-8

_DEBUG = False



def _prepare(inp):
    f32 = np.float32
    x = np.asarray(inp["L0_attn_hidden_states"], f32)

    def T(a):
        return np.ascontiguousarray(np.asarray(a, f32).T)

    wkT = T(inp["L0_attn_Wk"])
    wvT = T(inp["L0_attn_Wv"])
    wmbT = T(inp["L0_attn_Wmb"])
    wqT = T(inp["L0_attn_Wq"])
    wgT = T(inp["L0_attn_Wg"])
    row = lambda v: np.ascontiguousarray(np.asarray(v, f32).reshape(1, -1))
    bk, bv, bmb, bq, bg = (row(inp[k]) for k in
                           ("L0_attn_bk", "L0_attn_bv", "L0_attn_bmb",
                            "L0_attn_bq", "L0_attn_bg"))

    mem = np.asarray(inp["L0_attn_memory"], f32)
    nrm = np.asarray(inp["L0_attn_norm"], f32)
    mc = np.concatenate([mem, nrm[:, :, None]], axis=2)
    mc = mc.reshape(16, FS, 128, 129).transpose(2, 0, 1, 3)
    memcat = np.ascontiguousarray(mc.reshape(128, 16 * FS * 129))

    ln1 = np.ascontiguousarray(np.asarray(inp["L0_ffn_ln1_w"], f32).reshape(KB, 128).T)
    ln2 = np.ascontiguousarray(np.asarray(inp["L0_ffn_ln2_w"], f32).reshape(KB, 128).T)

    kv_d_w = np.ascontiguousarray(np.asarray(inp["L0_ffn_kv_d_w"], f32))
    q_d_w = np.ascontiguousarray(np.asarray(inp["L0_ffn_q_d_w"], f32))
    k_u_w = np.asarray(inp["L0_ffn_k_u_w"], f32)
    v_u_w = np.asarray(inp["L0_ffn_v_u_w"], f32)
    q_u_w = np.asarray(inp["L0_ffn_q_u_w"], f32)
    rope_k_w = np.asarray(inp["L0_ffn_rope_k_w"], f32)
    rope_q_w = np.asarray(inp["L0_ffn_rope_q_w"], f32)
    o_w = np.ascontiguousarray(np.asarray(inp["L0_ffn_o_w"], f32))
    m_gate = np.ascontiguousarray(np.concatenate(
        [np.asarray(inp["L0_ffn_sh_gate"], f32)[None],
         np.asarray(inp["L0_ffn_r_gate"], f32)], 0))
    m_up = np.ascontiguousarray(np.concatenate(
        [np.asarray(inp["L0_ffn_sh_up"], f32)[None],
         np.asarray(inp["L0_ffn_r_up"], f32)], 0))
    m_down = np.ascontiguousarray(np.concatenate(
        [np.asarray(inp["L0_ffn_sh_down"], f32)[None],
         np.asarray(inp["L0_ffn_r_down"], f32)], 0))
    router_w = np.ascontiguousarray(np.asarray(inp["L0_ffn_router_w"], f32))
    router_b = row(inp["L0_ffn_router_b"])

    inv_freq = 1.0 / (10000.0 ** (np.arange(0, RD, 2, dtype=np.float64) / RD))
    t_ = np.arange(S, dtype=np.float64)
    ang = np.outer(t_, inv_freq)
    cos32 = np.cos(ang).astype(f32)
    sin32 = np.sin(ang).astype(f32)
    cos_rep = np.ascontiguousarray(np.tile(cos32, (1, NHH)))
    sin_rep = np.ascontiguousarray(np.tile(sin32, (1, NHH)))

    identity = np.eye(128, dtype=f32)
    tri = np.triu(np.ones((128, 128), f32))

    shared = dict(
        wkT=wkT, wvT=wvT, wmbT=wmbT, wqT=wqT, wgT=wgT,
        bk=bk, bv=bv, bmb=bmb, bq=bq, bg=bg, memcat=memcat,
        ln1=ln1, ln2=ln2, kv_d_w=kv_d_w, q_d_w=q_d_w, o_w=o_w,
        m_gate=m_gate, m_up=m_up, m_down=m_down,
        router_w=router_w, router_b=router_b,
        cos_rep=cos_rep, sin_rep=sin_rep, identity=identity, tri=tri,
    )

    in_maps = []
    for c in range(8):
        b, half = c // 2, c % 2
        m = dict(shared)
        m["x_all"] = np.ascontiguousarray(x[b])
        m["x_own"] = np.ascontiguousarray(x[b, half * 2048:(half + 1) * 2048])
        hs = slice(half * 512, (half + 1) * 512)
        m["k_u_my"] = np.ascontiguousarray(k_u_w[:, hs])
        m["q_u_my"] = np.ascontiguousarray(q_u_w[:, hs])
        m["rope_k_my"] = np.ascontiguousarray(rope_k_w[:, hs])
        m["rope_q_my"] = np.ascontiguousarray(rope_q_w[:, hs])
        m["v_u_my"] = np.ascontiguousarray(v_u_w[:, half * 1024:(half + 1) * 1024])
        m["selA"] = np.full((128, 1), 1.0 - half, f32)
        m["selB"] = np.full((128, 1), float(half), f32)
        in_maps.append(m)
    return in_maps



class Sem:

    def __init__(self, nc, ctx, name):
        self.h = ctx.enter_context(nc.semaphore(name))
        self.n = 0

    def inc(self, inst, k=1):
        inst.then_inc(self.h, k)
        self.n += k
        return self.n


class _RecInst:
    __slots__ = ("calls",)

    def __init__(self):
        self.calls = []

    def then_inc(self, *a, **kw):
        self.calls.append(("then_inc", a, kw))
        return self


class _RecEng:

    def __init__(self):
        self.ops = []

    def __getattr__(self, name):
        def call(*a, **kw):
            ri = _RecInst()
            self.ops.append((name, a, kw, ri))
            return ri
        return call


_ACT_SETS = None


def _act_set_ids(func):
    global _ACT_SETS
    if _ACT_SETS is None:
        from concourse.hw_specs import get_activation_tables
        _ACT_SETS = list(get_activation_tables("gen3").values())
    return frozenset(i for i, fs in enumerate(_ACT_SETS) if func in fs)


class DeferredBlock:

    _ENGS = ("sync", "gpsimd", "tensor", "scalar", "vector")

    def __init__(self, real, auto=()):
        self._real = real
        self._auto = auto
        self._pend = {n: [] for n in self._ENGS}

    def __getattr__(self, name):
        if name in DeferredBlock._ENGS:
            def dec(fn):
                rec = _RecEng()
                fn(rec)
                self._pend[name].extend(rec.ops)
                if name in self._auto:
                    self.flush_one(name)
                return None
            return dec
        raise AttributeError(name)

    SPACE_EVERY = 0
    SCALAR_CAP = 0

    def _emit(self, name, ops):
        k = DeferredBlock.SPACE_EVERY

        def section(eng, ops=tuple(ops)):
            cnt = 0
            for (m, a, kw, ri) in ops:
                inst = getattr(eng, m)(*a, **kw)
                for (_, ia, ikw) in ri.calls:
                    inst.then_inc(*ia, **ikw)
                cnt += 1
                if k and cnt % k == 0 and name in ("tensor", "scalar", "vector"):
                    eng.drain()
        getattr(self._real, name)(section)

    def flush_one(self, which):
        ops = self._pend[which]
        if not ops:
            return
        self._pend[which] = []
        if which != "scalar":
            self._emit(which, ops)
            return
        cap = DeferredBlock.SCALAR_CAP
        if cap:
            for i in range(0, len(ops), cap):
                self._emit("scalar", ops[i:i + cap])
            return
        chunk, allowed = [], None
        for op in ops:
            (m, a, kw, ri) = op
            if m == "activation":
                func = a[2] if len(a) > 2 else kw.get("func")
                ids = _act_set_ids(func)
                if allowed is None:
                    allowed = ids
                else:
                    inter = allowed & ids
                    if not inter:
                        self._emit("scalar", chunk)
                        chunk, allowed = [], ids
                    else:
                        allowed = inter
            chunk.append(op)
        if chunk:
            self._emit("scalar", chunk)

    FLUSH_ORDER = ("sync", "gpsimd", "tensor", "scalar", "vector")

    def flush(self):
        for name in DeferredBlock.FLUSH_ORDER:
            self.flush_one(name)


class Slots:

    def __init__(self, nc, ctx, name, n):
        self.sems = [Sem(nc, ctx, f"{name}{i}") for i in range(n)]
        self.nslots = n

    def load(self, eng, i, out, in_, **kw):
        s = self.sems[i % self.nslots]
        eng.dma_start(out=out, in_=in_, **kw).then_inc(s.h, 16)
        s.n += 16
        return (i % self.nslots, s.n)

    def wait(self, eng, tok):
        i, v = tok
        eng.wait_ge(self.sems[i].h, v)




def _finish_early(nc, dbg, block, out_own, g, OST, SYO, MIR, GDO, mir_total, scopes):
    T_OWN_ = T_OWN

    @block.sync
    def _(sy):
        for t in range(T_OWN_):
            s_ = OST.sems[t % 2]
            if t >= 2:
                sy.wait_ge(s_.h, s_.n)
            sy.dma_start(out=out_own[t * 128:(t + 1) * 128, :],
                         in_=g["x_own"][t * 128:(t + 1) * 128, :]).then_inc(s_.h, 16)
            s_.n += 16
        for s_ in OST.sems:
            sy.wait_ge(s_.h, s_.n)
        sy.wait_ge(SYO.h, SYO.n)
        sy.wait_ge(MIR.h, mir_total)

    @block.gpsimd
    def _(gp):
        gp.wait_ge(GDO.h, GDO.n)
    block.flush()
    for sc in reversed(scopes):
        sc.close()
    return nc, dbg



def _build(debug=False, race_detect=False, nranks=8, phases=9,
           PA_FLUSH_A=False, PA_FLUSH_P=False, PA_SPLIT=(), PA_SKIP=(),
           PA_VAR=(), AUTOFLUSH=False):
    nc = bass.Bass("TRN2", target_bir_lowering=False,
                   detect_race_conditions=race_detect)
    g = {}

    def din(name, shape, dt=F32):
        g[name] = nc.dram_tensor(name, shape, dt, kind="ExternalInput").ap()
        return g[name]

    din("x_all", [S, D]); din("x_own", [TOK, D])
    din("wkT", [D, 512]); din("wvT", [D, 512]); din("wmbT", [D, D])
    din("wqT", [D, D]); din("wgT", [D, D])
    din("bk", [1, 512]); din("bv", [1, 512]); din("bmb", [1, D])
    din("bq", [1, D]); din("bg", [1, D])
    din("memcat", [128, 16 * FS * 129])
    din("ln1", [128, KB]); din("ln2", [128, KB])
    din("kv_d_w", [D, LAT]); din("q_d_w", [D, LAT])
    din("k_u_my", [LAT, 512]); din("q_u_my", [LAT, 512]); din("v_u_my", [LAT, 1024])
    din("rope_k_my", [D, 512]); din("rope_q_my", [LAT, 512])
    din("o_w", [D, D])
    din("m_gate", [NE, D, IM]); din("m_up", [NE, D, IM]); din("m_down", [NE, IM, D])
    din("router_w", [D, 7]); din("router_b", [1, 7])
    din("cos_rep", [S, 256]); din("sin_rep", [S, 256])
    din("identity", [128, 128]); din("tri", [128, 128])
    din("selA", [128, 1]); din("selB", [128, 1])

    out_own = nc.dram_tensor("out_own", [TOK, D], F32, kind="ExternalOutput").ap()

    scr_xnT = nc.dram_tensor("scr_xnT", [T_OWN * 128, KB * 128], BF16).ap()
    scr_xnT_all = nc.dram_tensor("scr_xnT_all", [2 * T_OWN * 128, KB * 128], BF16).ap()
    scr_r1 = nc.dram_tensor("scr_r1", [128, T_OWN], F32).ap()
    wq_mir = nc.dram_tensor("wq_mir", [D, D], BF16).ap()
    scr_r1_all = nc.dram_tensor("scr_r1_all", [2 * 128, T_OWN], F32).ap()
    scr_kT = nc.dram_tensor("scr_kT", [512, S], BF16).ap()
    scr_qT = nc.dram_tensor("scr_qT", [512, S], BF16).ap()
    ar_in_t = nc.dram_tensor("ar_in", [128, 16 * FS * 129], F32)
    ar_out_t = nc.dram_tensor("ar_out", [128, 16 * FS * 129], F32)
    ag_in_t = nc.dram_tensor("ag_in", [NHH * 128, S], BF16)
    ag_out_t = nc.dram_tensor("ag_out", [2 * NHH * 128, S], BF16)
    scr_mh = nc.dram_tensor("scr_mh", [TOK, D], F32).ap()
    scr_h = nc.dram_tensor("scr_h", [TOK, D], F32).ap()
    scr_g = nc.dram_tensor("scr_g", [TOK, D], F32).ap()
    scr_heT = nc.dram_tensor("scr_heT", [NE, T_OWN * 128, IM], BF16).ap()
    mir_down2 = nc.dram_tensor("mir_down2", [NE, IM, D], BF16).ap()
    scr_moe1 = nc.dram_tensor("scr_moe1", [TOK, D], F32).ap()

    dbg = {}

    def dout(name, shape, dt=F32):
        dbg[name] = nc.dram_tensor(name, shape, dt, kind="ExternalOutput").ap()
        return dbg[name]

    with ExitStack() as top:
        ent = top.enter_context
        PE = Sem(nc, top, "PE")
        VE = Sem(nc, top, "VE")
        SE = Sem(nc, top, "SE")
        GP = Sem(nc, top, "GP")
        CC = Sem(nc, top, "CC")
        PRE = Sem(nc, top, "PRE")
        MIR = Sem(nc, top, "MIR")
        SYO = Sem(nc, top, "SYO")
        GDO = Sem(nc, top, "GDO")
        XSP = Sem(nc, top, "XSP")
        OST = Slots(nc, top, "ost", 2)
        xls = Slots(nc, top, "p0x", 2)
        wgrp = Slots(nc, top, "pawg", 2)
        pgrp = Slots(nc, top, "papg", 2)
        DEL = Slots(nc, top, "del", 2)

        id_bf = ent(nc.sbuf_tensor("id_bf", [128, 128], BF16))
        tri_bf = ent(nc.sbuf_tensor("tri_bf", [128, 128], BF16))
        ones_r = ent(nc.sbuf_tensor("ones_r", [1, 128], BF16))
        ln1_sb = ent(nc.sbuf_tensor("ln1_sb", [128, KB], F32))
        ln2_sb = ent(nc.sbuf_tensor("ln2_sb", [128, KB], F32))
        sel_sb = ent(nc.sbuf_tensor("sel_sb", [128, 2], F32))
        rstd_own = ent(nc.sbuf_tensor("rstd_own", [128, T_OWN], F32))
        nmr_own = ent(nc.sbuf_tensor("nmr_own", [128, T_OWN], F32))
        rstd1_own = ent(nc.sbuf_tensor("rstd1_own", [128, T_OWN], F32))
        rstd2_own = ent(nc.sbuf_tensor("rstd2_own", [128, T_OWN], F32))
        consts = ent(nc.sbuf_tensor("consts", [128, 2], F32))
        EPSC_LN = consts[:, 0:1]
        EPSC_ATT = consts[:, 1:2]

        block_real = ent(nc.Block())
        block = DeferredBlock(block_real, auto=AUTOFLUSH if AUTOFLUSH else ("sync", "gpsimd", "scalar"))

        @block.gpsimd
        def _(gp):
            gp.dma_start(out=id_bf[:, :], in_=g["identity"]).then_inc(PRE.h, 16)
            gp.dma_start(out=tri_bf[:, :], in_=g["tri"]).then_inc(PRE.h, 16)
            gp.dma_start(out=ln1_sb[:, :], in_=g["ln1"]).then_inc(PRE.h, 16)
            gp.dma_start(out=ln2_sb[:, :], in_=g["ln2"]).then_inc(PRE.h, 16)
            gp.dma_start(out=sel_sb[:, 0:1], in_=g["selA"]).then_inc(PRE.h, 16)
            gp.dma_start(out=sel_sb[:, 1:2], in_=g["selB"]).then_inc(PRE.h, 16)
            PRE.n += 6 * 16
            gp.memset(ones_r[:, :], 1.0)
            gp.memset(consts[:, 0:1], EPS_LN)
            GP.inc(gp.memset(consts[:, 1:2], EPS_ATT))
        pre_total = PRE.n
        consts_mark = GP.n

        mir_total = MIR.n
        block.flush()

        scopes = []
        xnT_scope = ExitStack()
        scopes.append(xnT_scope)
        xnT = xnT_scope.enter_context(
            nc.sbuf_tensor("xnT", [128, KB * T_OWN * 128], BF16))

        def ch(eng, sem, inst):
            sem.inc(inst)
            eng.wait_ge(sem.h, sem.n)

        def emit_rsqrt_newton(v, s, out_ap, vpe_ap, scr, mode="lnexp"):
            if mode == "fake":
                @block.vector
                def _(v_):
                    v_.wait_ge(SE.h, SE.n)
                    VE.inc(v_.reciprocal(out_ap, vpe_ap))
                return

            @block.scalar
            def _(s_):
                s_.wait_ge(VE.h, VE.n)
                s_.wait_ge(SE.h, SE.n)
                ch(s_, SE, s_.activation(scr, vpe_ap, AF.Ln))
                SE.inc(s_.activation(out_ap, scr, AF.Exp, scale=-0.5))

            @block.vector
            def _(v_):
                v_.wait_ge(SE.h, SE.n)
                ch(v_, VE, v_.tensor_mul(scr, out_ap, out_ap))
                ch(v_, VE, v_.tensor_mul(scr, scr, vpe_ap))
                ch(v_, VE, v_.tensor_scalar(scr, scr, -0.5, 1.5, ALU.mult, ALU.add))
                VE.inc(v_.tensor_mul(out_ap, out_ap, scr))

        def emit_ln_stats_to(dst_rstd, dst_nmr, dst_rstd1, xf_ap, st1, sq_scr,
                             first=False):
            @block.vector
            def _(v):
                VE.inc(v.reduce_sum(st1[:, 0:1], xf_ap, axis=AX.X))
            red_mark = VE.n

            @block.scalar
            def _(s):
                if first:
                    s.wait_ge(GP.h, consts_mark)
                s.activation(sq_scr, xf_ap, AF.Square,
                             accum_out=st1[:, 1:2])
                SE.inc(s.drain())
            sq_mark = SE.n

            @block.vector
            def _(v):
                v.wait_ge(SE.h, SE.n)
                v.wait_ge(VE.h, red_mark)
                ch(v, VE, v.tensor_scalar_mul(st1[:, 2:3], st1[:, 0:1], 1.0 / D))
                ch(v, VE, v.tensor_mul(st1[:, 3:4], st1[:, 2:3], st1[:, 2:3]))
                ch(v, VE, v.scalar_tensor_tensor(st1[:, 4:5], st1[:, 1:2], 1.0 / D,
                                                 st1[:, 3:4], ALU.mult, ALU.subtract))
                VE.inc(v.tensor_scalar_add(st1[:, 5:6], st1[:, 4:5], EPS_LN))

            emit_rsqrt_newton(None, None, dst_rstd, st1[:, 5:6], st1[:, 6:7])

            @block.vector
            def _(v):
                r = dst_rstd
                v.wait_ge(VE.h, VE.n)
                ch(v, VE, v.tensor_scalar(dst_nmr, st1[:, 2:3], r,
                                          -1.0, ALU.mult, ALU.mult))
                ch(v, VE, v.tensor_mul(st1[:, 5:6], st1[:, 4:5], r))
                ch(v, VE, v.tensor_mul(st1[:, 5:6], st1[:, 5:6], r))
                VE.inc(v.tensor_scalar_add(st1[:, 5:6], st1[:, 5:6], EPS_RMS))

            emit_rsqrt_newton(None, None, dst_rstd1, st1[:, 5:6], st1[:, 6:7])
            return sq_mark

        def emit_ln_stats(t, xf_ap, st1, sq_scr, first=False):
            return emit_ln_stats_to(rstd_own[:, t:t + 1], nmr_own[:, t:t + 1],
                                    rstd1_own[:, t:t + 1], xf_ap, st1, sq_scr,
                                    first=first)

        with ExitStack() as ph:
            e0 = ph.enter_context
            xf = [e0(nc.sbuf_tensor(f"p0_xf{i}", [128, D], F32)) for i in range(2)]
            xnbf = [e0(nc.sbuf_tensor(f"p0_xn{i}", [128, D], BF16)) for i in range(2)]
            sq_scr = e0(nc.sbuf_tensor("p0_sq", [128, D], F32))
            st1 = e0(nc.sbuf_tensor("p0_st", [128, 8], F32))
            pt_bf = [e0(nc.psum_tensor(f"p0_pt{i}", [128, KB // 2 * 128], BF16))
                     for i in range(2)]

            ld_tok = {}
            xn_marks = {}
            pe_g = {}
            cp_g = {}

            for t in range(T_OWN):
                @block.sync
                def _(sy, t=t):
                    if t >= 2:
                        sy.wait_ge(SE.h, xn_marks[t - 2])
                    ld_tok[t] = xls.load(sy, t, xf[t % 2][:, :],
                                         g["x_own"][t * 128:(t + 1) * 128, :])

                @block.vector
                def _(v, t=t):
                    xls.wait(v, ld_tok[t])

                @block.scalar
                def _(s, t=t):
                    xls.wait(s, ld_tok[t])

                emit_ln_stats(t, xf[t % 2][:, :], st1, sq_scr[:, :], first=(t == 0))

                @block.scalar
                def _(s, t=t):
                    s.wait_ge(VE.h, VE.n)
                    if t >= 2:
                        s.wait_ge(PE.h, pe_g[(t - 2, 1)])
                    inst = s.activation(xnbf[t % 2][:, :], xf[t % 2][:, :],
                                        AF.Identity, scale=rstd_own[:, t:t + 1],
                                        bias=nmr_own[:, t:t + 1])
                    xn_marks[t] = SE.inc(inst)

                for hh in range(2):
                    @block.tensor
                    def _(te, t=t, hh=hh):
                        te.wait_ge(SE.h, xn_marks[t])
                        gi = t * 2 + hh
                        if gi == 0:
                            te.wait_ge(PRE.h, pre_total)
                        if gi >= 2:
                            te.wait_ge(VE.h, cp_g[gi - 2])
                        inst = None
                        for j in range(KB // 2):
                            k = hh * (KB // 2) + j
                            inst = te.transpose(
                                pt_bf[gi % 2][:, j * 128:(j + 1) * 128],
                                xnbf[t % 2][:, k * 128:(k + 1) * 128], id_bf[:, :])
                        pe_g[(t, hh)] = PE.inc(inst)

                    @block.vector
                    def _(v, t=t, hh=hh):
                        gi = t * 2 + hh
                        v.wait_ge(PE.h, pe_g[(t, hh)])
                        dst = xnT[:, :].rearrange("p (k t c) -> p k t c",
                                                  k=KB, t=T_OWN)[
                            :, hh * (KB // 2):(hh + 1) * (KB // 2), t, :]
                        cp_g[gi] = VE.inc(v.tensor_copy(dst, pt_bf[gi % 2][:, :]))
                if t % 4 == 3 and hh == 1:
                    block.flush()

            p0_done_ve = VE.n

            @block.sync
            def _(sy):
                sy.wait_ge(VE.h, p0_done_ve)
                xv = xnT[:, :].rearrange("p (k t c) -> p k t c", t=T_OWN, c=128)
                for t_ in range(T_OWN):
                    sy.dma_start(out=scr_xnT[t_ * 128:(t_ + 1) * 128, :].rearrange(
                        "p (k c) -> p k c", c=128),
                        in_=xv[:, :, t_, :]).then_inc(XSP.h, 16)
                sy.dma_start(out=scr_r1, in_=rstd1_own[:, :]).then_inc(XSP.h, 16)
                XSP.n += 17 * 16

            @block.gpsimd
            def _(gp):
                gp.wait_ge(XSP.h, XSP.n)
                if nranks == 8:
                    gp.collective_compute(
                        "AllGather", ALU.bypass,
                        replica_groups=[[0, 1], [2, 3], [4, 5], [6, 7]],
                        ins=[scr_xnT.opt()], outs=[scr_xnT_all.opt()],
                    ).then_inc(CC.h)
                    CC.n += 1
                    gp.collective_compute(
                        "AllGather", ALU.bypass,
                        replica_groups=[[0, 1], [2, 3], [4, 5], [6, 7]],
                        ins=[scr_r1.opt()], outs=[scr_r1_all.opt()],
                    ).then_inc(CC.h)
                    CC.n += 1
                else:
                    gp.dma_start(out=scr_xnT_all[0:T_OWN * 128, :], in_=scr_xnT).then_inc(GDO.h, 16)
                    gp.dma_start(out=scr_xnT_all[T_OWN * 128:, :], in_=scr_xnT).then_inc(GDO.h, 16)
                    gp.dma_start(out=scr_r1_all[0:128, :], in_=scr_r1).then_inc(GDO.h, 16)
                    gp.dma_start(out=scr_r1_all[128:256, :], in_=scr_r1).then_inc(GDO.h, 16)
                    GDO.n += 64
                    gp.wait_ge(GDO.h, GDO.n)
                    gp.wait_ge(CC.h, CC.n)
                    CC.inc(gp.memset(consts[:, 0:1], EPS_LN))
                    gp.wait_ge(CC.h, CC.n)
                    CC.inc(gp.memset(consts[:, 0:1], EPS_LN))
            xn_ag_cc = CC.n
            block.flush()

        if debug:
            dout("dbg_xnT", [128, KB * T_OWN * 128])
            dout("dbg_rstd", [128, T_OWN])
            dout("dbg_rstd1", [128, T_OWN])

            @block.gpsimd
            def _(gp):
                gp.wait_ge(VE.h, p0_done_ve)
                gp.dma_start(out=dbg["dbg_xnT"], in_=xnT[:, :]).then_inc(GDO.h, 16)
                gp.dma_start(out=dbg["dbg_rstd"], in_=rstd_own[:, :]).then_inc(GDO.h, 16)
                gp.dma_start(out=dbg["dbg_rstd1"], in_=rstd1_own[:, :]).then_inc(GDO.h, 16)
                GDO.n += 48

        if phases < 2:
            return _finish_early(nc, dbg, block, out_own, g, OST, SYO, MIR, GDO, mir_total, scopes)
        with ExitStack() as ph:
            e0 = ph.enter_context
            wk_g = [e0(nc.sbuf_tensor(f"pa_wk{i}", [128, KB * 128], BF16)) for i in range(2)]
            wv_g = [e0(nc.sbuf_tensor(f"pa_wv{i}", [128, KB * 128], BF16)) for i in range(2)]
            wmb_g = [e0(nc.sbuf_tensor(f"pa_wmb{i}", [128, KB * 512], BF16)) for i in range(2)]
            bk_g = [e0(nc.sbuf_tensor(f"pa_bk{i}", [1, 128], BF16)) for i in range(2)]
            bv_g = [e0(nc.sbuf_tensor(f"pa_bv{i}", [1, 128], BF16)) for i in range(2)]
            bmb_g = [e0(nc.sbuf_tensor(f"pa_bmb{i}", [1, 512], BF16)) for i in range(2)]
            mc_h = [e0(nc.sbuf_tensor(f"pa_mc{i}", [128, FS * 129], BF16)) for i in range(2)]
            x2p_k = e0(nc.sbuf_tensor("pa_x2p", [128, 2048], BF16))
            x2n_k = e0(nc.sbuf_tensor("pa_x2n", [128, 2048], BF16))
            mk2 = e0(nc.sbuf_tensor("pa_mk2", [128, FS * 2048], BF16))
            mkT_sb = e0(nc.sbuf_tensor("pa_mkT", [128, FS * T_OWN * 128], BF16))
            v_g = e0(nc.sbuf_tensor("pa_v", [128, T_OWN * 128], BF16))
            mb4_sb = e0(nc.sbuf_tensor("pa_mb4", [128, T_OWN * 512], BF16))
            mbe_sb = e0(nc.sbuf_tensor("pa_mbe", [128, 512], BF16))
            sqa_k = e0(nc.sbuf_tensor("pa_sqa", [128, 512], F32))
            sqt_k = e0(nc.sbuf_tensor("pa_sqt", [128, 512], BF16))
            nstp = e0(nc.sbuf_tensor("pa_nstp", [128, 16], F32))
            rstd_p = e0(nc.sbuf_tensor("pa_rstd", [128, 16], F32))
            nrstd_p = e0(nc.sbuf_tensor("pa_nrstd", [128, 16], F32))
            epss_p = e0(nc.sbuf_tensor("pa_epss", [128, 16], F32))
            nsc_p = e0(nc.sbuf_tensor("pa_nsc", [128, 16], F32))
            tmp1 = e0(nc.sbuf_tensor("pa_tmp1", [128, 256], F32))
            den3 = e0(nc.sbuf_tensor("pa_den3", [128, 8], F32))
            wmv_bf = [e0(nc.sbuf_tensor(f"pa_wmv{i}", [128, 132], BF16)) for i in range(2)]
            da_st = [e0(nc.sbuf_tensor(f"pa_da{i}", [128, FS * 129], F32)) for i in range(2)]
            pf = [e0(nc.psum_tensor(f"pa_pf{i}", [128, 512], F32)) for i in range(7)]
            pb = e0(nc.psum_tensor("pa_pb", [128, 1024], BF16))

            wld = {}
            hld = {}
            del_tok = {}
            stageA_pe_done = {}
            head_pe_done = {}
            krelu_se = {}
            kexp_se = {}
            vcp_ve = {}
            proj_pe = {}
            dp_half_ve = {}
            tpk_pe = {}
            cpk_mk = {}
            num_pe = {}
            vn = {}
            wmv_pe = {}
            da_read = 0
            num_slot_vn = [0, 0]
            sigp = {"v": 0}
            mb4_ve = {"v": 0}
            mk2_free = {"pe": 0}

            for gi in range(4):
                sl = gi % 2
                if gi == 1:
                    @block.gpsimd
                    def _(gp):
                        gp.dma_start(out=wq_mir, in_=g["wqT"]).then_inc(MIR.h, 16)
                        MIR.n += 16

                @block.gpsimd
                def _(gp, gi=gi, sl=sl):
                    if gi == 0:
                        gp.wait_ge(VE.h, p0_done_ve)
                    if gi >= 2:
                        gp.wait_ge(PE.h, stageA_pe_done[gi - 2])
                    sem = wgrp.sems[sl]
                    for out_sb, src, w in ((wk_g[sl], g["wkT"], 128),
                                           (wv_g[sl], g["wvT"], 128),
                                           (wmb_g[sl], g["wmbT"], 512)):
                        src_ap = src.rearrange("(k p) c -> p k c", p=128)[
                            :, :, gi * w:(gi + 1) * w]
                        gp.dma_start(out=out_sb[:, :].rearrange(
                            "p (k c) -> p k c", k=KB), in_=src_ap).then_inc(sem.h, 16)
                        sem.n += 16
                    for out_sb, src, w in ((bk_g[sl], g["bk"], 128),
                                           (bv_g[sl], g["bv"], 128),
                                           (bmb_g[sl], g["bmb"], 512)):
                        gp.dma_start(out=out_sb[:, :],
                                     in_=src[:, gi * w:(gi + 1) * w]).then_inc(sem.h, 16)
                        sem.n += 16
                    wld[gi] = (sl, sem.n)

                def pa_dpfp_half(gi, half):
                    @block.vector
                    def _(v, gi=gi, half=half):
                        v.wait_ge(SE.h, krelu_se[(gi, half * 8 + 7)])
                        if gi >= 1 and half == 0:
                            v.wait_ge(PE.h, mk2_free["pe"])
                        c0 = half * 1024
                        inst = None
                        for j in (1, 2, 3):
                            bp = (j - 1) * 2 * 2048
                            bn = bp + 2048
                            s0 = c0 + j if half == 0 else c0
                            inst = v.tensor_mul(mk2[:, bp + s0:bp + c0 + 1024],
                                                x2p_k[:, s0:c0 + 1024],
                                                x2p_k[:, s0 - j:c0 + 1024 - j])
                            inst = v.tensor_mul(mk2[:, bn + s0:bn + c0 + 1024],
                                                x2n_k[:, s0:c0 + 1024],
                                                x2n_k[:, s0 - j:c0 + 1024 - j])
                        ch(v, VE, inst)
                        pv = x2p_k[:, :].rearrange("p (t c) -> p t c", c=128)
                        nv = x2n_k[:, :].rearrange("p (t c) -> p t c", c=128)
                        t0, t1 = half * 8, half * 8 + 8
                        for j in (1, 2, 3):
                            bp = (j - 1) * 2 * 2048
                            bn = bp + 2048
                            mpv = mk2[:, bp:bp + 2048].rearrange(
                                "p (t c) -> p t c", c=128)
                            mnv = mk2[:, bn:bn + 2048].rearrange(
                                "p (t c) -> p t c", c=128)
                            inst = v.tensor_mul(mpv[:, t0:t1, 0:j],
                                                pv[:, t0:t1, 0:j],
                                                nv[:, t0:t1, 128 - j:128])
                            inst = v.tensor_mul(mnv[:, t0:t1, 0:j],
                                                nv[:, t0:t1, 0:j],
                                                pv[:, t0:t1, 128 - j:128])
                        dp_half_ve[(gi, half)] = VE.inc(inst)

                def pa_tp(gi, t):
                    @block.tensor
                    def _(te, gi=gi, t=t):
                        te.wait_ge(VE.h, dp_half_ve[(gi, t // 8)])
                        if t >= 1:
                            e_, m_ = cpk_mk[(gi, t - 1)]
                            te.wait_ge(e_.h, m_)
                        elif gi >= 1:
                            e_, m_ = cpk_mk[(gi - 1, 15)]
                            te.wait_ge(e_.h, m_)
                        inst = None
                        for fs in range(FS):
                            inst = te.transpose(
                                pb[:, fs * 128:(fs + 1) * 128],
                                mk2[:, fs * 2048 + t * 128:fs * 2048 + (t + 1) * 128],
                                id_bf[:, :])
                        tpk_pe[(gi, t)] = PE.inc(inst)

                def pa_cp(gi, t):
                    dst = mkT_sb[:, :].rearrange("p (f t c) -> p f t c",
                                                 f=FS, t=T_OWN)[:, :, t, :]
                    if t % 2 == 0:
                        @block.vector
                        def _(v, gi=gi, t=t, dst=dst):
                            v.wait_ge(PE.h, tpk_pe[(gi, t)])
                            cpk_mk[(gi, t)] = (VE, VE.inc(
                                v.tensor_copy(dst, pb[:, 0:FDD])))
                    else:
                        @block.scalar
                        def _(s, gi=gi, t=t, dst=dst):
                            s.wait_ge(PE.h, tpk_pe[(gi, t)])
                            cpk_mk[(gi, t)] = (SE, SE.inc(
                                s.activation(dst, pb[:, 0:FDD], AF.Identity)))

                for t in range(T_OWN):
                    @block.tensor
                    def _(te, gi=gi, sl=sl, t=t):
                        if t == 0:
                            te.wait_ge(wgrp.sems[sl].h, wld[gi][1])
                            if gi == 0:
                                te.wait_ge(VE.h, p0_done_ve)
                            else:
                                te.wait_ge(VE.h, da_read)
                        if (gi, t - 2) in krelu_se:
                            te.wait_ge(SE.h, kexp_se[(gi, t - 2)])
                            te.wait_ge(VE.h, vcp_ve[(gi, t - 2)])
                        elif gi >= 1 and t <= 1:
                            te.wait_ge(SE.h, kexp_se[(gi - 1, 14 + t)])
                            te.wait_ge(VE.h, vcp_ve[(gi - 1, 14 + t)])
                        kvb = pf[t % 2]
                        mbb = pf[2 + t % 2]
                        for cc0, wsb, bsb in ((0, wk_g[sl], bk_g[sl]),
                                              (128, wv_g[sl], bv_g[sl])):
                            for k in range(KB):
                                te.matmul(kvb[:, cc0:cc0 + 128],
                                          xnT[:, (k * T_OWN + t) * 128:(k * T_OWN + t) * 128 + 128],
                                          wsb[:, k * 128:(k + 1) * 128],
                                          start=(k == 0), stop=False)
                            te.matmul(kvb[:, cc0:cc0 + 128], ones_r[:, :],
                                      bsb[:, :], start=False, stop=True)
                        for k in range(KB):
                            te.matmul(mbb[:, 0:512],
                                      xnT[:, (k * T_OWN + t) * 128:(k * T_OWN + t) * 128 + 128],
                                      wmb_g[sl][:, k * 512:(k + 1) * 512],
                                      start=(k == 0), stop=False)
                        inst = te.matmul(mbb[:, 0:512], ones_r[:, :],
                                         bmb_g[sl][:, :], start=False, stop=True)
                        proj_pe[(gi, t)] = PE.inc(inst)

                    if t % 2 == 1:
                        @block.scalar
                        def _(s, gi=gi, t=t):
                            for tt in (t - 1, t):
                                s.wait_ge(PE.h, proj_pe[(gi, tt)])
                                if tt == 0 and gi >= 1:
                                    s.wait_ge(VE.h, dp_half_ve[(gi - 1, 1)])
                                if tt >= 1 or gi >= 1:
                                    s.wait_ge(VE.h, sigp["v"])
                                kvb = pf[tt % 2]
                                mbb = pf[2 + tt % 2]
                                s.activation(x2p_k[:, tt * 128:(tt + 1) * 128],
                                             kvb[:, 0:128], AF.Relu)
                                inst = s.activation(x2n_k[:, tt * 128:(tt + 1) * 128],
                                                    kvb[:, 0:128], AF.Relu, scale=-1.0)
                                krelu_se[(gi, tt)] = SE.inc(inst)
                                inst = s.activation(mbe_sb[:, :], mbb[:, 0:512],
                                                    AF.Exp, scale=-1.0)
                                kexp_se[(gi, tt)] = SE.inc(inst)

                                @block.vector
                                def _(v, gi=gi, tt=tt):
                                    v.wait_ge(PE.h, proj_pe[(gi, tt)])
                                    VE.inc(v.tensor_copy(
                                        v_g[:, tt * 128:(tt + 1) * 128],
                                        pf[tt % 2][:, 128:256]))
                                    vcp_ve[(gi, tt)] = VE.n
                                    v.wait_ge(SE.h, kexp_se[(gi, tt)])
                                    ch(v, VE, v.tensor_scalar_add(
                                        mbe_sb[:, :], mbe_sb[:, :], 1.0))
                                    ch(v, VE, v.reciprocal(sqa_k[:, :],
                                                           mbe_sb[:, :]))
                                    inst = v.tensor_copy(
                                        mb4_sb[:, tt * 512:(tt + 1) * 512],
                                        sqa_k[:, :])
                                    sigp["v"] = VE.inc(inst)

                    if t == 7:
                        pa_dpfp_half(gi, 0)
                    if t >= 8:
                        pa_tp(gi, t - 8)
                        pa_cp(gi, t - 8)
                block.flush()
                pa_dpfp_half(gi, 1)
                for t in range(8, T_OWN):
                    pa_tp(gi, t)
                    pa_cp(gi, t)

                @block.vector
                def _(v, gi=gi):
                    v.wait_ge(VE.h, dp_half_ve[(gi, 1)])
                    for qf in range(4):
                        c0 = qf * 512
                        sl0 = mk2[:, c0:c0 + 512]
                        ch(v, VE, v.tensor_mul(sqt_k[:, :], sl0, sl0))
                        ch(v, VE, v.tensor_copy(sqa_k[:, :], sqt_k[:, :]))
                        for b in range(1, 6):
                            slb = mk2[:, b * 2048 + c0:b * 2048 + c0 + 512]
                            ch(v, VE, v.tensor_mul(sqt_k[:, :], slb, slb))
                            ch(v, VE, v.tensor_add(sqa_k[:, :], sqa_k[:, :],
                                                   sqt_k[:, :]))
                        ch(v, VE, v.reduce_sum(
                            nstp[:, qf * 4:(qf + 1) * 4],
                            sqa_k[:, :].rearrange("p (t c) -> p t c", c=128),
                            axis=AX.X))

                emit_rsqrt_newton(None, None, rstd_p[:, :], nstp[:, :],
                                  nsc_p[:, :])

                @block.vector
                def _(v, gi=gi):
                    v.wait_ge(VE.h, VE.n)
                    ch(v, VE, v.tensor_scalar_mul(nrstd_p[:, :], rstd_p[:, :],
                                                  -1.0))
                    ch(v, VE, v.tensor_mul(epss_p[:, :], nstp[:, :],
                                           rstd_p[:, :]))
                    ch(v, VE, v.tensor_scalar_mul(epss_p[:, :], epss_p[:, :],
                                                  EPS_ATT))
                    inst = None
                    for t in range(T_OWN):
                        inst = v.tensor_scalar_mul(
                            mb4_sb[:, t * 512:(t + 1) * 512],
                            mb4_sb[:, t * 512:(t + 1) * 512],
                            nrstd_p[:, t:t + 1])
                    mb4_ve["v"] = VE.inc(inst)
                block.flush()
                stageA_pe_done[gi] = PE.n

                for hi in range(4):
                    h = gi * 4 + hi

                    @block.gpsimd
                    def _(gp, h=h):
                        if h >= 2:
                            gp.wait_ge(PE.h, head_pe_done[h - 2])
                            gp.wait_ge(VE.h, da_read)
                        slh = h % 2
                        sem = pgrp.sems[slh]
                        gp.dma_start(out=mc_h[slh][:, :],
                                     in_=g["memcat"][:, h * FS * 129:(h + 1) * FS * 129]).then_inc(sem.h, 16)
                        sem.n += 16
                        hld[h] = (slh, sem.n)

                    for t2 in range(T_OWN // 2):
                        tA, tB = 2 * t2, 2 * t2 + 1

                        @block.tensor
                        def _(te, gi=gi, h=h, hi=hi, tA=tA, tB=tB):
                            slh = h % 2
                            if tA == 0:
                                te.wait_ge(pgrp.sems[slh].h, hld[h][1])
                                if hi == 0:
                                    e_, m_ = cpk_mk[(gi, 15)]
                                    te.wait_ge(e_.h, m_)
                                    e_, m_ = cpk_mk[(gi, 14)]
                                    te.wait_ge(e_.h, m_)
                            for t in (tA, tB):
                                ns = 256 * (t % 2)
                                te.wait_ge(VE.h, num_slot_vn[t % 2])
                                inst = None
                                for fs in range(FS):
                                    inst = te.matmul(
                                        pf[6][:, ns:ns + 129],
                                        mkT_sb[:, (fs * T_OWN + t) * 128:(fs * T_OWN + t) * 128 + 128],
                                        mc_h[slh][:, fs * 129:(fs + 1) * 129],
                                        start=(fs == 0), stop=(fs == FS - 1))
                                num_pe[(h, t)] = PE.inc(inst)

                        @block.vector
                        def _(v, gi=gi, h=h, hi=hi, tA=tA, tB=tB):
                            v.wait_ge(PE.h, num_pe[(h, tB)])
                            if tA >= 2:
                                v.wait_ge(PE.h, wmv_pe[(h, tB - 2)])
                            mk_ = {}

                            def step(t, i, fn):
                                off = 4 * (t % 2)
                                ns = 256 * (t % 2)
                                nm = pf[6][:, ns:ns + 129]
                                w = wmv_bf[t % 2]
                                tm = tmp1[:, 128 * (t % 2):128 * (t % 2) + 128]
                                d = lambda j: den3[:, off + j:off + j + 1]
                                mk_[(t, i)] = VE.inc(fn(nm, w, tm, d))

                            def wt(t, i):
                                v.wait_ge(VE.h, mk_[(t, i)])

                            for t in (tA, tB):
                                step(t, 1, lambda nm, w, tm, d, t=t: v.tensor_add(
                                    d(0), nm[:, 128:129], epss_p[:, t:t + 1]))
                                step(t, 5, lambda nm, w, tm, d, t=t: v.tensor_scalar(
                                    d(2), nm[:, 128:129], nrstd_p[:, t:t + 1],
                                    1.0 - EPS_ATT, ALU.mult, ALU.add))
                            for t in (tA, tB):
                                wt(t, 1)
                                step(t, 2, lambda nm, w, tm, d: v.reciprocal(
                                    d(1), d(0)))
                                wt(t, 5)
                                step(t, 6, lambda nm, w, tm, d: v.tensor_scalar(
                                    d(2), d(2), 1.0, 0.0, ALU.min, ALU.max))
                            for t in (tA, tB):
                                wt(t, 2)
                                step(t, 3, lambda nm, w, tm, d, t=t: v.scalar_tensor_tensor(
                                    tm, nm[:, 0:128], d(1),
                                    v_g[:, t * 128:(t + 1) * 128],
                                    ALU.mult, ALU.subtract))
                                wt(t, 6)
                                step(t, 7, lambda nm, w, tm, d, t=t: v.tensor_scalar_mul(
                                    w[:, 128:129], d(2), rstd_p[:, t:t + 1]))
                            for t in (tA, tB):
                                wt(t, 3)
                                step(t, 4, lambda nm, w, tm, d, t=t, hi=hi: v.tensor_mul(
                                    w[:, 0:128], tm,
                                    mb4_sb[:, t * 512 + hi * 128:t * 512 + hi * 128 + 128]))
                                vn[(h, t)] = mk_[(t, 4)]
                                num_slot_vn[t % 2] = vn[(h, t)]

                        @block.tensor
                        def _(te, gi=gi, h=h, hi=hi, tA=tA, tB=tB):
                            if tA == 0 and hi == 0:
                                te.wait_ge(SE.h, kexp_se[(gi, 15)])
                            for t in (tA, tB):
                                te.wait_ge(VE.h, vn[(h, t)])
                                w = wmv_bf[t % 2]
                                inst = None
                                for fs in range(FS):
                                    inst = te.matmul(
                                        pf[fs][:, 0:129],
                                        mk2[:, fs * 2048 + t * 128:fs * 2048 + (t + 1) * 128],
                                        w[:, 0:129],
                                        start=(t == 0), stop=(t == T_OWN - 1))
                                wmv_pe[(h, t)] = PE.inc(inst)

                    @block.vector
                    def _(v, h=h):
                        nonlocal da_read
                        v.wait_ge(PE.h, wmv_pe[(h, T_OWN - 1)])
                        if h >= 2:
                            DEL.wait(v, del_tok[h - 2])
                        d = da_st[h % 2]
                        mc = mc_h[h % 2]
                        inst = None
                        for fs in range(FS):
                            inst = v.scalar_tensor_tensor(
                                d[:, fs * 129:(fs + 1) * 129],
                                mc[:, fs * 129:(fs + 1) * 129], 1.0 / nranks,
                                pf[fs][:, 0:129], ALU.mult, ALU.add)
                        VE.inc(inst)
                        da_read = VE.n

                    @block.sync
                    def _(sy, h=h):
                        sy.wait_ge(VE.h, da_read)
                        del_tok[h] = DEL.load(
                            sy, h, ar_in_t.ap()[:, h * FS * 129:(h + 1) * FS * 129],
                            da_st[h % 2][:, :])
                    head_pe_done[h] = PE.n
                    block.flush()
                mk2_free["pe"] = PE.n

            @block.gpsimd
            def _(gp):
                for s_ in DEL.sems:
                    gp.wait_ge(s_.h, s_.n)
                gp.collective_compute(
                    "AllReduce", ALU.add, replica_groups=[list(range(nranks))],
                    ins=[ar_in_t.ap().opt()], outs=[ar_out_t.ap().opt()],
                ).then_inc(CC.h)
                CC.n += 1
            ar_cc_mark = CC.n
            block.flush()

        if debug:
            dout("dbg_arin", [128, 16 * FS * 129])
            dout("dbg_arout", [128, 16 * FS * 129])

            @block.gpsimd
            def _(gp):
                gp.wait_ge(CC.h, CC.n)
                gp.dma_start(out=dbg["dbg_arin"], in_=ar_in_t.ap()).then_inc(GDO.h, 16)
                gp.dma_start(out=dbg["dbg_arout"], in_=ar_out_t.ap()).then_inc(GDO.h, 16)
                GDO.n += 32

        if phases < 3:
            return _finish_early(nc, dbg, block, out_own, g, OST, SYO, MIR, GDO, mir_total, scopes)
        xnT_scope.close()
        scopes.remove(xnT_scope)

        kvq_scope = ExitStack()
        scopes.append(kvq_scope)
        kv_dT = kvq_scope.enter_context(
            nc.sbuf_tensor("kv_dT", [128, 4 * T_ALL * 128], BF16))
        q_dT = kvq_scope.enter_context(
            nc.sbuf_tensor("q_dT", [128, 4 * T_ALL * 128], BF16))
        f1x = Slots(nc, top, "f1x", 2)
        f1c = Slots(nc, top, "f1c", 2)
        f1w = Slots(nc, top, "f1w", 1)
        with ExitStack() as ph:
            e0 = ph.enter_context
            kvdw = e0(nc.sbuf_tensor("f1_kvdw", [128, KB * 512], BF16))
            qdw = e0(nc.sbuf_tensor("f1_qdw", [128, KB * 512], BF16))
            rkw = e0(nc.sbuf_tensor("f1_rkw", [128, KB * 512], BF16))
            rqw = e0(nc.sbuf_tensor("f1_rqw", [128, 4 * 512], BF16))
            cs = [e0(nc.sbuf_tensor(f"f1_cs{i}", [128, 512], F32)) for i in range(2)]
            xnt = [e0(nc.sbuf_tensor(f"f1_xnt{i}", [128, KB * 128], BF16)) for i in range(2)]
            r1_sb = e0(nc.sbuf_tensor("f1_r1", [128, 2 * T_OWN], F32))
            pbuf_kv = e0(nc.sbuf_tensor("f1_pbkv", [128, 512], BF16))
            pbuf_q = e0(nc.sbuf_tensor("f1_pbq", [128, 512], BF16))
            krp_k = e0(nc.sbuf_tensor("f1_krpk", [128, 512], F32))
            krp_q = e0(nc.sbuf_tensor("f1_krpq", [128, 512], F32))
            rt = e0(nc.sbuf_tensor("f1_rt", [128, 2048], F32))
            krbf_k = e0(nc.sbuf_tensor("f1_krbk", [128, 512], BF16))
            krbf_q = e0(nc.sbuf_tensor("f1_krbq", [128, 512], BF16))
            rstage = [e0(nc.sbuf_tensor(f"f1_rst{i}", [128, 512], BF16))
                      for i in range(2)]
            rstq = Slots(nc, top, "rstq", 2)
            pp = [e0(nc.psum_tensor(f"f1_pp{i}", [128, 512], F32)) for i in range(4)]
            pt = [e0(nc.psum_tensor(f"f1_pt{i}", [128, 1024], BF16)) for i in range(2)]

            @block.gpsimd
            def _(gp):
                gp.wait_ge(VE.h, VE.n)
                gp.wait_ge(XSP.h, XSP.n)
                gp.wait_ge(GDO.h, GDO.n)
                gp.wait_ge(CC.h, xn_ag_cc)
                sem = f1w.sems[0]
                for dst, src in ((kvdw, g["kv_d_w"]), (qdw, g["q_d_w"]),
                                 (rkw, g["rope_k_my"])):
                    gp.dma_start(out=dst[:, :].rearrange("p (k c) -> p k c", k=KB),
                                 in_=src.rearrange("(k p) c -> p k c", p=128)).then_inc(sem.h, 16)
                    sem.n += 16
                gp.dma_start(out=rqw[:, :].rearrange("p (k c) -> p k c", k=4),
                             in_=g["rope_q_my"].rearrange("(k p) c -> p k c", p=128)).then_inc(sem.h, 16)
                sem.n += 16
                gp.dma_start(out=r1_sb[:, 0:T_OWN],
                             in_=scr_r1_all[0:128, :]).then_inc(sem.h, 16)
                gp.dma_start(out=r1_sb[:, T_OWN:2 * T_OWN],
                             in_=scr_r1_all[128:256, :]).then_inc(sem.h, 16)
                sem.n += 32
            f1w_tok = (0, f1w.sems[0].n)

            @block.vector
            def _(v):
                f1w.wait(v, f1w_tok)
                v.wait_ge(GP.h, consts_mark)
                inst = None
                for dst in (kvdw, qdw, rkw):
                    for k in range(KB):
                        inst = v.tensor_scalar_mul(dst[:, k * 512:(k + 1) * 512],
                                                   dst[:, k * 512:(k + 1) * 512],
                                                   ln1_sb[:, k:k + 1])
                VE.inc(inst)
            f1w_ve = VE.n

            xld = {}
            cld = {}
            tp_marks = {}
            pp_read = [0, 0, 0, 0]
            pt_read = [0, 0]
            rope_done = {}
            rst_tok = {}
            xnt_last_pe = {}

            pa_pe_end = PE.n
            pa_se_end = SE.n
            for t in range(T_ALL):
                hb, tw = t // T_OWN, t % T_OWN

                @block.sync
                def _(sy, t=t, hb=hb, tw=tw):
                    if t == 0:
                        sy.wait_ge(PE.h, pa_pe_end)
                        sy.wait_ge(SE.h, pa_se_end)
                        sy.wait_ge(VE.h, f1w_ve)
                        sy.wait_ge(CC.h, xn_ag_cc)
                    if t >= 2:
                        sy.wait_ge(PE.h, xnt_last_pe[t - 2])
                        sy.wait_ge(VE.h, rope_done.get(t - 2, 0))
                    r0 = hb * T_OWN * 128 + tw * 128
                    xld[t] = f1x.load(sy, t, xnt[t % 2][:, :],
                                      scr_xnT_all[r0:r0 + 128, :])
                    cld[t] = f1c.load(sy, t, cs[t % 2][:, 0:256],
                                      g["cos_rep"][t * 128:(t + 1) * 128, :])
                    s_ = f1c.sems[t % 2]
                    sy.dma_start(out=cs[t % 2][:, 256:512],
                                 in_=g["sin_rep"][t * 128:(t + 1) * 128, :]).then_inc(s_.h, 16)
                    s_.n += 16
                    cld[t] = (cld[t][0], s_.n)

                r1c = r1_sb[:, hb * T_OWN + tw:hb * T_OWN + tw + 1]

                def proj(t, lhs_blocks, w_sb, nk, bank, scale_ap, out_bf,
                         extra_pe_wait=None, lhs_tok=None):
                    @block.tensor
                    def _(te):
                        if t == 0:
                            te.wait_ge(VE.h, f1w_ve)
                            te.wait_ge(SE.h, pa_se_end)
                        if lhs_tok is not None:
                            f1x.wait(te, lhs_tok)
                        te.wait_ge(VE.h, pp_read[bank])
                        if extra_pe_wait is not None:
                            te.wait_ge(VE.h, extra_pe_wait)
                        inst = None
                        for k in range(nk):
                            inst = te.matmul(pp[bank][:, :], lhs_blocks(k),
                                             w_sb[:, k * 512:(k + 1) * 512],
                                             start=(k == 0), stop=(k == nk - 1))
                        tp_marks[(t, "pj", bank)] = PE.inc(inst)

                    @block.vector
                    def _(v):
                        v.wait_ge(PE.h, tp_marks[(t, "pj", bank)])
                        if scale_ap is not None:
                            inst = v.tensor_scalar_mul(out_bf[:, :], pp[bank][:, :],
                                                       scale_ap)
                        else:
                            inst = v.tensor_copy(out_bf[:, :], pp[bank][:, :])
                        pp_read[bank] = VE.inc(inst)
                    return VE.n

                def t4(t, src_bf, dst_sb, dst_block_cols, hh, rst_slot=None):
                    @block.tensor
                    def _(te):
                        te.wait_ge(VE.h, VE.n)
                        te.wait_ge(VE.h, pt_read[hh])
                        inst = None
                        for j in range(4):
                            inst = te.transpose(pt[hh][:, j * 128:(j + 1) * 128],
                                                src_bf[:, j * 128:(j + 1) * 128],
                                                id_bf[:, :])
                        tp_marks[(t, "t4", hh)] = PE.inc(inst)

                    @block.vector
                    def _(v):
                        v.wait_ge(PE.h, tp_marks[(t, "t4", hh)])
                        if rst_slot is not None and rst_tok.get(rst_slot):
                            rstq.wait(v, rst_tok[rst_slot])
                        inst = None
                        for j in range(4):
                            inst = v.tensor_copy(dst_block_cols(j),
                                                 pt[hh][:, j * 128:(j + 1) * 128])
                        pt_read[hh] = VE.inc(inst)
                    return VE.n

                xblk = lambda k, t=t: xnt[t % 2][:, k * 128:(k + 1) * 128]

                proj(t, xblk, kvdw, KB, 0, r1c, pbuf_kv, lhs_tok=xld[t])
                t4(t, pbuf_kv, kv_dT,
                   lambda j, t=t: kv_dT[:, (j * T_ALL + t) * 128:(j * T_ALL + t) * 128 + 128],
                   0)
                proj(t, xblk, qdw, KB, 1, r1c, pbuf_q)
                qdT_ve = t4(t, pbuf_q, q_dT,
                            lambda j, t=t: q_dT[:, (j * T_ALL + t) * 128:(j * T_ALL + t) * 128 + 128],
                            1)
                proj(t, xblk, rkw, KB, 2, r1c, krp_k)
                xnt_last_pe[t] = PE.n
                proj(t, lambda k, t=t: q_dT[:, (k * T_ALL + t) * 128:(k * T_ALL + t) * 128 + 128],
                     rqw, 4, 3, None, krp_q, extra_pe_wait=qdT_ve)

                @block.vector
                def _(v, t=t):
                    f1c.wait(v, cld[t])
                    v.wait_ge(VE.h, VE.n)
                    cvv = cs[t % 2][:, 0:256].rearrange("p (h c) -> p h c", h=8)
                    svv = cs[t % 2][:, 256:512].rearrange("p (h c) -> p h c", h=8)
                    parts = []
                    marks = []
                    for kr, ob, r0 in ((krp_k, krbf_k, 0), (krp_q, krbf_q, 1024)):
                        kv_ = kr[:, :].rearrange("p (h c) -> p h c", h=8)
                        sl4 = [rt[:, r0 + i * 256:r0 + (i + 1) * 256].rearrange(
                            "p (h c) -> p h c", h=8) for i in range(4)]
                        v.tensor_mul(sl4[0], kv_[:, :, 0:32], cvv)
                        v.tensor_mul(sl4[1], kv_[:, :, 32:64], svv)
                        v.tensor_mul(sl4[2], kv_[:, :, 32:64], cvv)
                        inst = v.tensor_mul(sl4[3], kv_[:, :, 0:32], svv)
                        marks.append(VE.inc(inst))
                        parts.append((ob, sl4))
                    inst = None
                    for ci, (ob, sl4) in enumerate(parts):
                        v.wait_ge(VE.h, marks[ci])
                        ov = ob[:, :].rearrange("p (h c) -> p h c", h=8)
                        inst = v.tensor_sub(ov[:, :, 0:32], sl4[0], sl4[1])
                        inst = v.tensor_add(ov[:, :, 32:64], sl4[2], sl4[3])
                    rope_done[t] = VE.inc(inst)

                t4(t, krbf_k, None,
                   lambda j: rstage[0][:, j * 128:(j + 1) * 128], 0, rst_slot=0)

                @block.sync
                def _(sy, t=t):
                    sy.wait_ge(VE.h, VE.n)
                    dst = scr_kT.rearrange("(f p) s -> p f s", p=128)[
                        :, :, t * 128:(t + 1) * 128]
                    rst_tok[0] = rstq.load(
                        sy, 0, dst, rstage[0][:, :].rearrange("p (f c) -> p f c", f=4))

                t4(t, krbf_q, None,
                   lambda j: rstage[1][:, j * 128:(j + 1) * 128], 1, rst_slot=1)

                @block.sync
                def _(sy, t=t):
                    sy.wait_ge(VE.h, VE.n)
                    dst = scr_qT.rearrange("(f p) s -> p f s", p=128)[
                        :, :, t * 128:(t + 1) * 128]
                    rst_tok[1] = rstq.load(
                        sy, 1, dst, rstage[1][:, :].rearrange("p (f c) -> p f c", f=4))
                if t % 4 == 3:
                    block.flush()
            f1_done_ve = VE.n
            f1_done_syo = SYO.n
            block.flush()

        if debug:
            dout("dbg_kvdT", [128, 4 * T_ALL * 128], BF16)
            dout("dbg_qdT", [128, 4 * T_ALL * 128], BF16)
            dout("dbg_krT", [512, S], BF16)
            dout("dbg_qrT", [512, S], BF16)

            @block.gpsimd
            def _(gp):
                gp.wait_ge(VE.h, f1_done_ve)
                for s_ in rstq.sems:
                    gp.wait_ge(s_.h, s_.n)
                for dst, src in ((dbg["dbg_kvdT"], kv_dT[:, :]),
                                 (dbg["dbg_qdT"], q_dT[:, :]),
                                 (dbg["dbg_krT"], scr_kT),
                                 (dbg["dbg_qrT"], scr_qT)):
                    gp.dma_start(out=dst, in_=src).then_inc(GDO.h, 16)
                    GDO.n += 16

        if phases < 4:
            return _finish_early(nc, dbg, block, out_own, g, OST, SYO, MIR, GDO, mir_total, scopes)
        f2w = Slots(nc, top, "f2w", 1)
        rpl = Slots(nc, top, "rpl", 2)
        AVS = Slots(nc, top, "avs", 2)
        with ExitStack() as ph:
            e0 = ph.enter_context
            ku_sb = e0(nc.sbuf_tensor("f2_ku", [128, 4 * 512], BF16))
            qu_sb = e0(nc.sbuf_tensor("f2_qu", [128, 4 * 512], BF16))
            vu_sb = e0(nc.sbuf_tensor("f2_vu", [128, 4 * 1024], BF16))
            kT_grp = [e0(nc.sbuf_tensor(f"f2_kT{j}", [128, S], BF16)) for j in range(2)]
            qT_grp = [e0(nc.sbuf_tensor(f"f2_qT{j}", [128, S], BF16)) for j in range(2)]
            V_grp = e0(nc.sbuf_tensor("f2_V", [128, T_ALL * 260], BF16))
            stexp = e0(nc.sbuf_tensor("f2_stexp", [128, (28 + 32) * 512], BF16))
            av_bf = [e0(nc.sbuf_tensor(f"f2_avbf{i}", [128, 128], BF16))
                     for i in range(2)]
            avstage = [e0(nc.sbuf_tensor(f"f2_avs{j}", [128, S], BF16)) for j in range(2)]
            rcp = e0(nc.sbuf_tensor("f2_rcp", [128, 2], F32))
            st_ps = [e0(nc.psum_tensor(f"f2_st{i}", [128, 512], F32)) for i in range(3)]
            av_ps = [e0(nc.psum_tensor(f"f2_av{i}", [128, 512], F32)) for i in range(2)]
            bld_ps = [e0(nc.psum_tensor(f"f2_bld{i}", [128, 512], F32))
                      for i in range(2)]
            tp_ps = e0(nc.psum_tensor("f2_tp", [128, 1024], BF16))

            @block.gpsimd
            def _(gp):
                gp.wait_ge(VE.h, VE.n)
                sem = f2w.sems[0]
                for dst, src, w in ((ku_sb, g["k_u_my"], 512),
                                    (qu_sb, g["q_u_my"], 512),
                                    (vu_sb, g["v_u_my"], 1024)):
                    gp.dma_start(out=dst[:, :].rearrange("p (k c) -> p k c", k=4),
                                 in_=src.rearrange("(k p) c -> p k c", p=128)).then_inc(sem.h, 16)
                    sem.n += 16
            f2w_tok = (0, f2w.sems[0].n)

            bld_read = [0, 0]
            st_free = [0, 0, 0]
            av_free = [0, 0]
            slab_free = [0, 0]
            avT_marks = {}
            tp_read = [0, 0]
            avT_pe = 0
            av_done_pe = {}
            exp_marks = {}
            sc_marks = {}
            rope_tok = {}
            avs_tok = {}
            grp_ve_done = {}

            for gr in range(4):
                @block.sync
                def _(sy, gr=gr):
                    if gr >= 1:
                        sy.wait_ge(PE.h, grp_pe_done[gr - 1])
                    for s_ in rstq.sems:
                        sy.wait_ge(s_.h, s_.n)
                    for j in range(2):
                        hl = gr * 2 + j
                        s_ = rpl.sems[j]
                        sy.dma_start(out=kT_grp[j][64:128, :],
                                     in_=scr_kT[hl * 64:(hl + 1) * 64, :]).then_inc(s_.h, 16)
                        s_.n += 16
                        sy.dma_start(out=qT_grp[j][64:128, :],
                                     in_=scr_qT[hl * 64:(hl + 1) * 64, :]).then_inc(s_.h, 16)
                        s_.n += 16
                        rope_tok[(gr, j)] = (j, s_.n)

                for j in range(2):
                    hl = gr * 2 + j
                    for c in range(8):
                        @block.tensor
                        def _(te, gr=gr, j=j, hl=hl, c=c):
                            if gr == 0 and j == 0 and c == 0:
                                te.wait_ge(f2w.sems[0].h, f2w_tok[1])
                                te.wait_ge(VE.h, f1_done_ve)
                            if gr >= 1 and j == 0 and c == 0:
                                te.wait_ge(VE.h, grp_ve_done[gr - 1])
                            te.wait_ge(VE.h, bld_read[c % 2])
                            inst = None
                            for kk in range(4):
                                inst = te.matmul(
                                    bld_ps[c % 2][0:64, 0:512],
                                    ku_sb[:, kk * 512 + hl * 64:kk * 512 + hl * 64 + 64],
                                    kv_dT[:, (kk * T_ALL + 4 * c) * 128:(kk * T_ALL + 4 * c) * 128 + 512],
                                    start=(kk == 0), stop=(kk == 3))
                            sc_marks[("kb", gr, j, c)] = PE.inc(inst)

                        @block.vector
                        def _(v, gr=gr, j=j, c=c):
                            v.wait_ge(PE.h, sc_marks[("kb", gr, j, c)])
                            VE.inc(v.tensor_copy(kT_grp[j][0:64, c * 512:(c + 1) * 512],
                                                 bld_ps[c % 2][0:64, 0:512]))
                            bld_read[c % 2] = VE.n
                    for c in range(8):
                        @block.tensor
                        def _(te, gr=gr, j=j, hl=hl, c=c):
                            te.wait_ge(VE.h, bld_read[c % 2])
                            inst = None
                            for kk in range(4):
                                inst = te.matmul(
                                    bld_ps[c % 2][0:64, 0:512],
                                    qu_sb[:, kk * 512 + hl * 64:kk * 512 + hl * 64 + 64],
                                    q_dT[:, (kk * T_ALL + 4 * c) * 128:(kk * T_ALL + 4 * c) * 128 + 512],
                                    start=(kk == 0), stop=(kk == 3))
                            sc_marks[("qb", gr, j, c)] = PE.inc(inst)

                        @block.vector
                        def _(v, gr=gr, j=j, c=c):
                            v.wait_ge(PE.h, sc_marks[("qb", gr, j, c)])
                            VE.inc(v.tensor_copy(qT_grp[j][0:64, c * 512:(c + 1) * 512],
                                                 bld_ps[c % 2][0:64, 0:512]))
                            bld_read[c % 2] = VE.n
                    for ta in range(T_ALL):
                        @block.tensor
                        def _(te, gr=gr, j=j, hl=hl, ta=ta):
                            te.wait_ge(VE.h, bld_read[ta % 2])
                            inst = None
                            for kk in range(4):
                                inst = te.matmul(
                                    bld_ps[ta % 2][:, 0:128],
                                    kv_dT[:, (kk * T_ALL + ta) * 128:(kk * T_ALL + ta) * 128 + 128],
                                    vu_sb[:, kk * 1024 + hl * 128:kk * 1024 + (hl + 1) * 128],
                                    start=(kk == 0), stop=(kk == 3))
                            sc_marks[("vb", gr, j, ta)] = PE.inc(inst)

                        @block.vector
                        def _(v, gr=gr, j=j, ta=ta):
                            v.wait_ge(PE.h, sc_marks[("vb", gr, j, ta)])
                            VE.inc(v.tensor_copy(
                                V_grp[:, ta * 260 + j * 130:ta * 260 + j * 130 + 128],
                                bld_ps[ta % 2][:, 0:128]))
                            bld_read[ta % 2] = VE.n

                @block.gpsimd
                def _(gp, gr=gr):
                    gp.wait_ge(VE.h, VE.n)
                    ones_view = V_grp[:, :].rearrange("p (t c) -> p t c", t=T_ALL)
                    gp.memset(ones_view[:, :, 128:129], 1.0)
                    GP.inc(gp.memset(ones_view[:, :, 258:259], 1.0))
                v_ones_mark = GP.n

                grp_pe_done = grp_pe_done if gr >= 1 else {}

                def f2_scores(gr, j, qg):
                    nkb = 4 * qg + 4
                    sb = (qg % 2) * 14336
                    for kb in range(nkb):
                        @block.tensor
                        def _(te, gr=gr, j=j, qg=qg, kb=kb):
                            if kb == 0 and qg == 0 and j == 0:
                                rpl.wait(te, rope_tok[(gr, 0)])
                                rpl.wait(te, rope_tok[(gr, 1)])
                                te.wait_ge(GP.h, v_ones_mark)
                            te.wait_ge(SE.h, st_free[kb % 3])
                            inst = te.matmul(
                                st_ps[kb % 3][:, :],
                                kT_grp[j][:, kb * 128:(kb + 1) * 128],
                                qT_grp[j][:, qg * 512:(qg + 1) * 512],
                                start=True, stop=True)
                            sc_marks[("st", j, qg, kb)] = PE.inc(inst)

                        @block.scalar
                        def _(s, gr=gr, j=j, qg=qg, kb=kb, sb=sb):
                            s.wait_ge(PE.h, sc_marks[("st", j, qg, kb)])
                            s.wait_ge(PE.h, slab_free[qg % 2])
                            inst = s.activation(
                                stexp[:, sb + kb * 512:sb + (kb + 1) * 512],
                                st_ps[kb % 3][:, :], AF.Exp,
                                scale=1.0 / math.sqrt(128.0))
                            exp_marks[(j, qg, kb)] = SE.inc(inst)
                            st_free[kb % 3] = SE.n

                        if kb >= 4 * qg:
                            @block.vector
                            def _(v, gr=gr, j=j, qg=qg, kb=kb, sb=sb):
                                v.wait_ge(SE.h, exp_marks[(j, qg, kb)])
                                inst = None
                                for i in range(4):
                                    qt = 4 * qg + i
                                    sl = stexp[:, sb + kb * 512 + i * 128:
                                               sb + kb * 512 + (i + 1) * 128]
                                    if kb == qt:
                                        inst = v.tensor_mul(sl, sl, tri_bf[:, :])
                                    elif kb > qt:
                                        inst = v.memset(sl, 0.0)
                                if inst is not None:
                                    sc_marks[("mveq", j, qg, kb)] = VE.inc(inst)

                def f2_avs(gr, j, qg):
                    nonlocal avT_pe
                    sb = (qg % 2) * 14336
                    for i in range(4):
                        qt = 4 * qg + i

                        @block.tensor
                        def _(te, gr=gr, j=j, qg=qg, i=i, qt=qt, sb=sb):
                            te.wait_ge(VE.h, av_free[i % 2])
                            te.wait_ge(SE.h, exp_marks[(j, qg, 4 * qg + 3)])
                            vm = max((sc_marks.get(("mveq", j, qg, kb), 0)
                                      for kb in range(4 * qg, 4 * qg + 4)),
                                     default=0)
                            if vm:
                                te.wait_ge(VE.h, vm)
                            inst = None
                            for kb in range(qt + 1):
                                inst = te.matmul(
                                    av_ps[i % 2][:, 0:129],
                                    stexp[:, sb + kb * 512 + i * 128:
                                          sb + kb * 512 + (i + 1) * 128],
                                    V_grp[:, kb * 260 + j * 130:kb * 260 + j * 130 + 129],
                                    start=(kb == 0), stop=(kb == qt))
                            av_done_pe[(j, qg)] = PE.inc(inst)
                            sc_marks[("avmm", j, qt)] = PE.n

                        @block.vector
                        def _(v, gr=gr, j=j, i=i, qt=qt):
                            v.wait_ge(PE.h, sc_marks[("avmm", j, qt)])
                            v.wait_ge(PE.h, avT_marks.get(qt % 2, 0))
                            ch(v, VE, v.reciprocal(rcp[:, 1:2],
                                                   av_ps[i % 2][:, 128:129]))
                            inst = v.tensor_scalar_mul(
                                av_bf[qt % 2][:, :], av_ps[i % 2][:, 0:128],
                                rcp[:, 1:2])
                            av_free[i % 2] = VE.inc(inst)

                        @block.tensor
                        def _(te, j=j, qt=qt):
                            te.wait_ge(VE.h, VE.n)
                            te.wait_ge(VE.h, tp_read[qt % 2])
                            avT_marks[qt % 2] = PE.inc(te.transpose(
                                tp_ps[:, (qt % 2) * 128:(qt % 2) * 128 + 128],
                                av_bf[qt % 2][:, :], id_bf[:, :]))

                        @block.vector
                        def _(v, gr=gr, j=j, qt=qt):
                            v.wait_ge(PE.h, avT_marks[qt % 2])
                            if avs_tok.get(j):
                                AVS.wait(v, avs_tok[j])
                            inst = v.tensor_copy(
                                avstage[j][:, qt * 128:(qt + 1) * 128],
                                tp_ps[:, (qt % 2) * 128:(qt % 2) * 128 + 128])
                            tp_read[qt % 2] = VE.inc(inst)
                    slab_free[qg % 2] = PE.n

                for j in range(2):
                    hl = gr * 2 + j
                    for qg in range(8):
                        f2_scores(gr, j, qg)
                        if qg >= 1:
                            f2_avs(gr, j, qg - 1)
                        if qg % 3 == 2:
                            block.flush()
                    f2_avs(gr, j, 7)

                    @block.sync
                    def _(sy, gr=gr, j=j, hl=hl):
                        sy.wait_ge(VE.h, VE.n)
                        avs_tok[j] = AVS.load(
                            sy, j, ag_in_t.ap()[hl * 128:(hl + 1) * 128, :],
                            avstage[j][:, :])
                grp_pe_done[gr] = PE.n
                grp_ve_done[gr] = VE.n
                block.flush()

            @block.gpsimd
            def _(gp):
                for s_ in AVS.sems:
                    gp.wait_ge(s_.h, s_.n)
                if nranks == 8:
                    gp.collective_compute(
                        "AllGather", ALU.bypass,
                        replica_groups=[[0, 1], [2, 3], [4, 5], [6, 7]],
                        ins=[ag_in_t.ap().opt()], outs=[ag_out_t.ap().opt()],
                    ).then_inc(CC.h)
                    CC.n += 1
                else:
                    gp.dma_start(out=ag_out_t.ap()[0:NHH * 128, :],
                                 in_=ag_in_t.ap()).then_inc(GDO.h, 16)
                    gp.dma_start(out=ag_out_t.ap()[NHH * 128:, :],
                                 in_=ag_in_t.ap()).then_inc(GDO.h, 16)
                    GDO.n += 32
                    gp.wait_ge(GDO.h, GDO.n)
                    gp.wait_ge(CC.h, CC.n)
                    CC.inc(gp.memset(rcp[:, 0:1], 0.0))
            block.flush()
        f2_done_cc = CC.n

        if debug:
            dout("dbg_agin", [NHH * 128, S], BF16)

            @block.gpsimd
            def _(gp):
                gp.wait_ge(CC.h, CC.n)
                gp.dma_start(out=dbg["dbg_agin"], in_=ag_in_t.ap()).then_inc(GDO.h, 16)
                GDO.n += 16

        if phases < 5:
            return _finish_early(nc, dbg, block, out_own, g, OST, SYO, MIR, GDO, mir_total, scopes)
        kvq_scope.close()
        scopes.remove(kvq_scope)

        a2w = Slots(nc, top, "a2w", 1)
        a2c = Slots(nc, top, "a2c", 4)
        MHS = Slots(nc, top, "mhs", 2)
        a2x = Slots(nc, top, "a2x", 2)
        a2q = Slots(nc, top, "a2q", 1)
        a2_se_entry, a2_ve_entry = SE.n, VE.n
        with ExitStack() as ph:
            e0 = ph.enter_context
            wq_bf = e0(nc.sbuf_tensor("a2_wq", [128, KB * D], BF16))
            bq_bf = e0(nc.sbuf_tensor("a2_bq", [1, D], BF16))
            newmem = e0(nc.sbuf_tensor("a2_nm", [128, 16 * FS * 129], BF16))
            xnt_a = [e0(nc.sbuf_tensor(f"a2_xnt{i}", [128, KB * 128], BF16))
                     for i in range(2)]
            x2p = [e0(nc.sbuf_tensor(f"a2_x2p{i}", [128, D], BF16)) for i in range(2)]
            x2n = [e0(nc.sbuf_tensor(f"a2_x2n{i}", [128, D], BF16)) for i in range(2)]
            mq_bf = [e0(nc.sbuf_tensor(f"a2_mq{i}", [128, 16 * FDD], BF16))
                     for i in range(2)]
            sqa = e0(nc.sbuf_tensor("a2_sqa", [128, D], BF16))
            sqt = e0(nc.sbuf_tensor("a2_sqt", [128, D], BF16))
            nsta = [e0(nc.sbuf_tensor(f"a2_nst{i}", [128, 16], F32)) for i in range(2)]
            escr = e0(nc.sbuf_tensor("a2_escr", [128, 64], F32))
            epss = [e0(nc.sbuf_tensor(f"a2_eps{i}", [128, 16], F32)) for i in range(2)]
            mqT_sb = [e0(nc.sbuf_tensor(f"a2_mqT{i}", [128, FDD], BF16))
                      for i in range(2)]
            mh_sb = [e0(nc.sbuf_tensor(f"a2_mh{i}", [128, D], F32)) for i in range(2)]
            den_sc = e0(nc.sbuf_tensor("a2_den", [128, 4], F32))
            qp_ps = [e0(nc.psum_tensor(f"a2_qp{i}", [128, 512], F32)) for i in range(4)]
            na_ps = [e0(nc.psum_tensor(f"a2_na{i}", [128, 512], F32)) for i in range(2)]
            tpq_ps = [e0(nc.psum_tensor(f"a2_tp{i}", [128, 1024], BF16))
                      for i in range(2)]

            @block.gpsimd
            def _(gp):
                gp.wait_ge(VE.h, VE.n)
                gp.wait_ge(SE.h, SE.n)
                gp.wait_ge(PE.h, PE.n)
                gp.wait_ge(GDO.h, GDO.n)
                sem = a2w.sems[0]
                gp.dma_start(out=bq_bf[:, :], in_=g["bq"]).then_inc(sem.h, 16)
                sem.n += 16
            a2w_tok = (0, a2w.sems[0].n)

            nm_tok = {}

            @block.gpsimd
            def _(gp):
                gp.wait_ge(CC.h, ar_cc_mark)
                for s_ in AVS.sems:
                    gp.wait_ge(s_.h, s_.n)
                for i in range(4):
                    nm_tok[i] = a2c.load(
                        gp, i, newmem[:, i * 3096:(i + 1) * 3096],
                        ar_out_t.ap()[:, i * 3096:(i + 1) * 3096])

            xld = {}

            @block.sync
            def _(sy):
                sy.wait_ge(VE.h, VE.n)
                sy.wait_ge(SE.h, SE.n)
                sy.wait_ge(PE.h, PE.n)
                sy.wait_ge(XSP.h, XSP.n)
                sy.wait_ge(MIR.h, MIR.n)
                sy.dma_start(out=wq_bf[:, :].rearrange("p (k c) -> p k c", k=KB),
                             in_=wq_mir.rearrange("(k p) c -> p k c", p=128)).then_inc(a2q.sems[0].h, 16)
                a2q.sems[0].n += 16
                for j in (0, 1):
                    xld[j] = a2x.load(sy, j, xnt_a[j % 2][:, :],
                                      scr_xnT[j * 128:(j + 1) * 128, :])

            qpe = {}
            relu_se = {}
            mul_ve = {}
            nst_ve = {}
            eps_se = {}
            epss_ve = {}
            tp_pe = {}
            cp_mk = {}
            num_pe = {}
            epi_ve = {}
            mh_tok = {}

            def a2_xload(j):
                @block.sync
                def _(sy, j=j):
                    sy.wait_ge(PE.h, qpe[(j - 2, 3)])
                    xld[j] = a2x.load(sy, j, xnt_a[j % 2][:, :],
                                      scr_xnT[j * 128:(j + 1) * 128, :])

            def a2_qproj(t):
                @block.tensor
                def _(te, t=t):
                    if t == 0:
                        te.wait_ge(a2w.sems[0].h, a2w_tok[1])
                        te.wait_ge(a2q.sems[0].h, a2q.sems[0].n)
                        te.wait_ge(SE.h, a2_se_entry)
                        te.wait_ge(VE.h, a2_ve_entry)
                    a2x.wait(te, xld[t])
                    for cq in range(4):
                        if t >= 1:
                            te.wait_ge(SE.h, relu_se[(t - 1, cq)])
                        inst = None
                        for k in range(KB):
                            inst = te.matmul(
                                qp_ps[cq][:, :],
                                xnt_a[t % 2][:, k * 128:(k + 1) * 128],
                                wq_bf[:, k * D + cq * 512:k * D + cq * 512 + 512],
                                start=(k == 0), stop=False)
                        inst = te.matmul(qp_ps[cq][:, :], ones_r[:, :],
                                         bq_bf[:, cq * 512:(cq + 1) * 512],
                                         start=False, stop=True)
                        qpe[(t, cq)] = PE.inc(inst)

            def a2_relus(t):
                @block.scalar
                def _(s, t=t):
                    if t >= 2:
                        s.wait_ge(VE.h, mul_ve[t - 2])
                    for cq in range(4):
                        s.wait_ge(PE.h, qpe[(t, cq)])
                        s.activation(x2p[t % 2][:, cq * 512:(cq + 1) * 512],
                                     qp_ps[cq][:, :], AF.Relu)
                        inst = s.activation(x2n[t % 2][:, cq * 512:(cq + 1) * 512],
                                            qp_ps[cq][:, :], AF.Relu, scale=-1.0)
                        relu_se[(t, cq)] = SE.inc(inst)

            def a2_muls(t):
                @block.vector
                def _(v, t=t):
                    v.wait_ge(SE.h, relu_se[(t, 3)])
                    if t >= 2:
                        v.wait_ge(PE.h, tp_pe[(t - 2, 15)])
                        v.wait_ge(SE.h, eps_se[t - 2])
                    xp = x2p[t % 2]
                    xn = x2n[t % 2]
                    mq = mq_bf[t % 2]
                    inst = None
                    for j in (1, 2, 3):
                        bp = (j - 1) * 2 * D
                        bn = bp + D
                        inst = v.tensor_mul(mq[:, bp + j:bp + D],
                                            xp[:, j:D], xp[:, 0:D - j])
                        inst = v.tensor_mul(mq[:, bn + j:bn + D],
                                            xn[:, j:D], xn[:, 0:D - j])
                    ch(v, VE, inst)
                    pv = xp[:, :].rearrange("p (h c) -> p h c", c=128)
                    nv = xn[:, :].rearrange("p (h c) -> p h c", c=128)
                    for j in (1, 2, 3):
                        bp = (j - 1) * 2 * D
                        bn = bp + D
                        mpv = mq[:, bp:bp + D].rearrange("p (h c) -> p h c", c=128)
                        mnv = mq[:, bn:bn + D].rearrange("p (h c) -> p h c", c=128)
                        inst = v.tensor_mul(mpv[:, :, 0:j], pv[:, :, 0:j],
                                            nv[:, :, 128 - j:128])
                        inst = v.tensor_mul(mnv[:, :, 0:j], nv[:, :, 0:j],
                                            pv[:, :, 128 - j:128])
                    mul_ve[t] = VE.inc(inst)
                    v.wait_ge(VE.h, mul_ve[t])
                    ch(v, VE, v.tensor_mul(sqa[:, :], mq[:, 0:D], mq[:, 0:D]))
                    for b in range(1, 6):
                        ch(v, VE, v.tensor_mul(sqt[:, :], mq[:, b * D:(b + 1) * D],
                                               mq[:, b * D:(b + 1) * D]))
                        ch(v, VE, v.tensor_add(sqa[:, :], sqa[:, :], sqt[:, :]))
                    inst = v.reduce_sum(
                        nsta[t % 2][:, :],
                        sqa[:, :].rearrange("p (h c) -> p h c", c=128),
                        axis=AX.X)
                    nst_ve[t] = VE.inc(inst)

            def a2_eps(t):
                @block.scalar
                def _(s, t=t):
                    s.wait_ge(VE.h, nst_ve[t])
                    if t >= 2:
                        s.wait_ge(VE.h, epss_ve[t - 2])
                    a = escr[:, 32 * (t % 2):32 * (t % 2) + 16]
                    b = escr[:, 32 * (t % 2) + 16:32 * (t % 2) + 32]
                    ch(s, SE, s.activation(a, nsta[t % 2][:, :], AF.Ln))
                    eps_se[t] = SE.inc(s.activation(b, a, AF.Exp, scale=0.5))

            def a2_epss(t):
                @block.vector
                def _(v, t=t):
                    v.wait_ge(SE.h, eps_se[t])
                    epss_ve[t] = VE.inc(v.tensor_scalar_mul(
                        epss[t % 2][:, :],
                        escr[:, 32 * (t % 2) + 16:32 * (t % 2) + 32], EPS_ATT))

            def a2_tp(t, h):
                @block.tensor
                def _(te, t=t, h=h):
                    if h == 0:
                        te.wait_ge(VE.h, mul_ve[t])
                    if h >= 2:
                        e_, m_ = cp_mk[(t, h - 2)]
                        te.wait_ge(e_.h, m_)
                    elif t >= 1:
                        e_, m_ = cp_mk[(t - 1, 14 + h)]
                        te.wait_ge(e_.h, m_)
                    inst = None
                    for fs in range(FS):
                        inst = te.transpose(
                            tpq_ps[h % 2][:, fs * 128:(fs + 1) * 128],
                            mq_bf[t % 2][:, fs * D + h * 128:fs * D + (h + 1) * 128],
                            id_bf[:, :])
                    tp_pe[(t, h)] = PE.inc(inst)

            def a2_copy(t, h):
                def waits(eng):
                    eng.wait_ge(PE.h, tp_pe[(t, h)])
                    if h >= 2:
                        eng.wait_ge(PE.h, num_pe[(t, h - 2)])
                    elif t >= 1:
                        eng.wait_ge(PE.h, num_pe[(t - 1, 14 + h)])
                if h % 2 == 0:
                    @block.vector
                    def _(v, t=t, h=h):
                        waits(v)
                        cp_mk[(t, h)] = (VE, VE.inc(v.tensor_copy(
                            mqT_sb[h % 2][:, :], tpq_ps[h % 2][:, 0:FDD])))
                else:
                    @block.scalar
                    def _(s, t=t, h=h):
                        waits(s)
                        cp_mk[(t, h)] = (SE, SE.inc(s.activation(
                            mqT_sb[h % 2][:, :], tpq_ps[h % 2][:, 0:FDD],
                            AF.Identity)))

            def a2_num(t, h):
                @block.tensor
                def _(te, t=t, h=h):
                    e_, m_ = cp_mk[(t, h)]
                    te.wait_ge(e_.h, m_)
                    if t == 0 and h == 0:
                        for i in range(4):
                            a2c.wait(te, nm_tok[i])
                    if h >= 2:
                        te.wait_ge(VE.h, epi_ve[(t, h - 2)])
                    elif t >= 1:
                        te.wait_ge(VE.h, epi_ve[(t - 1, 14 + h)])
                    inst = None
                    for fs in range(FS):
                        inst = te.matmul(
                            na_ps[h % 2][:, 0:129],
                            mqT_sb[h % 2][:, fs * 128:(fs + 1) * 128],
                            newmem[:, (h * FS + fs) * 129:(h * FS + fs + 1) * 129],
                            start=(fs == 0), stop=(fs == FS - 1))
                    num_pe[(t, h)] = PE.inc(inst)

            def a2_epi(t, h):
                @block.vector
                def _(v, t=t, h=h):
                    v.wait_ge(PE.h, num_pe[(t, h)])
                    if h == 0 and t >= 2:
                        MHS.wait(v, mh_tok[t - 2])
                    a = den_sc[:, 2 * (h % 2):2 * (h % 2) + 1]
                    b = den_sc[:, 2 * (h % 2) + 1:2 * (h % 2) + 2]
                    ch(v, VE, v.tensor_add(a, na_ps[h % 2][:, 128:129],
                                           epss[t % 2][:, h:h + 1]))
                    ch(v, VE, v.reciprocal(b, a))
                    inst = v.tensor_scalar_mul(
                        mh_sb[t % 2][:, h * 128:(h + 1) * 128],
                        na_ps[h % 2][:, 0:128], b)
                    epi_ve[(t, h)] = VE.inc(inst)

            a2_qproj(0)
            a2_relus(0)
            a2_muls(0)
            block.flush()

            for t in range(T_OWN):
                if t + 2 < T_OWN:
                    a2_xload(t + 2)
                a2_eps(t)
                a2_epss(t)
                for h in range(16):
                    if h == 0 and t + 1 < T_OWN:
                        a2_qproj(t + 1)
                    a2_tp(t, h)
                    if h == 0 and t + 1 < T_OWN:
                        a2_relus(t + 1)
                    a2_copy(t, h)
                    if h >= 1:
                        a2_num(t, h - 1)
                        a2_epi(t, h - 1)
                a2_num(t, 15)
                a2_epi(t, 15)

                @block.sync
                def _(sy, t=t):
                    sy.wait_ge(VE.h, epi_ve[(t, 15)])
                    mh_tok[t] = MHS.load(sy, t, scr_mh[t * 128:(t + 1) * 128, :],
                                         mh_sb[t % 2][:, :])
                if t + 1 < T_OWN:
                    a2_muls(t + 1)
                block.flush()

        if debug:
            dout("dbg_mh", [TOK, D])

            @block.gpsimd
            def _(gp):
                for s_ in MHS.sems:
                    gp.wait_ge(s_.h, s_.n)
                gp.dma_start(out=dbg["dbg_mh"], in_=scr_mh).then_inc(GDO.h, 16)
                GDO.n += 16

        if phases < 6:
            return _finish_early(nc, dbg, block, out_own, g, OST, SYO, MIR, GDO, mir_total, scopes)
        HT_scope = ExitStack()
        scopes.append(HT_scope)
        HT = HT_scope.enter_context(
            nc.sbuf_tensor("HT", [128, KB * T_OWN * 128], BF16))
        w8c_all = HT_scope.enter_context(
            nc.sbuf_tensor("w8c", [128, T_OWN * 8], F32))
        f3w = Slots(nc, top, "f3w", 1)
        agl = Slots(nc, top, "agl", 2)
        HST = Slots(nc, top, "hst", 2)
        with ExitStack() as ph:
            e0 = ph.enter_context
            ow_bf = e0(nc.sbuf_tensor("f3_ow", [128, KB * D], BF16))
            avt = [e0(nc.sbuf_tensor(f"f3_avt{i}", [128, KB * 128], BF16))
                   for i in range(2)]
            avt_ab = [e0(nc.sbuf_tensor(f"f3_ab{i}", [128, 2 * KB * 128], BF16))
                      for i in range(2)]
            xf3 = [e0(nc.sbuf_tensor(f"f3_xf{i}", [128, D], F32)) for i in range(2)]
            xn_f = e0(nc.sbuf_tensor("f3_xn", [128, D], F32))
            h_sb = [e0(nc.sbuf_tensor(f"f3_h{i}", [128, D], F32)) for i in range(2)]
            hbf = e0(nc.sbuf_tensor("f3_hbf", [128, D], BF16))
            st3 = e0(nc.sbuf_tensor("f3_st", [128, 4], F32))
            sq3 = e0(nc.sbuf_tensor("f3_sq", [128, D], F32))
            op_ps = [e0(nc.psum_tensor(f"f3_op{i}", [128, 512], F32)) for i in range(2)]
            tp3_ps = [e0(nc.psum_tensor(f"f3_tp{i}", [128, 1024], BF16)) for i in range(2)]

            @block.gpsimd
            def _(gp):
                gp.wait_ge(VE.h, VE.n)
                gp.wait_ge(SE.h, SE.n)
                gp.wait_ge(GDO.h, GDO.n)
                sem = f3w.sems[0]
                gp.dma_start(out=ow_bf[:, :].rearrange("p (k c) -> p k c", k=KB),
                             in_=g["o_w"].rearrange("(k p) c -> p k c", p=128)).then_inc(sem.h, 16)
                sem.n += 16
            f3w_tok = (0, f3w.sems[0].n)

            ag_tok = {}
            sel_ve = {}
            op_pe = {}
            op_read = [0, 0]
            h_ve = {}
            h_se = {}
            h_tok = {}
            tp3_pe = {}
            tp3_read = [0, 0]
            ht_ve = {}
            xld_f3 = {}

            for t in range(T_OWN):
                @block.sync
                def _(sy, t=t):
                    if t == 0:
                        sy.wait_ge(CC.h, f2_done_cc)
                        sy.wait_ge(PE.h, PE.n)
                        sy.wait_ge(SE.h, SE.n)
                        sy.wait_ge(VE.h, VE.n)
                        for s_ in MHS.sems:
                            sy.wait_ge(s_.h, s_.n)
                    if t >= 2:
                        sy.wait_ge(VE.h, sel_ve[t - 2])
                        sy.wait_ge(SE.h, h_se[t - 2])
                    sl = t % 2
                    s_ = agl.sems[sl]
                    dst = avt_ab[sl][:, :].rearrange("p (a k c) -> p a k c",
                                                     a=2, k=KB)
                    for a in range(2):
                        src = ag_out_t.ap()[:, a * TOK + t * 128:a * TOK + (t + 1) * 128]
                        sy.dma_start(out=dst[:, a, :, :],
                                     in_=src.rearrange("(k p) c -> p k c", p=128)).then_inc(s_.h, 16)
                        s_.n += 16
                    ag_tok[t] = (sl, s_.n)
                    xld3 = xls.load(sy, t, xf3[sl][:, :],
                                    g["x_own"][t * 128:(t + 1) * 128, :])
                    xld_f3[t] = xld3

                @block.vector
                def _(v, t=t):
                    agl.wait(v, ag_tok[t])
                    sl = t % 2
                    ab = avt_ab[sl]
                    ch(v, VE, v.tensor_scalar_mul(avt[sl][:, :],
                                                  ab[:, 0:KB * 128], sel_sb[:, 0:1]))
                    inst = v.scalar_tensor_tensor(avt[sl][:, :], ab[:, KB * 128:],
                                                  sel_sb[:, 1:2], avt[sl][:, :],
                                                  ALU.mult, ALU.add)
                    sel_ve[t] = VE.inc(inst)

                @block.scalar
                def _(s, t=t):
                    xls.wait(s, xld_f3[t])
                    SE.inc(s.activation(xn_f[:, :], xf3[t % 2][:, :], AF.Identity,
                                        scale=rstd_own[:, t:t + 1],
                                        bias=nmr_own[:, t:t + 1]))
                xn_se = SE.n

                for cq in range(4):
                    @block.tensor
                    def _(te, t=t, cq=cq):
                        if t == 0 and cq == 0:
                            te.wait_ge(f3w.sems[0].h, f3w_tok[1])
                        te.wait_ge(VE.h, sel_ve[t])
                        te.wait_ge(VE.h, op_read[cq % 2])
                        inst = None
                        for k in range(KB):
                            inst = te.matmul(
                                op_ps[cq % 2][:, :],
                                avt[t % 2][:, k * 128:(k + 1) * 128],
                                ow_bf[:, k * D + cq * 512:k * D + cq * 512 + 512],
                                start=(k == 0), stop=(k == KB - 1))
                        op_pe[(t, cq)] = PE.inc(inst)

                    @block.vector
                    def _(v, t=t, cq=cq):
                        v.wait_ge(SE.h, xn_se)
                        v.wait_ge(PE.h, op_pe[(t, cq)])
                        if cq == 0 and t >= 2:
                            HST.wait(v, h_tok[t - 2])
                        inst = v.tensor_add(
                            h_sb[t % 2][:, cq * 512:(cq + 1) * 512],
                            op_ps[cq % 2][:, :], xn_f[:, cq * 512:(cq + 1) * 512])
                        op_read[cq % 2] = VE.inc(inst)
                h_ve[t] = VE.n

                @block.scalar
                def _(s, t=t):
                    s.wait_ge(VE.h, h_ve[t])
                    s.activation(sq3[:, :], h_sb[t % 2][:, :], AF.Square,
                                 accum_out=st3[:, 0:1])
                    s.drain()
                    SE.inc(s.activation(hbf[:, :], h_sb[t % 2][:, :], AF.Identity))
                h_se[t] = SE.n

                @block.vector
                def _(v, t=t):
                    v.wait_ge(SE.h, h_se[t])
                    VE.inc(v.tensor_scalar(st3[:, 1:2], st3[:, 0:1], 1.0 / D,
                                           EPS_RMS, ALU.mult, ALU.add))

                emit_rsqrt_newton(None, None, rstd2_own[:, t:t + 1], st3[:, 1:2],
                                  st3[:, 2:3])

                @block.sync
                def _(sy, t=t):
                    sy.wait_ge(VE.h, h_ve[t])
                    h_tok[t] = HST.load(sy, t, scr_h[t * 128:(t + 1) * 128, :],
                                        h_sb[t % 2][:, :])

                for hh in range(2):
                    @block.tensor
                    def _(te, t=t, hh=hh):
                        te.wait_ge(SE.h, h_se[t])
                        te.wait_ge(VE.h, tp3_read[hh])
                        inst = None
                        for j in range(8):
                            k = hh * 8 + j
                            inst = te.transpose(tp3_ps[hh][:, j * 128:(j + 1) * 128],
                                                hbf[:, k * 128:(k + 1) * 128],
                                                id_bf[:, :])
                        tp3_pe[(t, hh)] = PE.inc(inst)

                    @block.vector
                    def _(v, t=t, hh=hh):
                        v.wait_ge(PE.h, tp3_pe[(t, hh)])
                        inst = None
                        for j in range(8):
                            k = hh * 8 + j
                            inst = v.tensor_scalar_mul(
                                HT[:, (k * T_OWN + t) * 128:(k * T_OWN + t) * 128 + 128],
                                tp3_ps[hh][:, j * 128:(j + 1) * 128],
                                ln2_sb[:, k:k + 1])
                        tp3_read[hh] = VE.inc(inst)
                    ht_ve[t] = VE.n
                if t % 4 == 3:
                    block.flush()
            f3_ht_ve = VE.n
            block.flush()

        if phases < 7:
            return _finish_early(nc, dbg, block, out_own, g, OST, SYO, MIR, GDO, mir_total, scopes)
        m1w = Slots(nc, top, "m1w", 4)
        HES = Slots(nc, top, "hes", 2)
        with ExitStack() as ph:
            e0 = ph.enter_context
            rw_bf = e0(nc.sbuf_tensor("m1_rw", [128, KB * 7], BF16))
            rb_bf = e0(nc.sbuf_tensor("m1_rb", [1, 7], BF16))
            wgr = [e0(nc.sbuf_tensor(f"m1_wg{i}", [128, KB * 512], BF16))
                   for i in range(2)]
            wur = [e0(nc.sbuf_tensor(f"m1_wu{i}", [128, KB * 512], BF16))
                   for i in range(2)]
            pr_sb = e0(nc.sbuf_tensor("m1_pr", [128, 7], F32))
            nrstd2_sb = e0(nc.sbuf_tensor("m1_nrstd2", [128, T_OWN], F32))
            tt_sb = e0(nc.sbuf_tensor("m1_tt", [128, 12], F32))
            rsc_sb = e0(nc.sbuf_tensor("m1_rsc", [128, 28], F32))
            silu_sb = [e0(nc.sbuf_tensor(f"m1_silu{i}", [128, 512], F32))
                       for i in range(2)]
            he_bf = [e0(nc.sbuf_tensor(f"m1_he{i}", [128, 512], BF16))
                     for i in range(2)]
            stg = [e0(nc.sbuf_tensor(f"m1_stg{i}", [128, 4 * T_OWN * 128], BF16))
                   for i in range(2)]
            rt_ps = e0(nc.psum_tensor("m1_rt", [128, 512], F32))
            pg_ps = [e0(nc.psum_tensor(f"m1_pg{i}", [128, 512], F32)) for i in range(2)]
            pu_ps = [e0(nc.psum_tensor(f"m1_pu{i}", [128, 512], F32)) for i in range(2)]
            te_ps = [e0(nc.psum_tensor(f"m1_te{i}", [128, 1024], BF16))
                     for i in range(2)]

            @block.gpsimd
            def _(gp):
                gp.wait_ge(VE.h, VE.n)
                gp.wait_ge(SE.h, SE.n)
                for s_ in HST.sems:
                    gp.wait_ge(s_.h, s_.n)
                sem = m1w.sems[2]
                gp.dma_start(out=rw_bf[:, :].rearrange("p (k c) -> p k c", k=KB),
                             in_=g["router_w"].rearrange("(k p) c -> p k c", p=128)).then_inc(sem.h, 16)
                gp.dma_start(out=rb_bf[:, :], in_=g["router_b"]).then_inc(sem.h, 16)
                sem.n += 32
            m1r_tok = (2, m1w.sems[2].n)

            rt_pe = {}
            rt_read = 0

            @block.vector
            def _(v):
                v.wait_ge(SE.h, SE.n)
                v.wait_ge(VE.h, VE.n)
                VE.inc(v.tensor_scalar_mul(nrstd2_sb[:, :], rstd2_own[:, :], -1.0))

            for t in range(T_OWN):
                @block.tensor
                def _(te, t=t):
                    if t == 0:
                        te.wait_ge(m1w.sems[2].h, m1r_tok[1])
                        te.wait_ge(VE.h, f3_ht_ve)
                    te.wait_ge(VE.h, rt_read)
                    inst = None
                    for k in range(KB):
                        inst = te.matmul(rt_ps[:, 0:7],
                                         HT[:, (k * T_OWN + t) * 128:(k * T_OWN + t) * 128 + 128],
                                         rw_bf[:, k * 7:(k + 1) * 7],
                                         start=(k == 0), stop=False)
                    inst = te.matmul(rt_ps[:, 0:7], ones_r[:, :], rb_bf[:, :],
                                     start=False, stop=True)
                    rt_pe[t] = PE.inc(inst)

                @block.scalar
                def _(s, t=t):
                    s.wait_ge(PE.h, rt_pe[t])
                    s.wait_ge(VE.h, VE.n)
                    SE.inc(s.activation(pr_sb[:, :], rt_ps[:, 0:7], AF.Exp,
                                        scale=nrstd2_sb[:, t:t + 1]))

                @block.vector
                def _(v, t=t):
                    nonlocal rt_read
                    v.wait_ge(SE.h, SE.n)
                    ch(v, VE, v.tensor_scalar_add(pr_sb[:, :], pr_sb[:, :], 1.0))
                    ch(v, VE, v.reciprocal(pr_sb[:, :], pr_sb[:, :]))
                    eq1 = rsc_sb[:, 0:7]
                    prm = rsc_sb[:, 7:14]
                    selm = rsc_sb[:, 14:21]
                    wraw = rsc_sb[:, 21:28]
                    ch(v, VE, v.tensor_reduce(tt_sb[:, 1:2], pr_sb[:, :], AX.X,
                                              ALU.max))
                    ch(v, VE, v.tensor_scalar(eq1, pr_sb[:, :], tt_sb[:, 1:2],
                                              None, ALU.is_ge))
                    ch(v, VE, v.scalar_tensor_tensor(prm, eq1, -1e9, pr_sb[:, :],
                                                     ALU.mult, ALU.add))
                    ch(v, VE, v.tensor_reduce(tt_sb[:, 2:3], prm, AX.X, ALU.max))
                    ch(v, VE, v.tensor_scalar(selm, pr_sb[:, :], tt_sb[:, 2:3],
                                              None, ALU.is_ge))
                    ch(v, VE, v.tensor_mul(wraw, pr_sb[:, :], selm))
                    ch(v, VE, v.tensor_add(tt_sb[:, 3:4], tt_sb[:, 1:2],
                                           tt_sb[:, 2:3]))
                    ch(v, VE, v.reciprocal(tt_sb[:, 4:5], tt_sb[:, 3:4]))
                    ch(v, VE, v.tensor_scalar_mul(tt_sb[:, 5:6],
                                                  rstd2_own[:, t:t + 1], 0.125))
                    ch(v, VE, v.tensor_scalar(w8c_all[:, t * 8 + 1:t * 8 + 8],
                                              wraw, tt_sb[:, 4:5], tt_sb[:, 5:6],
                                              ALU.mult, ALU.mult))
                    inst = v.tensor_copy(w8c_all[:, t * 8:t * 8 + 1], tt_sb[:, 5:6])
                    VE.inc(inst)
                    rt_read = VE.n
                if t % 4 == 3:
                    block.flush()
            router_ve = VE.n

            gl_tok = {}
            m1_pe = {}
            m1_silu_marks = {}
            m1_he_marks = {}
            m1_tp_marks = {}
            m1_stg_ve = {}
            hes_tok = {}
            pg_read = [0, 0]
            pu_read = [0, 0]
            for e in range(NE):
                for qq in range(2):
                    gi2 = e * 2 + qq
                    sl = gi2 % 2

                    if gi2 >= 8:
                        @block.gpsimd
                        def _(gp, gi2=gi2):
                            gp.dma_start(out=mir_down2[gi2 - 8],
                                         in_=g["m_down"][gi2 - 8]).then_inc(MIR.h, 16)
                            MIR.n += 16

                    @block.gpsimd
                    def _(gp, e=e, qq=qq, gi2=gi2, sl=sl):
                        if gi2 >= 2:
                            gp.wait_ge(PE.h, m1_pe[(gi2 - 2, T_OWN - 1)])
                        s_ = m1w.sems[sl]
                        for dst, w in ((wgr[sl], g["m_gate"]), (wur[sl], g["m_up"])):
                            src = w[e].rearrange("(k p) c -> p k c", p=128)[
                                :, :, qq * 512:(qq + 1) * 512]
                            gp.dma_start(out=dst[:, :].rearrange(
                                "p (k c) -> p k c", k=KB), in_=src).then_inc(s_.h, 16)
                            s_.n += 16
                        gl_tok[gi2] = (sl, s_.n)

                    for t in range(T_OWN):
                        @block.tensor
                        def _(te, e=e, qq=qq, gi2=gi2, sl=sl, t=t):
                            if t == 0:
                                te.wait_ge(m1w.sems[sl].h, gl_tok[gi2][1])
                                if gi2 == 0:
                                    te.wait_ge(VE.h, router_ve)
                            te.wait_ge(SE.h, pg_read[t % 2])
                            te.wait_ge(VE.h, pu_read[t % 2])
                            inst = None
                            for k in range(KB):
                                te.matmul(pg_ps[t % 2][:, :],
                                          HT[:, (k * T_OWN + t) * 128:(k * T_OWN + t) * 128 + 128],
                                          wgr[sl][:, k * 512:(k + 1) * 512],
                                          start=(k == 0), stop=(k == KB - 1))
                            for k in range(KB):
                                inst = te.matmul(
                                    pu_ps[t % 2][:, :],
                                    HT[:, (k * T_OWN + t) * 128:(k * T_OWN + t) * 128 + 128],
                                    wur[sl][:, k * 512:(k + 1) * 512],
                                    start=(k == 0), stop=(k == KB - 1))
                            m1_pe[(gi2, t)] = PE.inc(inst)

                        @block.scalar
                        def _(s, e=e, gi2=gi2, t=t):
                            s.wait_ge(PE.h, m1_pe[(gi2, t)])
                            pk = (gi2, t - 2) if t >= 2 else (gi2 - 1, 14 + t)
                            s.wait_ge(VE.h, m1_he_marks.get(pk, 0))
                            inst = s.activation(silu_sb[t % 2][:, :],
                                                pg_ps[t % 2][:, :], AF.Silu,
                                                scale=rstd2_own[:, t:t + 1])
                            m1_silu_marks[(gi2, t)] = SE.inc(inst)
                            pg_read[t % 2] = SE.n

                        @block.vector
                        def _(v, e=e, gi2=gi2, t=t):
                            v.wait_ge(SE.h, m1_silu_marks[(gi2, t)])
                            pk = (gi2, t - 2) if t >= 2 else (gi2 - 1, 14 + t)
                            v.wait_ge(PE.h, m1_tp_marks.get(pk, 0))
                            inst = v.scalar_tensor_tensor(
                                he_bf[t % 2][:, :], pu_ps[t % 2][:, :],
                                w8c_all[:, t * 8 + e:t * 8 + e + 1],
                                silu_sb[t % 2][:, :], ALU.mult, ALU.mult)
                            m1_he_marks[(gi2, t)] = VE.inc(inst)
                            pu_read[t % 2] = VE.n

                        def m1_tp(gi2, sl, t):
                            @block.tensor
                            def _(te, gi2=gi2, t=t):
                                te.wait_ge(VE.h, m1_he_marks[(gi2, t)])
                                pk = (gi2, t - 2) if t >= 2 else (gi2 - 1, 14 + t)
                                te.wait_ge(VE.h, m1_stg_ve.get(pk, 0))
                                inst = None
                                for j in range(4):
                                    inst = te.transpose(te_ps[t % 2][:, j * 128:(j + 1) * 128],
                                                        he_bf[t % 2][:, j * 128:(j + 1) * 128],
                                                        id_bf[:, :])
                                m1_tp_marks[(gi2, t)] = PE.inc(inst)

                            @block.vector
                            def _(v, gi2=gi2, sl=sl, t=t):
                                v.wait_ge(PE.h, m1_tp_marks[(gi2, t)])
                                if t == 0 and hes_tok.get(sl):
                                    HES.wait(v, hes_tok[sl])
                                dst = stg[sl][:, :].rearrange("p (f t c) -> p f t c",
                                                              f=4, t=T_OWN)[:, :, t, :]
                                inst = v.tensor_copy(dst, te_ps[t % 2][:, 0:512])
                                m1_stg_ve[(gi2, t)] = VE.inc(inst)

                        if t >= 1:
                            m1_tp(gi2, sl, t - 1)
                        if t == T_OWN - 1:
                            m1_tp(gi2, sl, t)

                    @block.sync
                    def _(sy, e=e, qq=qq, gi2=gi2, sl=sl):
                        sy.wait_ge(VE.h, m1_stg_ve[(gi2, T_OWN - 1)])
                        s_ = HES.sems[sl]
                        hv = scr_heT[e].rearrange("(t p) (k c) -> p t k c",
                                                  p=128, c=128)
                        sv = stg[sl][:, :].rearrange("p (f t c) -> p f t c",
                                                     f=4, t=T_OWN)
                        for f in range(4):
                            sy.dma_start(out=hv[:, :, qq * 4 + f, :],
                                         in_=sv[:, f, :, :]).then_inc(s_.h, 16)
                            s_.n += 16
                        hes_tok[sl] = (sl, s_.n)
                    block.flush()
        HT_scope.close()
        scopes.remove(HT_scope)

        if phases < 8:
            return _finish_early(nc, dbg, block, out_own, g, OST, SYO, MIR, GDO, mir_total, scopes)
        m2w = Slots(nc, top, "m2w", 1)
        m2h = Slots(nc, top, "m2h", 2)
        m2x = Slots(nc, top, "m2x", 4)
        GST = Slots(nc, top, "gst", 2)
        with ExitStack() as ph:
            e0 = ph.enter_context
            wd_sb = [e0(nc.sbuf_tensor(f"m2_wd{i}", [128, 8 * D], BF16))
                     for i in range(4)]
            het = [e0(nc.sbuf_tensor(f"m2_het{i}", [128, 4 * 8 * 128], BF16))
                   for i in range(2)]
            hld_sb = [e0(nc.sbuf_tensor(f"m2_h{i}", [128, D], F32)) for i in range(2)]
            m1ld = [e0(nc.sbuf_tensor(f"m2_m1{i}", [128, D], F32)) for i in range(2)]
            g_sb = [e0(nc.sbuf_tensor(f"m2_g{i}", [128, D], F32)) for i in range(2)]
            acc_ps = [e0(nc.psum_tensor(f"m2_acc{i}", [128, 512], F32))
                      for i in range(8)]

            m2_pe = {}
            m2_ve = {}
            m2_se = {}
            mo_tok = {}
            g2_tok = {}
            wd_tok = {}
            het_tok = {}
            hm_tok = {}
            for half in range(2):
                @block.gpsimd
                def _(gp, half=half):
                    if half == 0:
                        gp.wait_ge(PE.h, PE.n)
                        gp.wait_ge(VE.h, VE.n)
                        gp.wait_ge(SE.h, SE.n)
                        for s_ in HES.sems:
                            gp.wait_ge(s_.h, s_.n)
                        for s_ in HST.sems:
                            gp.wait_ge(s_.h, s_.n)
                    else:
                        gp.wait_ge(PE.h, m2_pe[(0, T_OWN - 1)])
                    gp.wait_ge(MIR.h, MIR.n)
                    s_ = m2w.sems[0]
                    for i in range(4):
                        e_ = half * 4 + i
                        gp.dma_start(out=wd_sb[i][:, :].rearrange(
                            "p (k c) -> p k c", k=8),
                            in_=mir_down2[e_].rearrange("(k p) c -> p k c", p=128)).then_inc(s_.h, 16)
                        s_.n += 16
                    wd_tok[half] = (0, s_.n)

                for t in range(T_OWN):
                    @block.sync
                    def _(sy, half=half, t=t):
                        sl = t % 2
                        s_ = m2h.sems[sl]
                        if half == 0 and t == 0:
                            for s2 in HES.sems:
                                sy.wait_ge(s2.h, s2.n)
                        if t >= 2:
                            sy.wait_ge(PE.h, m2_pe[(half, t - 2)])
                        for i in range(4):
                            e_ = half * 4 + i
                            src = scr_heT[e_][t * 128:(t + 1) * 128, :]
                            sy.dma_start(
                                out=het[sl][:, i * 1024:(i + 1) * 1024],
                                in_=src).then_inc(s_.h, 16)
                            s_.n += 16
                        het_tok[(half, t)] = (sl, s_.n)
                        if half == 1:
                            if t >= 2:
                                sy.wait_ge(SE.h, m2_se[t - 2])
                            s2 = m2x.sems[t % 2]
                            sy.dma_start(out=hld_sb[t % 2][:, :],
                                         in_=scr_h[t * 128:(t + 1) * 128, :]).then_inc(s2.h, 16)
                            s2.n += 16
                            s3 = m2x.sems[2 + t % 2]
                            sy.dma_start(out=m1ld[t % 2][:, :],
                                         in_=scr_moe1[t * 128:(t + 1) * 128, :]).then_inc(s3.h, 16)
                            s3.n += 16
                            hm_tok[t] = ((t % 2, s2.n), (2 + t % 2, s3.n))

                    @block.tensor
                    def _(te, half=half, t=t):
                        sl = t % 2
                        if t == 0:
                            te.wait_ge(m2w.sems[0].h, wd_tok[half][1])
                        m2h.wait(te, het_tok[(half, t)])
                        te.wait_ge(VE.h, m2_ve.get((half, t - 2), 0))
                        inst = None
                        for nch in range(4):
                            for i in range(4):
                                for k in range(8):
                                    inst = te.matmul(
                                        acc_ps[(t % 2) * 4 + nch][:, :],
                                        het[sl][:, (i * 8 + k) * 128:(i * 8 + k) * 128 + 128],
                                        wd_sb[i][:, k * D + nch * 512:k * D + nch * 512 + 512],
                                        start=(i == 0 and k == 0),
                                        stop=(i == 3 and k == 7))
                        m2_pe[(half, t)] = PE.inc(inst)

                    if half == 0:
                        @block.vector
                        def _(v, t=t):
                            v.wait_ge(PE.h, m2_pe[(0, t)])
                            if t >= 2:
                                GST.wait(v, mo_tok[t - 2])
                            inst = None
                            for nch in range(4):
                                inst = v.tensor_copy(
                                    g_sb[t % 2][:, nch * 512:(nch + 1) * 512],
                                    acc_ps[(t % 2) * 4 + nch][:, :])
                            m2_ve[(0, t)] = VE.inc(inst)

                        @block.sync
                        def _(sy, t=t):
                            sy.wait_ge(VE.h, m2_ve[(0, t)])
                            mo_tok[t] = GST.load(
                                sy, t, scr_moe1[t * 128:(t + 1) * 128, :],
                                g_sb[t % 2][:, :])
                        if t % 4 == 3:
                            block.flush()
                    else:
                        @block.vector
                        def _(v, t=t):
                            v.wait_ge(PE.h, m2_pe[(1, t)])
                            m2x.wait(v, hm_tok[t][0])
                            m2x.wait(v, hm_tok[t][1])
                            if t >= 2:
                                GST.wait(v, g2_tok[t - 2])
                            else:
                                GST.wait(v, mo_tok[14 + t])
                            inst = None
                            for nch in range(4):
                                sl4 = (t % 2) * 4 + nch
                                cs_ = slice(nch * 512, (nch + 1) * 512)
                                ch(v, VE, v.tensor_add(g_sb[t % 2][:, cs_],
                                                       acc_ps[sl4][:, :],
                                                       m1ld[t % 2][:, cs_]))
                                inst = v.tensor_add(g_sb[t % 2][:, cs_],
                                                    g_sb[t % 2][:, cs_],
                                                    hld_sb[t % 2][:, cs_])
                                VE.inc(inst)
                            m2_ve[(1, t)] = VE.n

                        @block.scalar
                        def _(s, t=t):
                            s.wait_ge(VE.h, m2_ve[(1, t)])
                            inst = s.activation(g_sb[t % 2][:, :], g_sb[t % 2][:, :],
                                                AF.Silu)
                            m2_se[t] = SE.inc(inst)

                        @block.sync
                        def _(sy, t=t):
                            sy.wait_ge(SE.h, m2_se[t])
                            g2_tok[t] = GST.load(
                                sy, t, scr_g[t * 128:(t + 1) * 128, :],
                                g_sb[t % 2][:, :])
                    if t % 4 == 3:
                        block.flush()

            block.flush()

        if phases < 9:
            return _finish_early(nc, dbg, block, out_own, g, OST, SYO, MIR, GDO, mir_total, scopes)
        a2bw = Slots(nc, top, "a2bw", 1)
        a2bx = Slots(nc, top, "a2bx", 6)
        with ExitStack() as ph:
            e0 = ph.enter_context
            xnT3 = e0(nc.sbuf_tensor("a2b_xnT", [128, KB * T_OWN * 128], BF16))
            wg_bf = e0(nc.sbuf_tensor("a2b_wg", [128, KB * D], BF16))
            bg_bf = e0(nc.sbuf_tensor("a2b_bg", [1, D], BF16))
            xfb = [e0(nc.sbuf_tensor(f"a2b_x{i}", [128, D], F32)) for i in range(2)]
            mhb = [e0(nc.sbuf_tensor(f"a2b_mh{i}", [128, D], F32)) for i in range(2)]
            ggb = [e0(nc.sbuf_tensor(f"a2b_g{i}", [128, D], F32)) for i in range(2)]
            xnb = e0(nc.sbuf_tensor("a2b_xn", [128, D], F32))
            gate_sb = e0(nc.sbuf_tensor("a2b_gate", [128, D], F32))
            gp_ps = [e0(nc.psum_tensor(f"a2b_gp{i}", [128, 512], F32)) for i in range(2)]

            @block.gpsimd
            def _(gp):
                gp.wait_ge(VE.h, VE.n)
                gp.wait_ge(SE.h, SE.n)
                for s_ in GST.sems:
                    gp.wait_ge(s_.h, s_.n)
                for s_ in m2h.sems + m2x.sems + m2w.sems:
                    pass
                sem = a2bw.sems[0]
                x3v = xnT3[:, :].rearrange("p (k t c) -> p k t c", t=T_OWN, c=128)
                for t_ in range(T_OWN):
                    gp.dma_start(out=x3v[:, :, t_, :],
                                 in_=scr_xnT[t_ * 128:(t_ + 1) * 128, :].rearrange(
                                     "p (k c) -> p k c", c=128)).then_inc(sem.h, 16)
                gp.dma_start(out=wg_bf[:, :].rearrange("p (k c) -> p k c", k=KB),
                             in_=g["wgT"].rearrange("(k p) c -> p k c", p=128)).then_inc(sem.h, 16)
                gp.dma_start(out=bg_bf[:, :], in_=g["bg"]).then_inc(sem.h, 16)
                sem.n += 18 * 16
            a2b_tok = (0, a2bw.sems[0].n)

            gb_pe = {}
            gb_read = [0, 0]
            ob_ve = {}
            ob_tok = {}
            ld_toks = {}
            for t in range(T_OWN):
                @block.sync
                def _(sy, t=t):
                    if t == 0:
                        sy.wait_ge(PE.h, PE.n)
                        sy.wait_ge(VE.h, VE.n)
                        sy.wait_ge(SE.h, SE.n)
                        for s_ in GST.sems:
                            sy.wait_ge(s_.h, s_.n)
                        for s_ in MHS.sems:
                            sy.wait_ge(s_.h, s_.n)
                    if t >= 2:
                        sy.wait_ge(VE.h, ob_ve[t - 2])
                        OST.wait(sy, ob_tok[t - 2])
                    toks = []
                    for i, (buf, src) in enumerate((
                            (xfb[t % 2], g["x_own"][t * 128:(t + 1) * 128, :]),
                            (mhb[t % 2], scr_mh[t * 128:(t + 1) * 128, :]),
                            (ggb[t % 2], scr_g[t * 128:(t + 1) * 128, :]))):
                        s_ = a2bx.sems[i * 2 + t % 2]
                        sy.dma_start(out=buf[:, :], in_=src).then_inc(s_.h, 16)
                        s_.n += 16
                        toks.append((i * 2 + t % 2, s_.n))
                    ld_toks[t] = toks

                for cq in range(4):
                    @block.tensor
                    def _(te, t=t, cq=cq):
                        if t == 0 and cq == 0:
                            te.wait_ge(a2bw.sems[0].h, a2b_tok[1])
                        te.wait_ge(SE.h, gb_read[cq % 2])
                        inst = None
                        for k in range(KB):
                            inst = te.matmul(
                                gp_ps[cq % 2][:, :],
                                xnT3[:, (k * T_OWN + t) * 128:(k * T_OWN + t) * 128 + 128],
                                wg_bf[:, k * D + cq * 512:k * D + cq * 512 + 512],
                                start=(k == 0), stop=False)
                        inst = te.matmul(gp_ps[cq % 2][:, :], ones_r[:, :],
                                         bg_bf[:, cq * 512:(cq + 1) * 512],
                                         start=False, stop=True)
                        gb_pe[(t, cq)] = PE.inc(inst)

                    @block.scalar
                    def _(s, t=t, cq=cq):
                        s.wait_ge(PE.h, gb_pe[(t, cq)])
                        if cq == 0:
                            s.wait_ge(VE.h, ob_ve.get(t - 1, 0))
                        inst = s.activation(gate_sb[:, cq * 512:(cq + 1) * 512],
                                            gp_ps[cq % 2][:, :], AF.Sigmoid)
                        gb_read[cq % 2] = SE.inc(inst)
                gate_se = SE.n

                @block.scalar
                def _(s, t=t):
                    for tok in ld_toks[t]:
                        a2bx.wait(s, tok)
                    SE.inc(s.activation(xnb[:, :], xfb[t % 2][:, :], AF.Identity,
                                        scale=rstd_own[:, t:t + 1],
                                        bias=nmr_own[:, t:t + 1]))
                xn_se2 = SE.n

                @block.vector
                def _(v, t=t):
                    v.wait_ge(SE.h, xn_se2)
                    o = mhb[t % 2]
                    ch(v, VE, v.tensor_mul(o[:, :], gate_sb[:, :], o[:, :]))
                    ch(v, VE, v.tensor_add(o[:, :], o[:, :], xnb[:, :]))
                    ch(v, VE, v.tensor_mul(o[:, :], o[:, :], ggb[t % 2][:, :]))
                    inst = v.tensor_add(o[:, :], o[:, :], xfb[t % 2][:, :])
                    ob_ve[t] = VE.inc(inst)

                @block.sync
                def _(sy, t=t):
                    sy.wait_ge(VE.h, ob_ve[t])
                    ob_tok[t] = OST.load(sy, t, out_own[t * 128:(t + 1) * 128, :],
                                         mhb[t % 2][:, :])
                if t % 4 == 3:
                    block.flush()
            block.flush()

        @block.sync
        def _(sy):
            for s_ in OST.sems:
                sy.wait_ge(s_.h, s_.n)
            sy.wait_ge(SYO.h, SYO.n)
            sy.wait_ge(MIR.h, mir_total)

        @block.gpsimd
        def _(gp):
            gp.wait_ge(GDO.h, GDO.n)
        block.flush()

    return nc, dbg



def kernel(**inputs):
    in_maps = _prepare(inputs)
    nc, dbg = _build(debug=_DEBUG)
    res = run_bass_kernel_spmd(nc, in_maps, core_ids=list(range(8)))
    out = np.empty((4, S, D), np.float32)
    for c in range(8):
        b, half = c // 2, c % 2
        out[b, half * TOK:(half + 1) * TOK] = np.asarray(res.results[c]["out_own"],
                                                         np.float32)
    if _DEBUG:
        taps = [{k: np.asarray(res.results[c][k]) for k in dbg} for c in range(8)]
        return out, taps
    return out


# version 16
# speedup vs baseline: 1.0151x; 1.0151x over previous
import math
from contextlib import ExitStack

import numpy as np
import ml_dtypes

import concourse.bass as bass
import concourse.mybir as mybir
from concourse.bass_utils import run_bass_kernel_spmd

F32 = mybir.dt.float32
BF16 = mybir.dt.bfloat16
AF = mybir.ActivationFunctionType
ALU = mybir.AluOpType
AX = mybir.AxisListType
BF = ml_dtypes.bfloat16

D = 2048
S = 4096
TOK = 2048
T_OWN = 16
T_ALL = 32
KB = 16
HKV = 4
DH = 128
FDD = 768
FS = 6
NHH = 8
RD = 64
LAT = 512
IM = 1024
NE = 8
EPS_LN = 1e-5
EPS_RMS = 1e-5
EPS_ATT = 1e-8

_DEBUG = False



def _prepare(inp):
    f32 = np.float32
    x = np.asarray(inp["L0_attn_hidden_states"], f32)

    def T(a):
        return np.ascontiguousarray(np.asarray(a, f32).T)

    wkT = T(inp["L0_attn_Wk"])
    wvT = T(inp["L0_attn_Wv"])
    wmbT = T(inp["L0_attn_Wmb"])
    wqT = T(inp["L0_attn_Wq"])
    wgT = T(inp["L0_attn_Wg"])
    row = lambda v: np.ascontiguousarray(np.asarray(v, f32).reshape(1, -1))
    bk, bv, bmb, bq, bg = (row(inp[k]) for k in
                           ("L0_attn_bk", "L0_attn_bv", "L0_attn_bmb",
                            "L0_attn_bq", "L0_attn_bg"))

    mem = np.asarray(inp["L0_attn_memory"], f32)
    nrm = np.asarray(inp["L0_attn_norm"], f32)
    mc = np.concatenate([mem, nrm[:, :, None]], axis=2)
    mc = mc.reshape(16, FS, 128, 129).transpose(2, 0, 1, 3)
    memcat = np.ascontiguousarray(mc.reshape(128, 16 * FS * 129))

    ln1 = np.ascontiguousarray(np.asarray(inp["L0_ffn_ln1_w"], f32).reshape(KB, 128).T)
    ln2 = np.ascontiguousarray(np.asarray(inp["L0_ffn_ln2_w"], f32).reshape(KB, 128).T)

    kv_d_w = np.ascontiguousarray(np.asarray(inp["L0_ffn_kv_d_w"], f32))
    q_d_w = np.ascontiguousarray(np.asarray(inp["L0_ffn_q_d_w"], f32))
    k_u_w = np.asarray(inp["L0_ffn_k_u_w"], f32)
    v_u_w = np.asarray(inp["L0_ffn_v_u_w"], f32)
    q_u_w = np.asarray(inp["L0_ffn_q_u_w"], f32)
    rope_k_w = np.asarray(inp["L0_ffn_rope_k_w"], f32)
    rope_q_w = np.asarray(inp["L0_ffn_rope_q_w"], f32)
    o_w = np.ascontiguousarray(np.asarray(inp["L0_ffn_o_w"], f32))
    m_gate = np.ascontiguousarray(np.concatenate(
        [np.asarray(inp["L0_ffn_sh_gate"], f32)[None],
         np.asarray(inp["L0_ffn_r_gate"], f32)], 0))
    m_up = np.ascontiguousarray(np.concatenate(
        [np.asarray(inp["L0_ffn_sh_up"], f32)[None],
         np.asarray(inp["L0_ffn_r_up"], f32)], 0))
    m_down = np.ascontiguousarray(np.concatenate(
        [np.asarray(inp["L0_ffn_sh_down"], f32)[None],
         np.asarray(inp["L0_ffn_r_down"], f32)], 0))
    router_w = np.ascontiguousarray(np.asarray(inp["L0_ffn_router_w"], f32))
    router_b = row(inp["L0_ffn_router_b"])

    inv_freq = 1.0 / (10000.0 ** (np.arange(0, RD, 2, dtype=np.float64) / RD))
    t_ = np.arange(S, dtype=np.float64)
    ang = np.outer(t_, inv_freq)
    cos32 = np.cos(ang).astype(f32)
    sin32 = np.sin(ang).astype(f32)
    cos_rep = np.ascontiguousarray(np.tile(cos32, (1, NHH)))
    sin_rep = np.ascontiguousarray(np.tile(sin32, (1, NHH)))

    identity = np.eye(128, dtype=f32)
    tri = np.triu(np.ones((128, 128), f32))

    shared = dict(
        wkT=wkT, wvT=wvT, wmbT=wmbT, wqT=wqT, wgT=wgT,
        bk=bk, bv=bv, bmb=bmb, bq=bq, bg=bg, memcat=memcat,
        ln1=ln1, ln2=ln2, kv_d_w=kv_d_w, q_d_w=q_d_w, o_w=o_w,
        m_gate=m_gate, m_up=m_up, m_down=m_down,
        router_w=router_w, router_b=router_b,
        cos_rep=cos_rep, sin_rep=sin_rep, identity=identity, tri=tri,
    )

    in_maps = []
    for c in range(8):
        b, half = c // 2, c % 2
        m = dict(shared)
        m["x_all"] = np.ascontiguousarray(x[b])
        m["x_own"] = np.ascontiguousarray(x[b, half * 2048:(half + 1) * 2048])
        hs = slice(half * 512, (half + 1) * 512)
        m["k_u_my"] = np.ascontiguousarray(k_u_w[:, hs])
        m["q_u_my"] = np.ascontiguousarray(q_u_w[:, hs])
        m["rope_k_my"] = np.ascontiguousarray(rope_k_w[:, hs])
        m["rope_q_my"] = np.ascontiguousarray(rope_q_w[:, hs])
        m["v_u_my"] = np.ascontiguousarray(v_u_w[:, half * 1024:(half + 1) * 1024])
        m["selA"] = np.full((128, 1), 1.0 - half, f32)
        m["selB"] = np.full((128, 1), float(half), f32)
        in_maps.append(m)
    return in_maps



class Sem:

    def __init__(self, nc, ctx, name):
        self.h = ctx.enter_context(nc.semaphore(name))
        self.n = 0

    def inc(self, inst, k=1):
        inst.then_inc(self.h, k)
        self.n += k
        return self.n


class _RecInst:
    __slots__ = ("calls",)

    def __init__(self):
        self.calls = []

    def then_inc(self, *a, **kw):
        self.calls.append(("then_inc", a, kw))
        return self


class _RecEng:

    def __init__(self):
        self.ops = []

    def __getattr__(self, name):
        def call(*a, **kw):
            ri = _RecInst()
            self.ops.append((name, a, kw, ri))
            return ri
        return call


_ACT_SETS = None


def _act_set_ids(func):
    global _ACT_SETS
    if _ACT_SETS is None:
        from concourse.hw_specs import get_activation_tables
        _ACT_SETS = list(get_activation_tables("gen3").values())
    return frozenset(i for i, fs in enumerate(_ACT_SETS) if func in fs)


class DeferredBlock:

    _ENGS = ("sync", "gpsimd", "tensor", "scalar", "vector")

    def __init__(self, real, auto=()):
        self._real = real
        self._auto = auto
        self._pend = {n: [] for n in self._ENGS}

    def __getattr__(self, name):
        if name in DeferredBlock._ENGS:
            def dec(fn):
                rec = _RecEng()
                fn(rec)
                self._pend[name].extend(rec.ops)
                if name in self._auto:
                    self.flush_one(name)
                return None
            return dec
        raise AttributeError(name)

    SPACE_EVERY = 0
    SCALAR_CAP = 0

    def _emit(self, name, ops):
        k = DeferredBlock.SPACE_EVERY

        def section(eng, ops=tuple(ops)):
            cnt = 0
            for (m, a, kw, ri) in ops:
                inst = getattr(eng, m)(*a, **kw)
                for (_, ia, ikw) in ri.calls:
                    inst.then_inc(*ia, **ikw)
                cnt += 1
                if k and cnt % k == 0 and name in ("tensor", "scalar", "vector"):
                    eng.drain()
        getattr(self._real, name)(section)

    def flush_one(self, which):
        ops = self._pend[which]
        if not ops:
            return
        self._pend[which] = []
        if which != "scalar":
            self._emit(which, ops)
            return
        cap = DeferredBlock.SCALAR_CAP
        if cap:
            for i in range(0, len(ops), cap):
                self._emit("scalar", ops[i:i + cap])
            return
        chunk, allowed = [], None
        for op in ops:
            (m, a, kw, ri) = op
            if m == "activation":
                func = a[2] if len(a) > 2 else kw.get("func")
                ids = _act_set_ids(func)
                if allowed is None:
                    allowed = ids
                else:
                    inter = allowed & ids
                    if not inter:
                        self._emit("scalar", chunk)
                        chunk, allowed = [], ids
                    else:
                        allowed = inter
            chunk.append(op)
        if chunk:
            self._emit("scalar", chunk)

    FLUSH_ORDER = ("sync", "gpsimd", "tensor", "scalar", "vector")

    def flush(self):
        for name in DeferredBlock.FLUSH_ORDER:
            self.flush_one(name)


class Slots:

    def __init__(self, nc, ctx, name, n):
        self.sems = [Sem(nc, ctx, f"{name}{i}") for i in range(n)]
        self.nslots = n

    def load(self, eng, i, out, in_, **kw):
        s = self.sems[i % self.nslots]
        eng.dma_start(out=out, in_=in_, **kw).then_inc(s.h, 16)
        s.n += 16
        return (i % self.nslots, s.n)

    def wait(self, eng, tok):
        i, v = tok
        eng.wait_ge(self.sems[i].h, v)




def _finish_early(nc, dbg, block, out_own, g, OST, SYO, MIR, GDO, mir_total, scopes):
    T_OWN_ = T_OWN

    @block.sync
    def _(sy):
        for t in range(T_OWN_):
            s_ = OST.sems[t % 2]
            if t >= 2:
                sy.wait_ge(s_.h, s_.n)
            sy.dma_start(out=out_own[t * 128:(t + 1) * 128, :],
                         in_=g["x_own"][t * 128:(t + 1) * 128, :]).then_inc(s_.h, 16)
            s_.n += 16
        for s_ in OST.sems:
            sy.wait_ge(s_.h, s_.n)
        sy.wait_ge(SYO.h, SYO.n)
        sy.wait_ge(MIR.h, mir_total)

    @block.gpsimd
    def _(gp):
        gp.wait_ge(GDO.h, GDO.n)
    block.flush()
    for sc in reversed(scopes):
        sc.close()
    return nc, dbg



def _build(debug=False, race_detect=False, nranks=8, phases=9,
           PA_FLUSH_A=False, PA_FLUSH_P=False, PA_SPLIT=(), PA_SKIP=(),
           PA_VAR=(), AUTOFLUSH=False):
    nc = bass.Bass("TRN2", target_bir_lowering=False,
                   detect_race_conditions=race_detect)
    g = {}

    def din(name, shape, dt=F32):
        g[name] = nc.dram_tensor(name, shape, dt, kind="ExternalInput").ap()
        return g[name]

    din("x_all", [S, D]); din("x_own", [TOK, D])
    din("wkT", [D, 512]); din("wvT", [D, 512]); din("wmbT", [D, D])
    din("wqT", [D, D]); din("wgT", [D, D])
    din("bk", [1, 512]); din("bv", [1, 512]); din("bmb", [1, D])
    din("bq", [1, D]); din("bg", [1, D])
    din("memcat", [128, 16 * FS * 129])
    din("ln1", [128, KB]); din("ln2", [128, KB])
    din("kv_d_w", [D, LAT]); din("q_d_w", [D, LAT])
    din("k_u_my", [LAT, 512]); din("q_u_my", [LAT, 512]); din("v_u_my", [LAT, 1024])
    din("rope_k_my", [D, 512]); din("rope_q_my", [LAT, 512])
    din("o_w", [D, D])
    din("m_gate", [NE, D, IM]); din("m_up", [NE, D, IM]); din("m_down", [NE, IM, D])
    din("router_w", [D, 7]); din("router_b", [1, 7])
    din("cos_rep", [S, 256]); din("sin_rep", [S, 256])
    din("identity", [128, 128]); din("tri", [128, 128])
    din("selA", [128, 1]); din("selB", [128, 1])

    out_own = nc.dram_tensor("out_own", [TOK, D], F32, kind="ExternalOutput").ap()

    scr_xnT = nc.dram_tensor("scr_xnT", [T_OWN * 128, KB * 128], BF16).ap()
    scr_xnT_all = nc.dram_tensor("scr_xnT_all", [2 * T_OWN * 128, KB * 128], BF16).ap()
    scr_r1 = nc.dram_tensor("scr_r1", [128, T_OWN], F32).ap()
    wq_mir = nc.dram_tensor("wq_mir", [D, D], BF16).ap()
    scr_r1_all = nc.dram_tensor("scr_r1_all", [2 * 128, T_OWN], F32).ap()
    scr_kT = nc.dram_tensor("scr_kT", [512, S], BF16).ap()
    scr_qT = nc.dram_tensor("scr_qT", [512, S], BF16).ap()
    ar_in_t = nc.dram_tensor("ar_in", [128, 16 * FS * 129], F32)
    ar_out_t = nc.dram_tensor("ar_out", [128, 16 * FS * 129], F32,
                              addr_space="Shared")
    ag_in_t = nc.dram_tensor("ag_in", [NHH * 128, S], BF16)
    ag_out_t = nc.dram_tensor("ag_out", [2 * NHH * 128, S], BF16)
    scr_mh = nc.dram_tensor("scr_mh", [TOK, D], F32).ap()
    scr_h = nc.dram_tensor("scr_h", [TOK, D], F32).ap()
    scr_g = nc.dram_tensor("scr_g", [TOK, D], F32).ap()
    scr_heT = nc.dram_tensor("scr_heT", [NE, IM, TOK], BF16).ap()
    scr_moe1 = nc.dram_tensor("scr_moe1", [TOK, D], F32).ap()

    dbg = {}

    def dout(name, shape, dt=F32):
        dbg[name] = nc.dram_tensor(name, shape, dt, kind="ExternalOutput").ap()
        return dbg[name]

    with ExitStack() as top:
        ent = top.enter_context
        PE = Sem(nc, top, "PE")
        VE = Sem(nc, top, "VE")
        SE = Sem(nc, top, "SE")
        GP = Sem(nc, top, "GP")
        CC = Sem(nc, top, "CC")
        PRE = Sem(nc, top, "PRE")
        MIR = Sem(nc, top, "MIR")
        SYO = Sem(nc, top, "SYO")
        GDO = Sem(nc, top, "GDO")
        XSP = Sem(nc, top, "XSP")
        OST = Slots(nc, top, "ost", 2)
        xls = Slots(nc, top, "p0x", 2)
        wgrp = Slots(nc, top, "pawg", 2)
        pgrp = Slots(nc, top, "papg", 2)
        DEL = Slots(nc, top, "del", 2)

        id_bf = ent(nc.sbuf_tensor("id_bf", [128, 128], BF16))
        tri_bf = ent(nc.sbuf_tensor("tri_bf", [128, 128], BF16))
        ones_r = ent(nc.sbuf_tensor("ones_r", [1, 128], BF16))
        ln1_sb = ent(nc.sbuf_tensor("ln1_sb", [128, KB], F32))
        ln2_sb = ent(nc.sbuf_tensor("ln2_sb", [128, KB], F32))
        sel_sb = ent(nc.sbuf_tensor("sel_sb", [128, 2], F32))
        rstd_own = ent(nc.sbuf_tensor("rstd_own", [128, T_OWN], F32))
        nmr_own = ent(nc.sbuf_tensor("nmr_own", [128, T_OWN], F32))
        rstd1_own = ent(nc.sbuf_tensor("rstd1_own", [128, T_OWN], F32))
        rstd2_own = ent(nc.sbuf_tensor("rstd2_own", [128, T_OWN], F32))
        consts = ent(nc.sbuf_tensor("consts", [128, 2], F32))
        EPSC_LN = consts[:, 0:1]
        EPSC_ATT = consts[:, 1:2]

        block_real = ent(nc.Block())
        block = DeferredBlock(block_real, auto=AUTOFLUSH if AUTOFLUSH else ("sync", "gpsimd", "scalar"))

        @block.gpsimd
        def _(gp):
            gp.dma_start(out=id_bf[:, :], in_=g["identity"]).then_inc(PRE.h, 16)
            gp.dma_start(out=tri_bf[:, :], in_=g["tri"]).then_inc(PRE.h, 16)
            gp.dma_start(out=ln1_sb[:, :], in_=g["ln1"]).then_inc(PRE.h, 16)
            gp.dma_start(out=ln2_sb[:, :], in_=g["ln2"]).then_inc(PRE.h, 16)
            gp.dma_start(out=sel_sb[:, 0:1], in_=g["selA"]).then_inc(PRE.h, 16)
            gp.dma_start(out=sel_sb[:, 1:2], in_=g["selB"]).then_inc(PRE.h, 16)
            PRE.n += 6 * 16
            gp.memset(ones_r[:, :], 1.0)
            gp.memset(consts[:, 0:1], EPS_LN)
            GP.inc(gp.memset(consts[:, 1:2], EPS_ATT))
        pre_total = PRE.n
        consts_mark = GP.n

        mir_total = MIR.n
        block.flush()

        scopes = []
        xnT_scope = ExitStack()
        scopes.append(xnT_scope)
        xnT = xnT_scope.enter_context(
            nc.sbuf_tensor("xnT", [128, KB * T_OWN * 128], BF16))

        def ch(eng, sem, inst):
            sem.inc(inst)
            eng.wait_ge(sem.h, sem.n)

        def emit_rsqrt_newton(v, s, out_ap, vpe_ap, scr, mode="lnexp"):
            if mode == "fake":
                @block.vector
                def _(v_):
                    v_.wait_ge(SE.h, SE.n)
                    VE.inc(v_.reciprocal(out_ap, vpe_ap))
                return

            @block.scalar
            def _(s_):
                s_.wait_ge(VE.h, VE.n)
                s_.wait_ge(SE.h, SE.n)
                ch(s_, SE, s_.activation(scr, vpe_ap, AF.Ln))
                SE.inc(s_.activation(out_ap, scr, AF.Exp, scale=-0.5))

            @block.vector
            def _(v_):
                v_.wait_ge(SE.h, SE.n)
                ch(v_, VE, v_.tensor_mul(scr, out_ap, out_ap))
                ch(v_, VE, v_.tensor_mul(scr, scr, vpe_ap))
                ch(v_, VE, v_.tensor_scalar(scr, scr, -0.5, 1.5, ALU.mult, ALU.add))
                VE.inc(v_.tensor_mul(out_ap, out_ap, scr))

        def emit_ln_stats_to(dst_rstd, dst_nmr, dst_rstd1, xf_ap, st1, sq_scr,
                             first=False):
            @block.vector
            def _(v):
                VE.inc(v.reduce_sum(st1[:, 0:1], xf_ap, axis=AX.X))
            red_mark = VE.n

            @block.scalar
            def _(s):
                if first:
                    s.wait_ge(GP.h, consts_mark)
                s.activation(sq_scr, xf_ap, AF.Square,
                             accum_out=st1[:, 1:2])
                SE.inc(s.drain())
            sq_mark = SE.n

            @block.vector
            def _(v):
                v.wait_ge(SE.h, SE.n)
                v.wait_ge(VE.h, red_mark)
                ch(v, VE, v.tensor_scalar_mul(st1[:, 2:3], st1[:, 0:1], 1.0 / D))
                ch(v, VE, v.tensor_mul(st1[:, 3:4], st1[:, 2:3], st1[:, 2:3]))
                ch(v, VE, v.scalar_tensor_tensor(st1[:, 4:5], st1[:, 1:2], 1.0 / D,
                                                 st1[:, 3:4], ALU.mult, ALU.subtract))
                VE.inc(v.tensor_scalar_add(st1[:, 5:6], st1[:, 4:5], EPS_LN))

            emit_rsqrt_newton(None, None, dst_rstd, st1[:, 5:6], st1[:, 6:7])

            @block.vector
            def _(v):
                r = dst_rstd
                v.wait_ge(VE.h, VE.n)
                ch(v, VE, v.tensor_scalar(dst_nmr, st1[:, 2:3], r,
                                          -1.0, ALU.mult, ALU.mult))
                ch(v, VE, v.tensor_mul(st1[:, 5:6], st1[:, 4:5], r))
                ch(v, VE, v.tensor_mul(st1[:, 5:6], st1[:, 5:6], r))
                VE.inc(v.tensor_scalar_add(st1[:, 5:6], st1[:, 5:6], EPS_RMS))

            emit_rsqrt_newton(None, None, dst_rstd1, st1[:, 5:6], st1[:, 6:7])
            return sq_mark

        def emit_ln_stats(t, xf_ap, st1, sq_scr, first=False):
            return emit_ln_stats_to(rstd_own[:, t:t + 1], nmr_own[:, t:t + 1],
                                    rstd1_own[:, t:t + 1], xf_ap, st1, sq_scr,
                                    first=first)

        with ExitStack() as ph:
            e0 = ph.enter_context
            xf = [e0(nc.sbuf_tensor(f"p0_xf{i}", [128, D], F32)) for i in range(2)]
            xnbf = [e0(nc.sbuf_tensor(f"p0_xn{i}", [128, D], BF16)) for i in range(2)]
            sq_scr = e0(nc.sbuf_tensor("p0_sq", [128, D], F32))
            st1 = e0(nc.sbuf_tensor("p0_st", [128, 8], F32))
            pt_bf = [e0(nc.psum_tensor(f"p0_pt{i}", [128, KB // 2 * 128], BF16))
                     for i in range(2)]

            ld_tok = {}
            xn_marks = {}
            pe_g = {}
            cp_g = {}

            for t in range(T_OWN):
                @block.sync
                def _(sy, t=t):
                    if t >= 2:
                        sy.wait_ge(SE.h, xn_marks[t - 2])
                    ld_tok[t] = xls.load(sy, t, xf[t % 2][:, :],
                                         g["x_own"][t * 128:(t + 1) * 128, :])

                @block.vector
                def _(v, t=t):
                    xls.wait(v, ld_tok[t])

                @block.scalar
                def _(s, t=t):
                    xls.wait(s, ld_tok[t])

                emit_ln_stats(t, xf[t % 2][:, :], st1, sq_scr[:, :], first=(t == 0))

                @block.scalar
                def _(s, t=t):
                    s.wait_ge(VE.h, VE.n)
                    if t >= 2:
                        s.wait_ge(PE.h, pe_g[(t - 2, 1)])
                    inst = s.activation(xnbf[t % 2][:, :], xf[t % 2][:, :],
                                        AF.Identity, scale=rstd_own[:, t:t + 1],
                                        bias=nmr_own[:, t:t + 1])
                    xn_marks[t] = SE.inc(inst)

                for hh in range(2):
                    @block.tensor
                    def _(te, t=t, hh=hh):
                        te.wait_ge(SE.h, xn_marks[t])
                        gi = t * 2 + hh
                        if gi == 0:
                            te.wait_ge(PRE.h, pre_total)
                        if gi >= 2:
                            te.wait_ge(VE.h, cp_g[gi - 2])
                        inst = None
                        for j in range(KB // 2):
                            k = hh * (KB // 2) + j
                            inst = te.transpose(
                                pt_bf[gi % 2][:, j * 128:(j + 1) * 128],
                                xnbf[t % 2][:, k * 128:(k + 1) * 128], id_bf[:, :])
                        pe_g[(t, hh)] = PE.inc(inst)

                    @block.vector
                    def _(v, t=t, hh=hh):
                        gi = t * 2 + hh
                        v.wait_ge(PE.h, pe_g[(t, hh)])
                        dst = xnT[:, :].rearrange("p (k t c) -> p k t c",
                                                  k=KB, t=T_OWN)[
                            :, hh * (KB // 2):(hh + 1) * (KB // 2), t, :]
                        cp_g[gi] = VE.inc(v.tensor_copy(dst, pt_bf[gi % 2][:, :]))
                if t % 4 == 3 and hh == 1:
                    block.flush()

            p0_done_ve = VE.n

            @block.sync
            def _(sy):
                sy.wait_ge(VE.h, p0_done_ve)
                xv = xnT[:, :].rearrange("p (k t c) -> p k t c", t=T_OWN, c=128)
                for t_ in range(T_OWN):
                    sy.dma_start(out=scr_xnT[t_ * 128:(t_ + 1) * 128, :].rearrange(
                        "p (k c) -> p k c", c=128),
                        in_=xv[:, :, t_, :]).then_inc(XSP.h, 16)
                sy.dma_start(out=scr_r1, in_=rstd1_own[:, :]).then_inc(XSP.h, 16)
                XSP.n += 17 * 16

            @block.gpsimd
            def _(gp):
                gp.wait_ge(XSP.h, XSP.n)
                if nranks == 8:
                    gp.collective_compute(
                        "AllGather", ALU.bypass,
                        replica_groups=[[0, 1], [2, 3], [4, 5], [6, 7]],
                        ins=[scr_xnT.opt()], outs=[scr_xnT_all.opt()],
                    ).then_inc(CC.h)
                    CC.n += 1
                    gp.collective_compute(
                        "AllGather", ALU.bypass,
                        replica_groups=[[0, 1], [2, 3], [4, 5], [6, 7]],
                        ins=[scr_r1.opt()], outs=[scr_r1_all.opt()],
                    ).then_inc(CC.h)
                    CC.n += 1
                else:
                    gp.dma_start(out=scr_xnT_all[0:T_OWN * 128, :], in_=scr_xnT).then_inc(GDO.h, 16)
                    gp.dma_start(out=scr_xnT_all[T_OWN * 128:, :], in_=scr_xnT).then_inc(GDO.h, 16)
                    gp.dma_start(out=scr_r1_all[0:128, :], in_=scr_r1).then_inc(GDO.h, 16)
                    gp.dma_start(out=scr_r1_all[128:256, :], in_=scr_r1).then_inc(GDO.h, 16)
                    GDO.n += 64
                    gp.wait_ge(GDO.h, GDO.n)
                    gp.wait_ge(CC.h, CC.n)
                    CC.inc(gp.memset(consts[:, 0:1], EPS_LN))
                    gp.wait_ge(CC.h, CC.n)
                    CC.inc(gp.memset(consts[:, 0:1], EPS_LN))
            xn_ag_cc = CC.n
            block.flush()

        if debug:
            dout("dbg_xnT", [128, KB * T_OWN * 128])
            dout("dbg_rstd", [128, T_OWN])
            dout("dbg_rstd1", [128, T_OWN])

            @block.gpsimd
            def _(gp):
                gp.wait_ge(VE.h, p0_done_ve)
                gp.dma_start(out=dbg["dbg_xnT"], in_=xnT[:, :]).then_inc(GDO.h, 16)
                gp.dma_start(out=dbg["dbg_rstd"], in_=rstd_own[:, :]).then_inc(GDO.h, 16)
                gp.dma_start(out=dbg["dbg_rstd1"], in_=rstd1_own[:, :]).then_inc(GDO.h, 16)
                GDO.n += 48

        if phases < 2:
            return _finish_early(nc, dbg, block, out_own, g, OST, SYO, MIR, GDO, mir_total, scopes)
        with ExitStack() as ph:
            e0 = ph.enter_context
            wk_g = [e0(nc.sbuf_tensor(f"pa_wk{i}", [128, KB * 128], BF16)) for i in range(2)]
            wv_g = [e0(nc.sbuf_tensor(f"pa_wv{i}", [128, KB * 128], BF16)) for i in range(2)]
            wmb_g = [e0(nc.sbuf_tensor(f"pa_wmb{i}", [128, KB * 512], BF16)) for i in range(2)]
            bk_g = [e0(nc.sbuf_tensor(f"pa_bk{i}", [1, 128], BF16)) for i in range(2)]
            bv_g = [e0(nc.sbuf_tensor(f"pa_bv{i}", [1, 128], BF16)) for i in range(2)]
            bmb_g = [e0(nc.sbuf_tensor(f"pa_bmb{i}", [1, 512], BF16)) for i in range(2)]
            x2p_k = e0(nc.sbuf_tensor("pa_x2p", [128, 2048], BF16))
            x2n_k = e0(nc.sbuf_tensor("pa_x2n", [128, 2048], BF16))
            mk2 = e0(nc.sbuf_tensor("pa_mk2", [128, FS * 2048], BF16))
            mkT_sb = e0(nc.sbuf_tensor("pa_mkT", [128, FS * T_OWN * 128], BF16))
            v_g = e0(nc.sbuf_tensor("pa_v", [128, T_OWN * 128], BF16))
            mb4_sb = e0(nc.sbuf_tensor("pa_mb4", [128, T_OWN * 512], BF16))
            mc_h = [e0(nc.sbuf_tensor(f"pa_mc{i}", [128, FS * 129], BF16)) for i in range(2)]
            mbe_sb = e0(nc.sbuf_tensor("pa_mbe", [128, 512], BF16))
            sqa_k = e0(nc.sbuf_tensor("pa_sqa", [128, 512], F32))
            tmp1 = sqa_k
            sqt_k = e0(nc.sbuf_tensor("pa_sqt", [128, 512], BF16))
            nstp = e0(nc.sbuf_tensor("pa_nstp", [128, 16], F32))
            rstd_p = e0(nc.sbuf_tensor("pa_rstd", [128, 16], F32))
            nrstd_p = e0(nc.sbuf_tensor("pa_nrstd", [128, 16], F32))
            epss_p = e0(nc.sbuf_tensor("pa_epss", [128, 16], F32))
            nsc_p = e0(nc.sbuf_tensor("pa_nsc", [128, 16], F32))
            den_b = e0(nc.sbuf_tensor("pa_denb", [128, 16], F32))
            den_st = e0(nc.sbuf_tensor("pa_denst", [128, 16], F32))
            rden_b = e0(nc.sbuf_tensor("pa_rdenb", [128, 16], F32))
            cf_b = e0(nc.sbuf_tensor("pa_cfb", [128, 16], F32))
            wmv_bf = [e0(nc.sbuf_tensor(f"pa_wmv{i}", [128, 132], BF16)) for i in range(2)]
            da_st = [e0(nc.sbuf_tensor("pa_da0", [128, FS * 129], F32))] * 2
            numstage = e0(nc.sbuf_tensor("pa_nums", [128, 16 * 129], BF16))
            pf = [e0(nc.psum_tensor(f"pa_pf{i}", [128, 512], F32)) for i in range(7)]
            pb = e0(nc.psum_tensor("pa_pb", [128, 1024], BF16))

            wld = {}
            hld = {}
            del_tok = {}
            stageA_pe_done = {}
            head_pe_done = {}
            krelu_se = {}
            kexp_se = {}
            vcp_ve = {}
            proj_pe = {}
            dp_half_ve = {}
            tpk_pe = {}
            cpk_mk = {}
            num_pe = {}
            ncp = {}
            vn = {}
            wmv_pe = {}
            da_read = 0
            num_slot_vn = [0, 0]
            sigp = {"v": 0}
            mb4_ve = {"v": 0}
            mk2_free = {"pe": 0}

            for gi in range(4):
                sl = gi % 2
                if gi == 1:
                    @block.gpsimd
                    def _(gp):
                        gp.dma_start(out=wq_mir, in_=g["wqT"]).then_inc(MIR.h, 16)
                        MIR.n += 16

                @block.gpsimd
                def _(gp, gi=gi, sl=sl):
                    if gi == 0:
                        gp.wait_ge(VE.h, p0_done_ve)
                    if gi >= 2:
                        gp.wait_ge(PE.h, stageA_pe_done[gi - 2])
                    sem = wgrp.sems[sl]
                    for out_sb, src, w in ((wk_g[sl], g["wkT"], 128),
                                           (wv_g[sl], g["wvT"], 128),
                                           (wmb_g[sl], g["wmbT"], 512)):
                        src_ap = src.rearrange("(k p) c -> p k c", p=128)[
                            :, :, gi * w:(gi + 1) * w]
                        gp.dma_start(out=out_sb[:, :].rearrange(
                            "p (k c) -> p k c", k=KB), in_=src_ap).then_inc(sem.h, 16)
                        sem.n += 16
                    for out_sb, src, w in ((bk_g[sl], g["bk"], 128),
                                           (bv_g[sl], g["bv"], 128),
                                           (bmb_g[sl], g["bmb"], 512)):
                        gp.dma_start(out=out_sb[:, :],
                                     in_=src[:, gi * w:(gi + 1) * w]).then_inc(sem.h, 16)
                        sem.n += 16
                    wld[gi] = (sl, sem.n)

                def pa_dpfp_half(gi, half):
                    @block.vector
                    def _(v, gi=gi, half=half):
                        v.wait_ge(SE.h, krelu_se[(gi, half * 8 + 7)])
                        if gi >= 1 and half == 0:
                            v.wait_ge(PE.h, mk2_free["pe"])
                        c0 = half * 1024
                        inst = None
                        for j in (1, 2, 3):
                            bp = (j - 1) * 2 * 2048
                            bn = bp + 2048
                            s0 = c0 + j if half == 0 else c0
                            inst = v.tensor_mul(mk2[:, bp + s0:bp + c0 + 1024],
                                                x2p_k[:, s0:c0 + 1024],
                                                x2p_k[:, s0 - j:c0 + 1024 - j])
                            inst = v.tensor_mul(mk2[:, bn + s0:bn + c0 + 1024],
                                                x2n_k[:, s0:c0 + 1024],
                                                x2n_k[:, s0 - j:c0 + 1024 - j])
                        ch(v, VE, inst)
                        pv = x2p_k[:, :].rearrange("p (t c) -> p t c", c=128)
                        nv = x2n_k[:, :].rearrange("p (t c) -> p t c", c=128)
                        t0, t1 = half * 8, half * 8 + 8
                        for j in (1, 2, 3):
                            bp = (j - 1) * 2 * 2048
                            bn = bp + 2048
                            mpv = mk2[:, bp:bp + 2048].rearrange(
                                "p (t c) -> p t c", c=128)
                            mnv = mk2[:, bn:bn + 2048].rearrange(
                                "p (t c) -> p t c", c=128)
                            inst = v.tensor_mul(mpv[:, t0:t1, 0:j],
                                                pv[:, t0:t1, 0:j],
                                                nv[:, t0:t1, 128 - j:128])
                            inst = v.tensor_mul(mnv[:, t0:t1, 0:j],
                                                nv[:, t0:t1, 0:j],
                                                pv[:, t0:t1, 128 - j:128])
                        dp_half_ve[(gi, half)] = VE.inc(inst)

                def pa_tp(gi, t):
                    @block.tensor
                    def _(te, gi=gi, t=t):
                        te.wait_ge(VE.h, dp_half_ve[(gi, t // 8)])
                        if t >= 1:
                            e_, m_ = cpk_mk[(gi, t - 1)]
                            te.wait_ge(e_.h, m_)
                        elif gi >= 1:
                            e_, m_ = cpk_mk[(gi - 1, 15)]
                            te.wait_ge(e_.h, m_)
                        inst = None
                        for fs in range(FS):
                            inst = te.transpose(
                                pb[:, fs * 128:(fs + 1) * 128],
                                mk2[:, fs * 2048 + t * 128:fs * 2048 + (t + 1) * 128],
                                id_bf[:, :])
                        tpk_pe[(gi, t)] = PE.inc(inst)

                def pa_cp(gi, t):
                    dst = mkT_sb[:, :].rearrange("p (f t c) -> p f t c",
                                                 f=FS, t=T_OWN)[:, :, t, :]
                    if t % 2 == 0:
                        @block.vector
                        def _(v, gi=gi, t=t, dst=dst):
                            v.wait_ge(PE.h, tpk_pe[(gi, t)])
                            cpk_mk[(gi, t)] = (VE, VE.inc(
                                v.tensor_copy(dst, pb[:, 0:FDD])))
                    else:
                        @block.scalar
                        def _(s, gi=gi, t=t, dst=dst):
                            s.wait_ge(PE.h, tpk_pe[(gi, t)])
                            cpk_mk[(gi, t)] = (SE, SE.inc(
                                s.activation(dst, pb[:, 0:FDD], AF.Identity)))

                for t in range(T_OWN):
                    @block.tensor
                    def _(te, gi=gi, sl=sl, t=t):
                        if t == 0:
                            te.wait_ge(wgrp.sems[sl].h, wld[gi][1])
                            if gi == 0:
                                te.wait_ge(VE.h, p0_done_ve)
                            else:
                                te.wait_ge(VE.h, da_read)
                        if (gi, t - 2) in krelu_se:
                            te.wait_ge(SE.h, kexp_se[(gi, t - 2)])
                            te.wait_ge(VE.h, vcp_ve[(gi, t - 2)])
                        elif gi >= 1 and t <= 1:
                            te.wait_ge(SE.h, kexp_se[(gi - 1, 14 + t)])
                            te.wait_ge(VE.h, vcp_ve[(gi - 1, 14 + t)])
                        kvb = pf[t % 2]
                        mbb = pf[2 + t % 2]
                        for cc0, wsb, bsb in ((0, wk_g[sl], bk_g[sl]),
                                              (128, wv_g[sl], bv_g[sl])):
                            for k in range(KB):
                                te.matmul(kvb[:, cc0:cc0 + 128],
                                          xnT[:, (k * T_OWN + t) * 128:(k * T_OWN + t) * 128 + 128],
                                          wsb[:, k * 128:(k + 1) * 128],
                                          start=(k == 0), stop=False)
                            te.matmul(kvb[:, cc0:cc0 + 128], ones_r[:, :],
                                      bsb[:, :], start=False, stop=True)
                        for k in range(KB):
                            te.matmul(mbb[:, 0:512],
                                      xnT[:, (k * T_OWN + t) * 128:(k * T_OWN + t) * 128 + 128],
                                      wmb_g[sl][:, k * 512:(k + 1) * 512],
                                      start=(k == 0), stop=False)
                        inst = te.matmul(mbb[:, 0:512], ones_r[:, :],
                                         bmb_g[sl][:, :], start=False, stop=True)
                        proj_pe[(gi, t)] = PE.inc(inst)

                    if t % 2 == 1:
                        @block.scalar
                        def _(s, gi=gi, t=t):
                            for tt in (t - 1, t):
                                s.wait_ge(PE.h, proj_pe[(gi, tt)])
                                if tt == 0 and gi >= 1:
                                    s.wait_ge(VE.h, dp_half_ve[(gi - 1, 1)])
                                if tt >= 1 or gi >= 1:
                                    s.wait_ge(VE.h, sigp["v"])
                                kvb = pf[tt % 2]
                                mbb = pf[2 + tt % 2]
                                s.activation(x2p_k[:, tt * 128:(tt + 1) * 128],
                                             kvb[:, 0:128], AF.Relu)
                                inst = s.activation(x2n_k[:, tt * 128:(tt + 1) * 128],
                                                    kvb[:, 0:128], AF.Relu, scale=-1.0)
                                krelu_se[(gi, tt)] = SE.inc(inst)
                                inst = s.activation(mbe_sb[:, :], mbb[:, 0:512],
                                                    AF.Exp, scale=-1.0)
                                kexp_se[(gi, tt)] = SE.inc(inst)

                                @block.vector
                                def _(v, gi=gi, tt=tt):
                                    v.wait_ge(PE.h, proj_pe[(gi, tt)])
                                    VE.inc(v.tensor_copy(
                                        v_g[:, tt * 128:(tt + 1) * 128],
                                        pf[tt % 2][:, 128:256]))
                                    vcp_ve[(gi, tt)] = VE.n
                                    v.wait_ge(SE.h, kexp_se[(gi, tt)])
                                    ch(v, VE, v.tensor_scalar_add(
                                        mbe_sb[:, :], mbe_sb[:, :], 1.0))
                                    ch(v, VE, v.reciprocal(sqa_k[:, :],
                                                           mbe_sb[:, :]))
                                    inst = v.tensor_copy(
                                        mb4_sb[:, tt * 512:(tt + 1) * 512],
                                        sqa_k[:, :])
                                    sigp["v"] = VE.inc(inst)

                    if t == 7:
                        pa_dpfp_half(gi, 0)
                    if t >= 8:
                        pa_tp(gi, t - 8)
                        pa_cp(gi, t - 8)
                block.flush()
                pa_dpfp_half(gi, 1)
                for t in range(8, T_OWN):
                    pa_tp(gi, t)
                    pa_cp(gi, t)

                @block.vector
                def _(v, gi=gi):
                    v.wait_ge(VE.h, dp_half_ve[(gi, 1)])
                    for qf in range(4):
                        c0 = qf * 512
                        sl0 = mk2[:, c0:c0 + 512]
                        ch(v, VE, v.tensor_mul(sqt_k[:, :], sl0, sl0))
                        ch(v, VE, v.tensor_copy(sqa_k[:, :], sqt_k[:, :]))
                        for b in range(1, 6):
                            slb = mk2[:, b * 2048 + c0:b * 2048 + c0 + 512]
                            ch(v, VE, v.tensor_mul(sqt_k[:, :], slb, slb))
                            ch(v, VE, v.tensor_add(sqa_k[:, :], sqa_k[:, :],
                                                   sqt_k[:, :]))
                        ch(v, VE, v.reduce_sum(
                            nstp[:, qf * 4:(qf + 1) * 4],
                            sqa_k[:, :].rearrange("p (t c) -> p t c", c=128),
                            axis=AX.X))

                emit_rsqrt_newton(None, None, rstd_p[:, :], nstp[:, :],
                                  nsc_p[:, :])

                @block.vector
                def _(v, gi=gi):
                    v.wait_ge(VE.h, VE.n)
                    ch(v, VE, v.tensor_scalar_mul(nrstd_p[:, :], rstd_p[:, :],
                                                  -1.0))
                    ch(v, VE, v.tensor_mul(epss_p[:, :], nstp[:, :],
                                           rstd_p[:, :]))
                    ch(v, VE, v.tensor_scalar_mul(epss_p[:, :], epss_p[:, :],
                                                  EPS_ATT))
                    inst = None
                    for t in range(T_OWN):
                        inst = v.tensor_scalar_mul(
                            mb4_sb[:, t * 512:(t + 1) * 512],
                            mb4_sb[:, t * 512:(t + 1) * 512],
                            nrstd_p[:, t:t + 1])
                    mb4_ve["v"] = VE.inc(inst)
                block.flush()
                stageA_pe_done[gi] = PE.n

                for hi in range(4):
                    h = gi * 4 + hi

                    @block.gpsimd
                    def _(gp, h=h):
                        if h >= 2:
                            gp.wait_ge(PE.h, head_pe_done[h - 2])
                            gp.wait_ge(VE.h, da_read)
                        slh = h % 2
                        sem = pgrp.sems[slh]
                        gp.dma_start(out=mc_h[slh][:, :],
                                     in_=g["memcat"][:, h * FS * 129:(h + 1) * FS * 129]).then_inc(sem.h, 16)
                        sem.n += 16
                        hld[h] = (slh, sem.n)

                    for t2 in range(T_OWN // 2):
                        tA, tB = 2 * t2, 2 * t2 + 1

                        @block.tensor
                        def _(te, gi=gi, h=h, hi=hi, tA=tA, tB=tB):
                            slh = h % 2
                            if tA == 0:
                                te.wait_ge(pgrp.sems[slh].h, hld[h][1])
                            if hi == 0:
                                e_, m_ = cpk_mk[(gi, tB)]
                                te.wait_ge(e_.h, m_)
                                e_, m_ = cpk_mk[(gi, tA)]
                                te.wait_ge(e_.h, m_)
                            te.wait_ge(SE.h, num_slot_vn[0])
                            for t in (tA, tB):
                                ns = 256 * (t % 2)
                                inst = None
                                for fs in range(FS):
                                    inst = te.matmul(
                                        pf[6][:, ns:ns + 129],
                                        mkT_sb[:, (fs * T_OWN + t) * 128:(fs * T_OWN + t) * 128 + 128],
                                        mc_h[slh][:, fs * 129:(fs + 1) * 129],
                                        start=(fs == 0), stop=(fs == FS - 1))
                                num_pe[(h, t)] = PE.inc(inst)

                        @block.scalar
                        def _(s, h=h, tA=tA, tB=tB):
                            s.wait_ge(PE.h, num_pe[(h, tB)])
                            for t in (tA, tB):
                                ns = 256 * (t % 2)
                                s.activation(den_st[:, t:t + 1],
                                             pf[6][:, ns + 128:ns + 129],
                                             AF.Identity)
                                inst = s.activation(
                                    numstage[:, t * 129:(t + 1) * 129],
                                    pf[6][:, ns:ns + 129], AF.Identity)
                                ncp[(h, t)] = SE.inc(inst)
                                num_slot_vn[0] = ncp[(h, t)]

                    @block.vector
                    def _(v, gi=gi, h=h):
                        v.wait_ge(SE.h, ncp[(h, 15)])
                        ch(v, VE, v.tensor_add(den_b[:, :], den_st[:, :],
                                               epss_p[:, :]))
                        ch(v, VE, v.reciprocal(rden_b[:, :], den_b[:, :]))
                        ch(v, VE, v.tensor_mul(cf_b[:, :], den_st[:, :],
                                               nrstd_p[:, :]))
                        ch(v, VE, v.tensor_scalar_add(cf_b[:, :], cf_b[:, :],
                                                      1.0 - EPS_ATT))
                        ch(v, VE, v.tensor_scalar(cf_b[:, :], cf_b[:, :],
                                                  1.0, 0.0, ALU.min, ALU.max))
                        ch(v, VE, v.tensor_mul(cf_b[:, :], cf_b[:, :],
                                               rstd_p[:, :]))

                    for t2 in range(T_OWN // 2):
                        tA, tB = 2 * t2, 2 * t2 + 1

                        @block.vector
                        def _(v, gi=gi, h=h, hi=hi, tA=tA, tB=tB):
                            if tA >= 2:
                                v.wait_ge(PE.h, wmv_pe[(h, tB - 2)])
                            mk_ = {}
                            for t in (tA, tB):
                                tm = tmp1[:, 128 * (t % 2):128 * (t % 2) + 128]
                                mk_[t] = VE.inc(v.scalar_tensor_tensor(
                                    tm, numstage[:, t * 129:t * 129 + 128],
                                    rden_b[:, t:t + 1],
                                    v_g[:, t * 128:(t + 1) * 128],
                                    ALU.mult, ALU.subtract))
                            for t in (tA, tB):
                                tm = tmp1[:, 128 * (t % 2):128 * (t % 2) + 128]
                                w = wmv_bf[t % 2]
                                v.wait_ge(VE.h, mk_[t])
                                v.tensor_mul(
                                    w[:, 0:128], tm,
                                    mb4_sb[:, t * 512 + hi * 128:t * 512 + hi * 128 + 128])
                                inst = v.tensor_copy(w[:, 128:129],
                                                     cf_b[:, t:t + 1])
                                vn[(h, t)] = VE.inc(inst)

                        @block.tensor
                        def _(te, gi=gi, h=h, hi=hi, tA=tA, tB=tB):
                            if tA == 0 and hi == 0:
                                te.wait_ge(SE.h, kexp_se[(gi, 15)])
                            for t in (tA, tB):
                                te.wait_ge(VE.h, vn[(h, t)])
                                w = wmv_bf[t % 2]
                                inst = None
                                for fs in range(FS):
                                    inst = te.matmul(
                                        pf[fs][:, 0:129],
                                        mk2[:, fs * 2048 + t * 128:fs * 2048 + (t + 1) * 128],
                                        w[:, 0:129],
                                        start=(t == 0), stop=(t == T_OWN - 1))
                                wmv_pe[(h, t)] = PE.inc(inst)

                    @block.vector
                    def _(v, h=h):
                        nonlocal da_read
                        v.wait_ge(PE.h, wmv_pe[(h, T_OWN - 1)])
                        if h >= 1:
                            DEL.wait(v, del_tok[h - 1])
                        d = da_st[h % 2]
                        mc = mc_h[h % 2]
                        inst = None
                        for fs in range(FS):
                            inst = v.scalar_tensor_tensor(
                                d[:, fs * 129:(fs + 1) * 129],
                                mc[:, fs * 129:(fs + 1) * 129], 1.0 / nranks,
                                pf[fs][:, 0:129], ALU.mult, ALU.add)
                        VE.inc(inst)
                        da_read = VE.n

                    @block.sync
                    def _(sy, h=h):
                        sy.wait_ge(VE.h, da_read)
                        del_tok[h] = DEL.load(
                            sy, h, ar_in_t.ap()[:, h * FS * 129:(h + 1) * FS * 129],
                            da_st[h % 2][:, :])
                    head_pe_done[h] = PE.n
                    block.flush()
                mk2_free["pe"] = PE.n

            @block.gpsimd
            def _(gp):
                for s_ in DEL.sems:
                    gp.wait_ge(s_.h, s_.n)
                gp.collective_compute(
                    "AllReduce", ALU.add, replica_groups=[list(range(nranks))],
                    ins=[ar_in_t.ap().opt()], outs=[ar_out_t.ap().opt()],
                ).then_inc(CC.h)
                CC.n += 1
            ar_cc_mark = CC.n
            block.flush()

        if debug:
            dout("dbg_arin", [128, 16 * FS * 129])
            dout("dbg_arout", [128, 16 * FS * 129])

            @block.gpsimd
            def _(gp):
                gp.wait_ge(CC.h, CC.n)
                gp.dma_start(out=dbg["dbg_arin"], in_=ar_in_t.ap()).then_inc(GDO.h, 16)
                gp.dma_start(out=dbg["dbg_arout"], in_=ar_out_t.ap()).then_inc(GDO.h, 16)
                GDO.n += 32

        if phases < 3:
            return _finish_early(nc, dbg, block, out_own, g, OST, SYO, MIR, GDO, mir_total, scopes)
        xnT_scope.close()
        scopes.remove(xnT_scope)

        kvq_scope = ExitStack()
        scopes.append(kvq_scope)
        kv_dT = kvq_scope.enter_context(
            nc.sbuf_tensor("kv_dT", [128, 4 * T_ALL * 128], BF16))
        q_dT = kvq_scope.enter_context(
            nc.sbuf_tensor("q_dT", [128, 4 * T_ALL * 128], BF16))
        f1x = Slots(nc, top, "f1x", 2)
        f1c = Slots(nc, top, "f1c", 2)
        f1w = Slots(nc, top, "f1w", 1)
        f1r = Slots(nc, top, "f1r", 1)
        with ExitStack() as ph:
            e0 = ph.enter_context
            kvdw = e0(nc.sbuf_tensor("f1_kvdw", [128, KB * 512], BF16))
            qdw = e0(nc.sbuf_tensor("f1_qdw", [128, KB * 512], BF16))
            rkw = e0(nc.sbuf_tensor("f1_rkw", [128, KB * 512], BF16))
            rqw = e0(nc.sbuf_tensor("f1_rqw", [128, 4 * 512], BF16))
            cs = [e0(nc.sbuf_tensor(f"f1_cs{i}", [128, 512], F32)) for i in range(2)]
            xnt = [e0(nc.sbuf_tensor(f"f1_xnt{i}", [128, KB * 128], BF16)) for i in range(2)]
            r1_sb = e0(nc.sbuf_tensor("f1_r1", [128, 2 * T_OWN], F32))
            pbuf_kv = e0(nc.sbuf_tensor("f1_pbkv", [128, 512], BF16))
            pbuf_q = e0(nc.sbuf_tensor("f1_pbq", [128, 512], BF16))
            krp_k = e0(nc.sbuf_tensor("f1_krpk", [128, 512], F32))
            krp_q = e0(nc.sbuf_tensor("f1_krpq", [128, 512], F32))
            rt = e0(nc.sbuf_tensor("f1_rt", [128, 2048], F32))
            krbf_k = e0(nc.sbuf_tensor("f1_krbk", [128, 512], BF16))
            krbf_q = e0(nc.sbuf_tensor("f1_krbq", [128, 512], BF16))
            rstage = [e0(nc.sbuf_tensor(f"f1_rst{i}", [128, 512], BF16))
                      for i in range(2)]
            rstq = Slots(nc, top, "rstq", 2)
            pp = [e0(nc.psum_tensor(f"f1_pp{i}", [128, 512], F32)) for i in range(4)]
            pt = [e0(nc.psum_tensor(f"f1_pt{i}", [128, 1024], BF16)) for i in range(2)]

            @block.gpsimd
            def _(gp):
                gp.wait_ge(PE.h, stageA_pe_done[3])
                gp.wait_ge(VE.h, mb4_ve["v"])
                gp.wait_ge(SE.h, kexp_se[(3, 15)])
                gp.wait_ge(XSP.h, XSP.n)
                gp.wait_ge(GDO.h, GDO.n)
                gp.wait_ge(CC.h, xn_ag_cc)
                sem = f1w.sems[0]
                for dst, src in ((kvdw, g["kv_d_w"]), (qdw, g["q_d_w"]),
                                 (rkw, g["rope_k_my"])):
                    gp.dma_start(out=dst[:, :].rearrange("p (k c) -> p k c", k=KB),
                                 in_=src.rearrange("(k p) c -> p k c", p=128)).then_inc(sem.h, 16)
                    sem.n += 16
                gp.dma_start(out=rqw[:, :].rearrange("p (k c) -> p k c", k=4),
                             in_=g["rope_q_my"].rearrange("(k p) c -> p k c", p=128)).then_inc(sem.h, 16)
                sem.n += 16
            f1w_tok = (0, f1w.sems[0].n)

            @block.vector
            def _(v):
                f1w.wait(v, f1w_tok)
                v.wait_ge(GP.h, consts_mark)
                inst = None
                for dst in (kvdw, qdw, rkw):
                    for k in range(KB):
                        inst = v.tensor_scalar_mul(dst[:, k * 512:(k + 1) * 512],
                                                   dst[:, k * 512:(k + 1) * 512],
                                                   ln1_sb[:, k:k + 1])
                VE.inc(inst)
            f1w_ve = VE.n

            xld = {}
            cld = {}
            tp_marks = {}
            pp_read = [0, 0, 0, 0]
            pt_read = [0, 0]
            rope_done = {}
            rst_tok = {}
            xnt_last_pe = {}

            pa_pe_end = PE.n
            pa_se_end = SE.n

            def f1_load(j):
                @block.sync
                def _(sy, j=j):
                    if j == 0:
                        sy.wait_ge(PE.h, pa_pe_end)
                        sy.wait_ge(SE.h, pa_se_end)
                        sy.wait_ge(VE.h, f1w_ve)
                        sy.wait_ge(CC.h, xn_ag_cc)
                        s0 = f1r.sems[0]
                        sy.dma_start(out=r1_sb[:, 0:T_OWN],
                                     in_=scr_r1_all[0:128, :]).then_inc(s0.h, 16)
                        sy.dma_start(out=r1_sb[:, T_OWN:2 * T_OWN],
                                     in_=scr_r1_all[128:256, :]).then_inc(s0.h, 16)
                        s0.n += 32
                    if j >= 2:
                        sy.wait_ge(PE.h, xnt_last_pe[j - 2])
                        sy.wait_ge(VE.h, rope_done.get(j - 2, 0))
                    hbj, twj = j // T_OWN, j % T_OWN
                    r0 = hbj * T_OWN * 128 + twj * 128
                    xld[j] = f1x.load(sy, j, xnt[j % 2][:, :],
                                      scr_xnT_all[r0:r0 + 128, :])
                    cld[j] = f1c.load(sy, j, cs[j % 2][:, 0:256],
                                      g["cos_rep"][j * 128:(j + 1) * 128, :])
                    s_ = f1c.sems[j % 2]
                    sy.dma_start(out=cs[j % 2][:, 256:512],
                                 in_=g["sin_rep"][j * 128:(j + 1) * 128, :]).then_inc(s_.h, 16)
                    s_.n += 16
                    cld[j] = (cld[j][0], s_.n)

            f1_load(0)
            for t in range(T_ALL):
                hb, tw = t // T_OWN, t % T_OWN
                if t + 1 < T_ALL:
                    f1_load(t + 1)

                r1c = r1_sb[:, hb * T_OWN + tw:hb * T_OWN + tw + 1]

                def proj(t, lhs_blocks, w_sb, nk, bank, scale_ap, out_bf,
                         extra_pe_wait=None, lhs_tok=None):
                    @block.tensor
                    def _(te):
                        if t == 0:
                            te.wait_ge(VE.h, f1w_ve)
                            te.wait_ge(SE.h, pa_se_end)
                        if lhs_tok is not None:
                            f1x.wait(te, lhs_tok)
                        te.wait_ge(VE.h, pp_read[bank])
                        if extra_pe_wait is not None:
                            te.wait_ge(VE.h, extra_pe_wait)
                        inst = None
                        for k in range(nk):
                            inst = te.matmul(pp[bank][:, :], lhs_blocks(k),
                                             w_sb[:, k * 512:(k + 1) * 512],
                                             start=(k == 0), stop=(k == nk - 1))
                        tp_marks[(t, "pj", bank)] = PE.inc(inst)

                    @block.vector
                    def _(v):
                        v.wait_ge(PE.h, tp_marks[(t, "pj", bank)])
                        if t == 0:
                            v.wait_ge(f1r.sems[0].h, f1r.sems[0].n)
                        if scale_ap is not None:
                            inst = v.tensor_scalar_mul(out_bf[:, :], pp[bank][:, :],
                                                       scale_ap)
                        else:
                            inst = v.tensor_copy(out_bf[:, :], pp[bank][:, :])
                        pp_read[bank] = VE.inc(inst)
                    return VE.n

                def t4(t, src_bf, dst_sb, dst_block_cols, hh, rst_slot=None):
                    @block.tensor
                    def _(te):
                        te.wait_ge(VE.h, VE.n)
                        te.wait_ge(VE.h, pt_read[hh])
                        inst = None
                        for j in range(4):
                            inst = te.transpose(pt[hh][:, j * 128:(j + 1) * 128],
                                                src_bf[:, j * 128:(j + 1) * 128],
                                                id_bf[:, :])
                        tp_marks[(t, "t4", hh)] = PE.inc(inst)

                    @block.vector
                    def _(v):
                        v.wait_ge(PE.h, tp_marks[(t, "t4", hh)])
                        if rst_slot is not None and rst_tok.get(rst_slot):
                            rstq.wait(v, rst_tok[rst_slot])
                        inst = None
                        for j in range(4):
                            inst = v.tensor_copy(dst_block_cols(j),
                                                 pt[hh][:, j * 128:(j + 1) * 128])
                        pt_read[hh] = VE.inc(inst)
                    return VE.n

                xblk = lambda k, t=t: xnt[t % 2][:, k * 128:(k + 1) * 128]

                proj(t, xblk, kvdw, KB, 0, r1c, pbuf_kv, lhs_tok=xld[t])
                t4(t, pbuf_kv, kv_dT,
                   lambda j, t=t: kv_dT[:, (j * T_ALL + t) * 128:(j * T_ALL + t) * 128 + 128],
                   0)
                proj(t, xblk, qdw, KB, 1, r1c, pbuf_q)
                qdT_ve = t4(t, pbuf_q, q_dT,
                            lambda j, t=t: q_dT[:, (j * T_ALL + t) * 128:(j * T_ALL + t) * 128 + 128],
                            1)
                proj(t, xblk, rkw, KB, 2, r1c, krp_k)
                xnt_last_pe[t] = PE.n
                proj(t, lambda k, t=t: q_dT[:, (k * T_ALL + t) * 128:(k * T_ALL + t) * 128 + 128],
                     rqw, 4, 3, None, krp_q, extra_pe_wait=qdT_ve)

                @block.vector
                def _(v, t=t):
                    f1c.wait(v, cld[t])
                    v.wait_ge(VE.h, VE.n)
                    cvv = cs[t % 2][:, 0:256].rearrange("p (h c) -> p h c", h=8)
                    svv = cs[t % 2][:, 256:512].rearrange("p (h c) -> p h c", h=8)
                    parts = []
                    marks = []
                    for kr, ob, r0 in ((krp_k, krbf_k, 0), (krp_q, krbf_q, 1024)):
                        kv_ = kr[:, :].rearrange("p (h c) -> p h c", h=8)
                        sl4 = [rt[:, r0 + i * 256:r0 + (i + 1) * 256].rearrange(
                            "p (h c) -> p h c", h=8) for i in range(4)]
                        v.tensor_mul(sl4[0], kv_[:, :, 0:32], cvv)
                        v.tensor_mul(sl4[1], kv_[:, :, 32:64], svv)
                        v.tensor_mul(sl4[2], kv_[:, :, 32:64], cvv)
                        inst = v.tensor_mul(sl4[3], kv_[:, :, 0:32], svv)
                        marks.append(VE.inc(inst))
                        parts.append((ob, sl4))
                    inst = None
                    for ci, (ob, sl4) in enumerate(parts):
                        v.wait_ge(VE.h, marks[ci])
                        ov = ob[:, :].rearrange("p (h c) -> p h c", h=8)
                        inst = v.tensor_sub(ov[:, :, 0:32], sl4[0], sl4[1])
                        inst = v.tensor_add(ov[:, :, 32:64], sl4[2], sl4[3])
                    rope_done[t] = VE.inc(inst)

                t4(t, krbf_k, None,
                   lambda j: rstage[0][:, j * 128:(j + 1) * 128], 0, rst_slot=0)

                @block.sync
                def _(sy, t=t):
                    sy.wait_ge(VE.h, VE.n)
                    dst = scr_kT.rearrange("(f p) s -> p f s", p=128)[
                        :, :, t * 128:(t + 1) * 128]
                    rst_tok[0] = rstq.load(
                        sy, 0, dst, rstage[0][:, :].rearrange("p (f c) -> p f c", f=4))

                t4(t, krbf_q, None,
                   lambda j: rstage[1][:, j * 128:(j + 1) * 128], 1, rst_slot=1)

                @block.sync
                def _(sy, t=t):
                    sy.wait_ge(VE.h, VE.n)
                    dst = scr_qT.rearrange("(f p) s -> p f s", p=128)[
                        :, :, t * 128:(t + 1) * 128]
                    rst_tok[1] = rstq.load(
                        sy, 1, dst, rstage[1][:, :].rearrange("p (f c) -> p f c", f=4))
                if t % 4 == 3:
                    block.flush()
            f1_done_ve = VE.n
            f1_done_syo = SYO.n
            block.flush()

        if debug:
            dout("dbg_kvdT", [128, 4 * T_ALL * 128], BF16)
            dout("dbg_qdT", [128, 4 * T_ALL * 128], BF16)
            dout("dbg_krT", [512, S], BF16)
            dout("dbg_qrT", [512, S], BF16)

            @block.gpsimd
            def _(gp):
                gp.wait_ge(VE.h, f1_done_ve)
                for s_ in rstq.sems:
                    gp.wait_ge(s_.h, s_.n)
                for dst, src in ((dbg["dbg_kvdT"], kv_dT[:, :]),
                                 (dbg["dbg_qdT"], q_dT[:, :]),
                                 (dbg["dbg_krT"], scr_kT),
                                 (dbg["dbg_qrT"], scr_qT)):
                    gp.dma_start(out=dst, in_=src).then_inc(GDO.h, 16)
                    GDO.n += 16

        if phases < 4:
            return _finish_early(nc, dbg, block, out_own, g, OST, SYO, MIR, GDO, mir_total, scopes)
        f2w = Slots(nc, top, "f2w", 1)
        rpl = Slots(nc, top, "rpl", 2)
        AVS = Slots(nc, top, "avs", 2)
        with ExitStack() as ph:
            e0 = ph.enter_context
            ku_sb = e0(nc.sbuf_tensor("f2_ku", [128, 4 * 512], BF16))
            qu_sb = e0(nc.sbuf_tensor("f2_qu", [128, 4 * 512], BF16))
            vu_sb = e0(nc.sbuf_tensor("f2_vu", [128, 4 * 1024], BF16))
            kT_grp = [e0(nc.sbuf_tensor(f"f2_kT{j}", [128, S], BF16)) for j in range(2)]
            qT_grp = [e0(nc.sbuf_tensor(f"f2_qT{j}", [128, S], BF16)) for j in range(2)]
            V_grp = e0(nc.sbuf_tensor("f2_V", [128, T_ALL * 260], BF16))
            stexp = e0(nc.sbuf_tensor("f2_stexp", [128, (28 + 32) * 512], BF16))
            av_bf = [e0(nc.sbuf_tensor(f"f2_avbf{i}", [128, 128], BF16))
                     for i in range(2)]
            avstage = [e0(nc.sbuf_tensor(f"f2_avs{j}", [128, S], BF16)) for j in range(2)]
            rcp = e0(nc.sbuf_tensor("f2_rcp", [128, 2], F32))
            st_ps = [e0(nc.psum_tensor(f"f2_st{i}", [128, 512], F32)) for i in range(3)]
            av_ps = [e0(nc.psum_tensor(f"f2_av{i}", [128, 512], F32)) for i in range(2)]
            bld_ps = [e0(nc.psum_tensor(f"f2_bld{i}", [128, 512], F32))
                      for i in range(2)]
            tp_ps = e0(nc.psum_tensor("f2_tp", [128, 1024], BF16))

            @block.gpsimd
            def _(gp):
                gp.wait_ge(VE.h, VE.n)
                sem = f2w.sems[0]
                for dst, src, w in ((ku_sb, g["k_u_my"], 512),
                                    (qu_sb, g["q_u_my"], 512),
                                    (vu_sb, g["v_u_my"], 1024)):
                    gp.dma_start(out=dst[:, :].rearrange("p (k c) -> p k c", k=4),
                                 in_=src.rearrange("(k p) c -> p k c", p=128)).then_inc(sem.h, 16)
                    sem.n += 16
            f2w_tok = (0, f2w.sems[0].n)

            bld_read = [0, 0]
            st_free = [0, 0, 0]
            av_free = [0, 0]
            slab_free = [0, 0]
            avT_marks = {}
            tp_read = [0, 0]
            avT_pe = 0
            av_done_pe = {}
            exp_marks = {}
            sc_marks = {}
            rope_tok = {}
            avs_tok = {}
            grp_ve_done = {}

            for gr in range(4):
                @block.sync
                def _(sy, gr=gr):
                    if gr >= 1:
                        sy.wait_ge(PE.h, grp_pe_done[gr - 1])
                    for s_ in rstq.sems:
                        sy.wait_ge(s_.h, s_.n)
                    for j in range(2):
                        hl = gr * 2 + j
                        s_ = rpl.sems[j]
                        sy.dma_start(out=kT_grp[j][64:128, :],
                                     in_=scr_kT[hl * 64:(hl + 1) * 64, :]).then_inc(s_.h, 16)
                        s_.n += 16
                        sy.dma_start(out=qT_grp[j][64:128, :],
                                     in_=scr_qT[hl * 64:(hl + 1) * 64, :]).then_inc(s_.h, 16)
                        s_.n += 16
                        rope_tok[(gr, j)] = (j, s_.n)

                for j in range(2):
                    hl = gr * 2 + j
                    for c in range(8):
                        @block.tensor
                        def _(te, gr=gr, j=j, hl=hl, c=c):
                            if gr == 0 and j == 0 and c == 0:
                                te.wait_ge(f2w.sems[0].h, f2w_tok[1])
                                te.wait_ge(VE.h, f1_done_ve)
                            if gr >= 1 and j == 0 and c == 0:
                                te.wait_ge(VE.h, grp_ve_done[gr - 1])
                            te.wait_ge(VE.h, bld_read[c % 2])
                            inst = None
                            for kk in range(4):
                                inst = te.matmul(
                                    bld_ps[c % 2][0:64, 0:512],
                                    ku_sb[:, kk * 512 + hl * 64:kk * 512 + hl * 64 + 64],
                                    kv_dT[:, (kk * T_ALL + 4 * c) * 128:(kk * T_ALL + 4 * c) * 128 + 512],
                                    start=(kk == 0), stop=(kk == 3))
                            sc_marks[("kb", gr, j, c)] = PE.inc(inst)

                        @block.vector
                        def _(v, gr=gr, j=j, c=c):
                            v.wait_ge(PE.h, sc_marks[("kb", gr, j, c)])
                            VE.inc(v.tensor_copy(kT_grp[j][0:64, c * 512:(c + 1) * 512],
                                                 bld_ps[c % 2][0:64, 0:512]))
                            bld_read[c % 2] = VE.n
                    for c in range(8):
                        @block.tensor
                        def _(te, gr=gr, j=j, hl=hl, c=c):
                            te.wait_ge(VE.h, bld_read[c % 2])
                            inst = None
                            for kk in range(4):
                                inst = te.matmul(
                                    bld_ps[c % 2][0:64, 0:512],
                                    qu_sb[:, kk * 512 + hl * 64:kk * 512 + hl * 64 + 64],
                                    q_dT[:, (kk * T_ALL + 4 * c) * 128:(kk * T_ALL + 4 * c) * 128 + 512],
                                    start=(kk == 0), stop=(kk == 3))
                            sc_marks[("qb", gr, j, c)] = PE.inc(inst)

                        @block.vector
                        def _(v, gr=gr, j=j, c=c):
                            v.wait_ge(PE.h, sc_marks[("qb", gr, j, c)])
                            VE.inc(v.tensor_copy(qT_grp[j][0:64, c * 512:(c + 1) * 512],
                                                 bld_ps[c % 2][0:64, 0:512]))
                            bld_read[c % 2] = VE.n
                    for ta in range(T_ALL):
                        @block.tensor
                        def _(te, gr=gr, j=j, hl=hl, ta=ta):
                            te.wait_ge(VE.h, bld_read[ta % 2])
                            inst = None
                            for kk in range(4):
                                inst = te.matmul(
                                    bld_ps[ta % 2][:, 0:128],
                                    kv_dT[:, (kk * T_ALL + ta) * 128:(kk * T_ALL + ta) * 128 + 128],
                                    vu_sb[:, kk * 1024 + hl * 128:kk * 1024 + (hl + 1) * 128],
                                    start=(kk == 0), stop=(kk == 3))
                            sc_marks[("vb", gr, j, ta)] = PE.inc(inst)

                        @block.vector
                        def _(v, gr=gr, j=j, ta=ta):
                            v.wait_ge(PE.h, sc_marks[("vb", gr, j, ta)])
                            VE.inc(v.tensor_copy(
                                V_grp[:, ta * 260 + j * 130:ta * 260 + j * 130 + 128],
                                bld_ps[ta % 2][:, 0:128]))
                            bld_read[ta % 2] = VE.n

                @block.gpsimd
                def _(gp, gr=gr):
                    gp.wait_ge(VE.h, VE.n)
                    ones_view = V_grp[:, :].rearrange("p (t c) -> p t c", t=T_ALL)
                    gp.memset(ones_view[:, :, 128:129], 1.0)
                    GP.inc(gp.memset(ones_view[:, :, 258:259], 1.0))
                v_ones_mark = GP.n

                grp_pe_done = grp_pe_done if gr >= 1 else {}

                def f2_scores(gr, j, qg):
                    nkb = 4 * qg + 4
                    sb = (qg % 2) * 14336
                    for kb in range(nkb):
                        @block.tensor
                        def _(te, gr=gr, j=j, qg=qg, kb=kb):
                            if kb == 0 and qg == 0 and j == 0:
                                rpl.wait(te, rope_tok[(gr, 0)])
                                rpl.wait(te, rope_tok[(gr, 1)])
                                te.wait_ge(GP.h, v_ones_mark)
                            te.wait_ge(SE.h, st_free[kb % 3])
                            inst = te.matmul(
                                st_ps[kb % 3][:, :],
                                kT_grp[j][:, kb * 128:(kb + 1) * 128],
                                qT_grp[j][:, qg * 512:(qg + 1) * 512],
                                start=True, stop=True)
                            sc_marks[("st", j, qg, kb)] = PE.inc(inst)

                        @block.scalar
                        def _(s, gr=gr, j=j, qg=qg, kb=kb, sb=sb):
                            s.wait_ge(PE.h, sc_marks[("st", j, qg, kb)])
                            s.wait_ge(PE.h, slab_free[qg % 2])
                            inst = s.activation(
                                stexp[:, sb + kb * 512:sb + (kb + 1) * 512],
                                st_ps[kb % 3][:, :], AF.Exp,
                                scale=1.0 / math.sqrt(128.0))
                            exp_marks[(j, qg, kb)] = SE.inc(inst)
                            st_free[kb % 3] = SE.n

                        if kb >= 4 * qg:
                            @block.vector
                            def _(v, gr=gr, j=j, qg=qg, kb=kb, sb=sb):
                                v.wait_ge(SE.h, exp_marks[(j, qg, kb)])
                                inst = None
                                for i in range(4):
                                    qt = 4 * qg + i
                                    sl = stexp[:, sb + kb * 512 + i * 128:
                                               sb + kb * 512 + (i + 1) * 128]
                                    if kb == qt:
                                        inst = v.tensor_mul(sl, sl, tri_bf[:, :])
                                    elif kb > qt:
                                        inst = v.memset(sl, 0.0)
                                if inst is not None:
                                    sc_marks[("mveq", j, qg, kb)] = VE.inc(inst)

                def f2_avs(gr, j, qg):
                    nonlocal avT_pe
                    sb = (qg % 2) * 14336
                    for i in range(4):
                        qt = 4 * qg + i

                        @block.tensor
                        def _(te, gr=gr, j=j, qg=qg, i=i, qt=qt, sb=sb):
                            te.wait_ge(VE.h, av_free[i % 2])
                            te.wait_ge(SE.h, exp_marks[(j, qg, 4 * qg + 3)])
                            vm = max((sc_marks.get(("mveq", j, qg, kb), 0)
                                      for kb in range(4 * qg, 4 * qg + 4)),
                                     default=0)
                            if vm:
                                te.wait_ge(VE.h, vm)
                            inst = None
                            for kb in range(qt + 1):
                                inst = te.matmul(
                                    av_ps[i % 2][:, 0:129],
                                    stexp[:, sb + kb * 512 + i * 128:
                                          sb + kb * 512 + (i + 1) * 128],
                                    V_grp[:, kb * 260 + j * 130:kb * 260 + j * 130 + 129],
                                    start=(kb == 0), stop=(kb == qt))
                            av_done_pe[(j, qg)] = PE.inc(inst)
                            sc_marks[("avmm", j, qt)] = PE.n

                        @block.vector
                        def _(v, gr=gr, j=j, i=i, qt=qt):
                            v.wait_ge(PE.h, sc_marks[("avmm", j, qt)])
                            v.wait_ge(PE.h, avT_marks.get(qt % 2, 0))
                            ch(v, VE, v.reciprocal(rcp[:, 1:2],
                                                   av_ps[i % 2][:, 128:129]))
                            inst = v.tensor_scalar_mul(
                                av_bf[qt % 2][:, :], av_ps[i % 2][:, 0:128],
                                rcp[:, 1:2])
                            av_free[i % 2] = VE.inc(inst)

                        @block.tensor
                        def _(te, j=j, qt=qt):
                            te.wait_ge(VE.h, VE.n)
                            te.wait_ge(VE.h, tp_read[qt % 2])
                            avT_marks[qt % 2] = PE.inc(te.transpose(
                                tp_ps[:, (qt % 2) * 128:(qt % 2) * 128 + 128],
                                av_bf[qt % 2][:, :], id_bf[:, :]))

                        @block.vector
                        def _(v, gr=gr, j=j, qt=qt):
                            v.wait_ge(PE.h, avT_marks[qt % 2])
                            if avs_tok.get(j):
                                AVS.wait(v, avs_tok[j])
                            inst = v.tensor_copy(
                                avstage[j][:, qt * 128:(qt + 1) * 128],
                                tp_ps[:, (qt % 2) * 128:(qt % 2) * 128 + 128])
                            tp_read[qt % 2] = VE.inc(inst)
                    slab_free[qg % 2] = PE.n

                for j in range(2):
                    hl = gr * 2 + j
                    for qg in range(8):
                        f2_scores(gr, j, qg)
                        if qg >= 1:
                            f2_avs(gr, j, qg - 1)
                        if qg % 3 == 2:
                            block.flush()
                    f2_avs(gr, j, 7)

                    @block.sync
                    def _(sy, gr=gr, j=j, hl=hl):
                        sy.wait_ge(VE.h, VE.n)
                        avs_tok[j] = AVS.load(
                            sy, j, ag_in_t.ap()[hl * 128:(hl + 1) * 128, :],
                            avstage[j][:, :])
                grp_pe_done[gr] = PE.n
                grp_ve_done[gr] = VE.n
                block.flush()

            @block.gpsimd
            def _(gp):
                for s_ in AVS.sems:
                    gp.wait_ge(s_.h, s_.n)
                if nranks == 8:
                    gp.collective_compute(
                        "AllGather", ALU.bypass,
                        replica_groups=[[0, 1], [2, 3], [4, 5], [6, 7]],
                        ins=[ag_in_t.ap().opt()], outs=[ag_out_t.ap().opt()],
                    ).then_inc(CC.h)
                    CC.n += 1
                else:
                    gp.dma_start(out=ag_out_t.ap()[0:NHH * 128, :],
                                 in_=ag_in_t.ap()).then_inc(GDO.h, 16)
                    gp.dma_start(out=ag_out_t.ap()[NHH * 128:, :],
                                 in_=ag_in_t.ap()).then_inc(GDO.h, 16)
                    GDO.n += 32
                    gp.wait_ge(GDO.h, GDO.n)
                    gp.wait_ge(CC.h, CC.n)
                    CC.inc(gp.memset(rcp[:, 0:1], 0.0))
            block.flush()
        f2_done_cc = CC.n

        if debug:
            dout("dbg_agin", [NHH * 128, S], BF16)

            @block.gpsimd
            def _(gp):
                gp.wait_ge(CC.h, CC.n)
                gp.dma_start(out=dbg["dbg_agin"], in_=ag_in_t.ap()).then_inc(GDO.h, 16)
                GDO.n += 16

        if phases < 5:
            return _finish_early(nc, dbg, block, out_own, g, OST, SYO, MIR, GDO, mir_total, scopes)
        kvq_scope.close()
        scopes.remove(kvq_scope)

        a2w = Slots(nc, top, "a2w", 1)
        a2c = Slots(nc, top, "a2c", 4)
        MHS = Slots(nc, top, "mhs", 2)
        a2x = Slots(nc, top, "a2x", 2)
        a2q = Slots(nc, top, "a2q", 1)
        a2_se_entry, a2_ve_entry = SE.n, VE.n
        with ExitStack() as ph:
            e0 = ph.enter_context
            wq_bf = e0(nc.sbuf_tensor("a2_wq", [128, KB * D], BF16))
            bq_bf = e0(nc.sbuf_tensor("a2_bq", [1, D], BF16))
            newmem = e0(nc.sbuf_tensor("a2_nm", [128, 16 * FS * 129], BF16))
            xnt_a = [e0(nc.sbuf_tensor(f"a2_xnt{i}", [128, KB * 128], BF16))
                     for i in range(2)]
            x2p = [e0(nc.sbuf_tensor(f"a2_x2p{i}", [128, D], BF16)) for i in range(2)]
            x2n = [e0(nc.sbuf_tensor(f"a2_x2n{i}", [128, D], BF16)) for i in range(2)]
            mq_bf = [e0(nc.sbuf_tensor(f"a2_mq{i}", [128, 16 * FDD], BF16))
                     for i in range(2)]
            sqa = e0(nc.sbuf_tensor("a2_sqa", [128, D], BF16))
            sqt = e0(nc.sbuf_tensor("a2_sqt", [128, D], BF16))
            nsta = [e0(nc.sbuf_tensor(f"a2_nst{i}", [128, 16], F32)) for i in range(2)]
            escr = e0(nc.sbuf_tensor("a2_escr", [128, 64], F32))
            epss = [e0(nc.sbuf_tensor(f"a2_eps{i}", [128, 16], F32)) for i in range(2)]
            mqT_sb = [e0(nc.sbuf_tensor(f"a2_mqT{i}", [128, FDD], BF16))
                      for i in range(2)]
            mh_sb = [e0(nc.sbuf_tensor(f"a2_mh{i}", [128, D], F32)) for i in range(2)]
            den_sc = e0(nc.sbuf_tensor("a2_den", [128, 4], F32))
            qp_ps = [e0(nc.psum_tensor(f"a2_qp{i}", [128, 512], F32)) for i in range(4)]
            na_ps = [e0(nc.psum_tensor(f"a2_na{i}", [128, 512], F32)) for i in range(2)]
            tpq_ps = [e0(nc.psum_tensor(f"a2_tp{i}", [128, 1024], BF16))
                      for i in range(2)]

            @block.gpsimd
            def _(gp):
                gp.wait_ge(VE.h, VE.n)
                gp.wait_ge(SE.h, SE.n)
                gp.wait_ge(PE.h, PE.n)
                gp.wait_ge(GDO.h, GDO.n)
                sem = a2w.sems[0]
                gp.dma_start(out=bq_bf[:, :], in_=g["bq"]).then_inc(sem.h, 16)
                sem.n += 16
            a2w_tok = (0, a2w.sems[0].n)

            nm_tok = {}

            @block.gpsimd
            def _(gp):
                gp.wait_ge(CC.h, ar_cc_mark)
                for s_ in AVS.sems:
                    gp.wait_ge(s_.h, s_.n)
                for i in range(4):
                    nm_tok[i] = a2c.load(
                        gp, i, newmem[:, i * 3096:(i + 1) * 3096],
                        ar_out_t.ap()[:, i * 3096:(i + 1) * 3096])

            xld = {}

            @block.sync
            def _(sy):
                sy.wait_ge(VE.h, VE.n)
                sy.wait_ge(SE.h, SE.n)
                sy.wait_ge(PE.h, PE.n)
                sy.wait_ge(XSP.h, XSP.n)
                sy.wait_ge(MIR.h, MIR.n)
                sy.dma_start(out=wq_bf[:, :].rearrange("p (k c) -> p k c", k=KB),
                             in_=wq_mir.rearrange("(k p) c -> p k c", p=128)).then_inc(a2q.sems[0].h, 16)
                a2q.sems[0].n += 16
                for j in (0, 1):
                    xld[j] = a2x.load(sy, j, xnt_a[j % 2][:, :],
                                      scr_xnT[j * 128:(j + 1) * 128, :])

            qpe = {}
            relu_se = {}
            mul_ve = {}
            nst_ve = {}
            eps_se = {}
            epss_ve = {}
            tp_pe = {}
            cp_mk = {}
            num_pe = {}
            epi_ve = {}
            mh_tok = {}

            def a2_xload(j):
                @block.sync
                def _(sy, j=j):
                    sy.wait_ge(PE.h, qpe[(j - 2, 3)])
                    xld[j] = a2x.load(sy, j, xnt_a[j % 2][:, :],
                                      scr_xnT[j * 128:(j + 1) * 128, :])

            def a2_qproj(t):
                @block.tensor
                def _(te, t=t):
                    if t == 0:
                        te.wait_ge(a2w.sems[0].h, a2w_tok[1])
                        te.wait_ge(a2q.sems[0].h, a2q.sems[0].n)
                        te.wait_ge(SE.h, a2_se_entry)
                        te.wait_ge(VE.h, a2_ve_entry)
                    a2x.wait(te, xld[t])
                    for cq in range(4):
                        if t >= 1:
                            te.wait_ge(SE.h, relu_se[(t - 1, cq)])
                        inst = None
                        for k in range(KB):
                            inst = te.matmul(
                                qp_ps[cq][:, :],
                                xnt_a[t % 2][:, k * 128:(k + 1) * 128],
                                wq_bf[:, k * D + cq * 512:k * D + cq * 512 + 512],
                                start=(k == 0), stop=False)
                        inst = te.matmul(qp_ps[cq][:, :], ones_r[:, :],
                                         bq_bf[:, cq * 512:(cq + 1) * 512],
                                         start=False, stop=True)
                        qpe[(t, cq)] = PE.inc(inst)

            def a2_relus(t):
                @block.scalar
                def _(s, t=t):
                    if t >= 2:
                        s.wait_ge(VE.h, mul_ve[t - 2])
                    for cq in range(4):
                        s.wait_ge(PE.h, qpe[(t, cq)])
                        s.activation(x2p[t % 2][:, cq * 512:(cq + 1) * 512],
                                     qp_ps[cq][:, :], AF.Relu)
                        inst = s.activation(x2n[t % 2][:, cq * 512:(cq + 1) * 512],
                                            qp_ps[cq][:, :], AF.Relu, scale=-1.0)
                        relu_se[(t, cq)] = SE.inc(inst)

            def a2_muls(t):
                @block.vector
                def _(v, t=t):
                    v.wait_ge(SE.h, relu_se[(t, 3)])
                    if t >= 2:
                        v.wait_ge(PE.h, tp_pe[(t - 2, 15)])
                        v.wait_ge(SE.h, eps_se[t - 2])
                    xp = x2p[t % 2]
                    xn = x2n[t % 2]
                    mq = mq_bf[t % 2]
                    inst = None
                    for j in (1, 2, 3):
                        bp = (j - 1) * 2 * D
                        bn = bp + D
                        inst = v.tensor_mul(mq[:, bp + j:bp + D],
                                            xp[:, j:D], xp[:, 0:D - j])
                        inst = v.tensor_mul(mq[:, bn + j:bn + D],
                                            xn[:, j:D], xn[:, 0:D - j])
                    ch(v, VE, inst)
                    pv = xp[:, :].rearrange("p (h c) -> p h c", c=128)
                    nv = xn[:, :].rearrange("p (h c) -> p h c", c=128)
                    for j in (1, 2, 3):
                        bp = (j - 1) * 2 * D
                        bn = bp + D
                        mpv = mq[:, bp:bp + D].rearrange("p (h c) -> p h c", c=128)
                        mnv = mq[:, bn:bn + D].rearrange("p (h c) -> p h c", c=128)
                        inst = v.tensor_mul(mpv[:, :, 0:j], pv[:, :, 0:j],
                                            nv[:, :, 128 - j:128])
                        inst = v.tensor_mul(mnv[:, :, 0:j], nv[:, :, 0:j],
                                            pv[:, :, 128 - j:128])
                    mul_ve[t] = VE.inc(inst)
                    v.wait_ge(VE.h, mul_ve[t])
                    ch(v, VE, v.tensor_mul(sqa[:, :], mq[:, 0:D], mq[:, 0:D]))
                    for b in range(1, 6):
                        ch(v, VE, v.tensor_mul(sqt[:, :], mq[:, b * D:(b + 1) * D],
                                               mq[:, b * D:(b + 1) * D]))
                        ch(v, VE, v.tensor_add(sqa[:, :], sqa[:, :], sqt[:, :]))
                    inst = v.reduce_sum(
                        nsta[t % 2][:, :],
                        sqa[:, :].rearrange("p (h c) -> p h c", c=128),
                        axis=AX.X)
                    nst_ve[t] = VE.inc(inst)

            def a2_eps(t):
                @block.scalar
                def _(s, t=t):
                    s.wait_ge(VE.h, nst_ve[t])
                    if t >= 2:
                        s.wait_ge(VE.h, epss_ve[t - 2])
                    a = escr[:, 32 * (t % 2):32 * (t % 2) + 16]
                    b = escr[:, 32 * (t % 2) + 16:32 * (t % 2) + 32]
                    ch(s, SE, s.activation(a, nsta[t % 2][:, :], AF.Ln))
                    eps_se[t] = SE.inc(s.activation(b, a, AF.Exp, scale=0.5))

            def a2_epss(t):
                @block.vector
                def _(v, t=t):
                    v.wait_ge(SE.h, eps_se[t])
                    epss_ve[t] = VE.inc(v.tensor_scalar_mul(
                        epss[t % 2][:, :],
                        escr[:, 32 * (t % 2) + 16:32 * (t % 2) + 32], EPS_ATT))

            def a2_tp(t, h):
                @block.tensor
                def _(te, t=t, h=h):
                    if h == 0:
                        te.wait_ge(VE.h, mul_ve[t])
                    if h >= 2:
                        e_, m_ = cp_mk[(t, h - 2)]
                        te.wait_ge(e_.h, m_)
                    elif t >= 1:
                        e_, m_ = cp_mk[(t - 1, 14 + h)]
                        te.wait_ge(e_.h, m_)
                    inst = None
                    for fs in range(FS):
                        inst = te.transpose(
                            tpq_ps[h % 2][:, fs * 128:(fs + 1) * 128],
                            mq_bf[t % 2][:, fs * D + h * 128:fs * D + (h + 1) * 128],
                            id_bf[:, :])
                    tp_pe[(t, h)] = PE.inc(inst)

            def a2_copy(t, h):
                def waits(eng):
                    eng.wait_ge(PE.h, tp_pe[(t, h)])
                    if h >= 2:
                        eng.wait_ge(PE.h, num_pe[(t, h - 2)])
                    elif t >= 1:
                        eng.wait_ge(PE.h, num_pe[(t - 1, 14 + h)])
                if h % 2 == 0:
                    @block.vector
                    def _(v, t=t, h=h):
                        waits(v)
                        cp_mk[(t, h)] = (VE, VE.inc(v.tensor_copy(
                            mqT_sb[h % 2][:, :], tpq_ps[h % 2][:, 0:FDD])))
                else:
                    @block.scalar
                    def _(s, t=t, h=h):
                        waits(s)
                        cp_mk[(t, h)] = (SE, SE.inc(s.activation(
                            mqT_sb[h % 2][:, :], tpq_ps[h % 2][:, 0:FDD],
                            AF.Identity)))

            def a2_num(t, h):
                @block.tensor
                def _(te, t=t, h=h):
                    e_, m_ = cp_mk[(t, h)]
                    te.wait_ge(e_.h, m_)
                    if t == 0 and h == 0:
                        for i in range(4):
                            a2c.wait(te, nm_tok[i])
                    if h >= 2:
                        te.wait_ge(VE.h, epi_ve[(t, h - 2)])
                    elif t >= 1:
                        te.wait_ge(VE.h, epi_ve[(t - 1, 14 + h)])
                    inst = None
                    for fs in range(FS):
                        inst = te.matmul(
                            na_ps[h % 2][:, 0:129],
                            mqT_sb[h % 2][:, fs * 128:(fs + 1) * 128],
                            newmem[:, (h * FS + fs) * 129:(h * FS + fs + 1) * 129],
                            start=(fs == 0), stop=(fs == FS - 1))
                    num_pe[(t, h)] = PE.inc(inst)

            def a2_epi(t, h):
                @block.vector
                def _(v, t=t, h=h):
                    v.wait_ge(PE.h, num_pe[(t, h)])
                    if h == 0 and t >= 2:
                        MHS.wait(v, mh_tok[t - 2])
                    a = den_sc[:, 2 * (h % 2):2 * (h % 2) + 1]
                    b = den_sc[:, 2 * (h % 2) + 1:2 * (h % 2) + 2]
                    ch(v, VE, v.tensor_add(a, na_ps[h % 2][:, 128:129],
                                           epss[t % 2][:, h:h + 1]))
                    ch(v, VE, v.reciprocal(b, a))
                    inst = v.tensor_scalar_mul(
                        mh_sb[t % 2][:, h * 128:(h + 1) * 128],
                        na_ps[h % 2][:, 0:128], b)
                    epi_ve[(t, h)] = VE.inc(inst)

            a2_qproj(0)
            a2_relus(0)
            a2_muls(0)
            block.flush()

            for t in range(T_OWN):
                if t + 2 < T_OWN:
                    a2_xload(t + 2)
                a2_eps(t)
                a2_epss(t)
                for h in range(16):
                    if h == 0 and t + 1 < T_OWN:
                        a2_qproj(t + 1)
                    a2_tp(t, h)
                    if h == 0 and t + 1 < T_OWN:
                        a2_relus(t + 1)
                    a2_copy(t, h)
                    if h >= 1:
                        a2_num(t, h - 1)
                        a2_epi(t, h - 1)
                a2_num(t, 15)
                a2_epi(t, 15)

                @block.sync
                def _(sy, t=t):
                    sy.wait_ge(VE.h, epi_ve[(t, 15)])
                    mh_tok[t] = MHS.load(sy, t, scr_mh[t * 128:(t + 1) * 128, :],
                                         mh_sb[t % 2][:, :])
                if t + 1 < T_OWN:
                    a2_muls(t + 1)
                block.flush()

        if debug:
            dout("dbg_mh", [TOK, D])

            @block.gpsimd
            def _(gp):
                for s_ in MHS.sems:
                    gp.wait_ge(s_.h, s_.n)
                gp.dma_start(out=dbg["dbg_mh"], in_=scr_mh).then_inc(GDO.h, 16)
                GDO.n += 16

        if phases < 6:
            return _finish_early(nc, dbg, block, out_own, g, OST, SYO, MIR, GDO, mir_total, scopes)
        HT_scope = ExitStack()
        scopes.append(HT_scope)
        HT = HT_scope.enter_context(
            nc.sbuf_tensor("HT", [128, KB * T_OWN * 128], BF16))
        w8c_all = HT_scope.enter_context(
            nc.sbuf_tensor("w8c", [128, T_OWN * 8], F32))
        f3w = Slots(nc, top, "f3w", 1)
        agl = Slots(nc, top, "agl", 2)
        HST = Slots(nc, top, "hst", 2)
        with ExitStack() as ph:
            e0 = ph.enter_context
            ow_bf = e0(nc.sbuf_tensor("f3_ow", [128, KB * D], BF16))
            avt = [e0(nc.sbuf_tensor(f"f3_avt{i}", [128, KB * 128], BF16))
                   for i in range(2)]
            avt_ab = [e0(nc.sbuf_tensor(f"f3_ab{i}", [128, 2 * KB * 128], BF16))
                      for i in range(2)]
            xf3 = [e0(nc.sbuf_tensor(f"f3_xf{i}", [128, D], F32)) for i in range(2)]
            xn_f = e0(nc.sbuf_tensor("f3_xn", [128, D], F32))
            h_sb = [e0(nc.sbuf_tensor(f"f3_h{i}", [128, D], F32)) for i in range(2)]
            hbf = e0(nc.sbuf_tensor("f3_hbf", [128, D], BF16))
            st3 = e0(nc.sbuf_tensor("f3_st", [128, 4], F32))
            sq3 = e0(nc.sbuf_tensor("f3_sq", [128, D], F32))
            op_ps = [e0(nc.psum_tensor(f"f3_op{i}", [128, 512], F32)) for i in range(2)]
            tp3_ps = [e0(nc.psum_tensor(f"f3_tp{i}", [128, 1024], BF16)) for i in range(2)]

            @block.gpsimd
            def _(gp):
                gp.wait_ge(VE.h, VE.n)
                gp.wait_ge(SE.h, SE.n)
                gp.wait_ge(GDO.h, GDO.n)
                sem = f3w.sems[0]
                gp.dma_start(out=ow_bf[:, :].rearrange("p (k c) -> p k c", k=KB),
                             in_=g["o_w"].rearrange("(k p) c -> p k c", p=128)).then_inc(sem.h, 16)
                sem.n += 16
            f3w_tok = (0, f3w.sems[0].n)

            ag_tok = {}
            sel_ve = {}
            op_pe = {}
            op_read = [0, 0]
            h_ve = {}
            h_se = {}
            h_tok = {}
            tp3_pe = {}
            tp3_read = [0, 0]
            ht_ve = {}
            xld_f3 = {}

            for t in range(T_OWN):
                @block.sync
                def _(sy, t=t):
                    if t == 0:
                        sy.wait_ge(CC.h, f2_done_cc)
                        sy.wait_ge(PE.h, PE.n)
                        sy.wait_ge(SE.h, SE.n)
                        sy.wait_ge(VE.h, VE.n)
                        for s_ in MHS.sems:
                            sy.wait_ge(s_.h, s_.n)
                    if t >= 2:
                        sy.wait_ge(VE.h, sel_ve[t - 2])
                        sy.wait_ge(SE.h, h_se[t - 2])
                    sl = t % 2
                    s_ = agl.sems[sl]
                    dst = avt_ab[sl][:, :].rearrange("p (a k c) -> p a k c",
                                                     a=2, k=KB)
                    for a in range(2):
                        src = ag_out_t.ap()[:, a * TOK + t * 128:a * TOK + (t + 1) * 128]
                        sy.dma_start(out=dst[:, a, :, :],
                                     in_=src.rearrange("(k p) c -> p k c", p=128)).then_inc(s_.h, 16)
                        s_.n += 16
                    ag_tok[t] = (sl, s_.n)
                    xld3 = xls.load(sy, t, xf3[sl][:, :],
                                    g["x_own"][t * 128:(t + 1) * 128, :])
                    xld_f3[t] = xld3

                @block.vector
                def _(v, t=t):
                    agl.wait(v, ag_tok[t])
                    sl = t % 2
                    ab = avt_ab[sl]
                    ch(v, VE, v.tensor_scalar_mul(avt[sl][:, :],
                                                  ab[:, 0:KB * 128], sel_sb[:, 0:1]))
                    inst = v.scalar_tensor_tensor(avt[sl][:, :], ab[:, KB * 128:],
                                                  sel_sb[:, 1:2], avt[sl][:, :],
                                                  ALU.mult, ALU.add)
                    sel_ve[t] = VE.inc(inst)

                @block.scalar
                def _(s, t=t):
                    xls.wait(s, xld_f3[t])
                    SE.inc(s.activation(xn_f[:, :], xf3[t % 2][:, :], AF.Identity,
                                        scale=rstd_own[:, t:t + 1],
                                        bias=nmr_own[:, t:t + 1]))
                xn_se = SE.n

                for cq in range(4):
                    @block.tensor
                    def _(te, t=t, cq=cq):
                        if t == 0 and cq == 0:
                            te.wait_ge(f3w.sems[0].h, f3w_tok[1])
                        te.wait_ge(VE.h, sel_ve[t])
                        te.wait_ge(VE.h, op_read[cq % 2])
                        inst = None
                        for k in range(KB):
                            inst = te.matmul(
                                op_ps[cq % 2][:, :],
                                avt[t % 2][:, k * 128:(k + 1) * 128],
                                ow_bf[:, k * D + cq * 512:k * D + cq * 512 + 512],
                                start=(k == 0), stop=(k == KB - 1))
                        op_pe[(t, cq)] = PE.inc(inst)

                    @block.vector
                    def _(v, t=t, cq=cq):
                        v.wait_ge(SE.h, xn_se)
                        v.wait_ge(PE.h, op_pe[(t, cq)])
                        if cq == 0 and t >= 2:
                            HST.wait(v, h_tok[t - 2])
                        inst = v.tensor_add(
                            h_sb[t % 2][:, cq * 512:(cq + 1) * 512],
                            op_ps[cq % 2][:, :], xn_f[:, cq * 512:(cq + 1) * 512])
                        op_read[cq % 2] = VE.inc(inst)
                h_ve[t] = VE.n

                @block.scalar
                def _(s, t=t):
                    s.wait_ge(VE.h, h_ve[t])
                    s.activation(sq3[:, :], h_sb[t % 2][:, :], AF.Square,
                                 accum_out=st3[:, 0:1])
                    s.drain()
                    SE.inc(s.activation(hbf[:, :], h_sb[t % 2][:, :], AF.Identity))
                h_se[t] = SE.n

                @block.vector
                def _(v, t=t):
                    v.wait_ge(SE.h, h_se[t])
                    VE.inc(v.tensor_scalar(st3[:, 1:2], st3[:, 0:1], 1.0 / D,
                                           EPS_RMS, ALU.mult, ALU.add))

                emit_rsqrt_newton(None, None, rstd2_own[:, t:t + 1], st3[:, 1:2],
                                  st3[:, 2:3])

                @block.sync
                def _(sy, t=t):
                    sy.wait_ge(VE.h, h_ve[t])
                    h_tok[t] = HST.load(sy, t, scr_h[t * 128:(t + 1) * 128, :],
                                        h_sb[t % 2][:, :])

                for hh in range(2):
                    @block.tensor
                    def _(te, t=t, hh=hh):
                        te.wait_ge(SE.h, h_se[t])
                        te.wait_ge(VE.h, tp3_read[hh])
                        inst = None
                        for j in range(8):
                            k = hh * 8 + j
                            inst = te.transpose(tp3_ps[hh][:, j * 128:(j + 1) * 128],
                                                hbf[:, k * 128:(k + 1) * 128],
                                                id_bf[:, :])
                        tp3_pe[(t, hh)] = PE.inc(inst)

                    @block.vector
                    def _(v, t=t, hh=hh):
                        v.wait_ge(PE.h, tp3_pe[(t, hh)])
                        inst = None
                        for j in range(8):
                            k = hh * 8 + j
                            inst = v.tensor_scalar_mul(
                                HT[:, (k * T_OWN + t) * 128:(k * T_OWN + t) * 128 + 128],
                                tp3_ps[hh][:, j * 128:(j + 1) * 128],
                                ln2_sb[:, k:k + 1])
                        tp3_read[hh] = VE.inc(inst)
                    ht_ve[t] = VE.n
                if t % 4 == 3:
                    block.flush()
            f3_ht_ve = VE.n
            block.flush()

        if phases < 7:
            return _finish_early(nc, dbg, block, out_own, g, OST, SYO, MIR, GDO, mir_total, scopes)
        m1w = Slots(nc, top, "m1w", 4)
        HES = Slots(nc, top, "hes", 2)
        with ExitStack() as ph:
            e0 = ph.enter_context
            rw_bf = e0(nc.sbuf_tensor("m1_rw", [128, KB * 7], BF16))
            rb_bf = e0(nc.sbuf_tensor("m1_rb", [1, 7], BF16))
            wgr = [e0(nc.sbuf_tensor(f"m1_wg{i}", [128, KB * 512], BF16))
                   for i in range(2)]
            wur = [e0(nc.sbuf_tensor(f"m1_wu{i}", [128, KB * 512], BF16))
                   for i in range(2)]
            pr_sb = e0(nc.sbuf_tensor("m1_pr", [128, 7], F32))
            nrstd2_sb = e0(nc.sbuf_tensor("m1_nrstd2", [128, T_OWN], F32))
            tt_sb = e0(nc.sbuf_tensor("m1_tt", [128, 12], F32))
            rsc_sb = e0(nc.sbuf_tensor("m1_rsc", [128, 28], F32))
            silu_sb = [e0(nc.sbuf_tensor(f"m1_silu{i}", [128, 512], F32))
                       for i in range(2)]
            he_bf = [e0(nc.sbuf_tensor(f"m1_he{i}", [128, 512], BF16))
                     for i in range(2)]
            stg = [e0(nc.sbuf_tensor(f"m1_stg{i}", [128, 4 * T_OWN * 128], BF16))
                   for i in range(2)]
            rt_ps = e0(nc.psum_tensor("m1_rt", [128, 512], F32))
            pg_ps = [e0(nc.psum_tensor(f"m1_pg{i}", [128, 512], F32)) for i in range(2)]
            pu_ps = [e0(nc.psum_tensor(f"m1_pu{i}", [128, 512], F32)) for i in range(2)]
            te_ps = [e0(nc.psum_tensor(f"m1_te{i}", [128, 1024], BF16))
                     for i in range(2)]

            @block.gpsimd
            def _(gp):
                gp.wait_ge(VE.h, VE.n)
                gp.wait_ge(SE.h, SE.n)
                for s_ in HST.sems:
                    gp.wait_ge(s_.h, s_.n)
                sem = m1w.sems[2]
                gp.dma_start(out=rw_bf[:, :].rearrange("p (k c) -> p k c", k=KB),
                             in_=g["router_w"].rearrange("(k p) c -> p k c", p=128)).then_inc(sem.h, 16)
                gp.dma_start(out=rb_bf[:, :], in_=g["router_b"]).then_inc(sem.h, 16)
                sem.n += 32
            m1r_tok = (2, m1w.sems[2].n)

            rt_pe = {}
            rt_read = 0

            @block.vector
            def _(v):
                v.wait_ge(SE.h, SE.n)
                v.wait_ge(VE.h, VE.n)
                VE.inc(v.tensor_scalar_mul(nrstd2_sb[:, :], rstd2_own[:, :], -1.0))

            for t in range(T_OWN):
                @block.tensor
                def _(te, t=t):
                    if t == 0:
                        te.wait_ge(m1w.sems[2].h, m1r_tok[1])
                        te.wait_ge(VE.h, f3_ht_ve)
                    te.wait_ge(VE.h, rt_read)
                    inst = None
                    for k in range(KB):
                        inst = te.matmul(rt_ps[:, 0:7],
                                         HT[:, (k * T_OWN + t) * 128:(k * T_OWN + t) * 128 + 128],
                                         rw_bf[:, k * 7:(k + 1) * 7],
                                         start=(k == 0), stop=False)
                    inst = te.matmul(rt_ps[:, 0:7], ones_r[:, :], rb_bf[:, :],
                                     start=False, stop=True)
                    rt_pe[t] = PE.inc(inst)

                @block.scalar
                def _(s, t=t):
                    s.wait_ge(PE.h, rt_pe[t])
                    s.wait_ge(VE.h, VE.n)
                    SE.inc(s.activation(pr_sb[:, :], rt_ps[:, 0:7], AF.Exp,
                                        scale=nrstd2_sb[:, t:t + 1]))

                @block.vector
                def _(v, t=t):
                    nonlocal rt_read
                    v.wait_ge(SE.h, SE.n)
                    ch(v, VE, v.tensor_scalar_add(pr_sb[:, :], pr_sb[:, :], 1.0))
                    ch(v, VE, v.reciprocal(pr_sb[:, :], pr_sb[:, :]))
                    eq1 = rsc_sb[:, 0:7]
                    prm = rsc_sb[:, 7:14]
                    selm = rsc_sb[:, 14:21]
                    wraw = rsc_sb[:, 21:28]
                    ch(v, VE, v.tensor_reduce(tt_sb[:, 1:2], pr_sb[:, :], AX.X,
                                              ALU.max))
                    ch(v, VE, v.tensor_scalar(eq1, pr_sb[:, :], tt_sb[:, 1:2],
                                              None, ALU.is_ge))
                    ch(v, VE, v.scalar_tensor_tensor(prm, eq1, -1e9, pr_sb[:, :],
                                                     ALU.mult, ALU.add))
                    ch(v, VE, v.tensor_reduce(tt_sb[:, 2:3], prm, AX.X, ALU.max))
                    ch(v, VE, v.tensor_scalar(selm, pr_sb[:, :], tt_sb[:, 2:3],
                                              None, ALU.is_ge))
                    ch(v, VE, v.tensor_mul(wraw, pr_sb[:, :], selm))
                    ch(v, VE, v.tensor_add(tt_sb[:, 3:4], tt_sb[:, 1:2],
                                           tt_sb[:, 2:3]))
                    ch(v, VE, v.reciprocal(tt_sb[:, 4:5], tt_sb[:, 3:4]))
                    ch(v, VE, v.tensor_scalar_mul(tt_sb[:, 5:6],
                                                  rstd2_own[:, t:t + 1], 0.125))
                    ch(v, VE, v.tensor_scalar(w8c_all[:, t * 8 + 1:t * 8 + 8],
                                              wraw, tt_sb[:, 4:5], tt_sb[:, 5:6],
                                              ALU.mult, ALU.mult))
                    inst = v.tensor_copy(w8c_all[:, t * 8:t * 8 + 1], tt_sb[:, 5:6])
                    VE.inc(inst)
                    rt_read = VE.n
                if t % 4 == 3:
                    block.flush()
            router_ve = VE.n

            gl_tok = {}
            m1_pe = {}
            m1_silu_marks = {}
            m1_he_marks = {}
            m1_tp_marks = {}
            m1_stg_ve = {}
            hes_tok = {}
            pg_read = [0, 0]
            pu_read = [0, 0]
            for e in range(NE):
                for qq in range(2):
                    gi2 = e * 2 + qq
                    sl = gi2 % 2

                    @block.gpsimd
                    def _(gp, e=e, qq=qq, gi2=gi2, sl=sl):
                        if gi2 >= 2:
                            gp.wait_ge(PE.h, m1_pe[(gi2 - 2, T_OWN - 1)])
                        s_ = m1w.sems[sl]
                        for dst, w in ((wgr[sl], g["m_gate"]), (wur[sl], g["m_up"])):
                            src = w[e].rearrange("(k p) c -> p k c", p=128)[
                                :, :, qq * 512:(qq + 1) * 512]
                            gp.dma_start(out=dst[:, :].rearrange(
                                "p (k c) -> p k c", k=KB), in_=src).then_inc(s_.h, 16)
                            s_.n += 16
                        gl_tok[gi2] = (sl, s_.n)

                    for t in range(T_OWN):
                        @block.tensor
                        def _(te, e=e, qq=qq, gi2=gi2, sl=sl, t=t):
                            if t == 0:
                                te.wait_ge(m1w.sems[sl].h, gl_tok[gi2][1])
                                if gi2 == 0:
                                    te.wait_ge(VE.h, router_ve)
                            te.wait_ge(SE.h, pg_read[t % 2])
                            te.wait_ge(VE.h, pu_read[t % 2])
                            inst = None
                            for k in range(KB):
                                te.matmul(pg_ps[t % 2][:, :],
                                          HT[:, (k * T_OWN + t) * 128:(k * T_OWN + t) * 128 + 128],
                                          wgr[sl][:, k * 512:(k + 1) * 512],
                                          start=(k == 0), stop=(k == KB - 1))
                            for k in range(KB):
                                inst = te.matmul(
                                    pu_ps[t % 2][:, :],
                                    HT[:, (k * T_OWN + t) * 128:(k * T_OWN + t) * 128 + 128],
                                    wur[sl][:, k * 512:(k + 1) * 512],
                                    start=(k == 0), stop=(k == KB - 1))
                            m1_pe[(gi2, t)] = PE.inc(inst)

                        @block.scalar
                        def _(s, e=e, gi2=gi2, t=t):
                            s.wait_ge(PE.h, m1_pe[(gi2, t)])
                            pk = (gi2, t - 2) if t >= 2 else (gi2 - 1, 14 + t)
                            s.wait_ge(VE.h, m1_he_marks.get(pk, 0))
                            inst = s.activation(silu_sb[t % 2][:, :],
                                                pg_ps[t % 2][:, :], AF.Silu,
                                                scale=rstd2_own[:, t:t + 1])
                            m1_silu_marks[(gi2, t)] = SE.inc(inst)
                            pg_read[t % 2] = SE.n

                        @block.vector
                        def _(v, e=e, gi2=gi2, t=t):
                            v.wait_ge(SE.h, m1_silu_marks[(gi2, t)])
                            pk = (gi2, t - 2) if t >= 2 else (gi2 - 1, 14 + t)
                            v.wait_ge(PE.h, m1_tp_marks.get(pk, 0))
                            inst = v.scalar_tensor_tensor(
                                he_bf[t % 2][:, :], pu_ps[t % 2][:, :],
                                w8c_all[:, t * 8 + e:t * 8 + e + 1],
                                silu_sb[t % 2][:, :], ALU.mult, ALU.mult)
                            m1_he_marks[(gi2, t)] = VE.inc(inst)
                            pu_read[t % 2] = VE.n

                        def m1_tp(gi2, sl, t):
                            @block.tensor
                            def _(te, gi2=gi2, t=t):
                                te.wait_ge(VE.h, m1_he_marks[(gi2, t)])
                                pk = (gi2, t - 2) if t >= 2 else (gi2 - 1, 14 + t)
                                te.wait_ge(VE.h, m1_stg_ve.get(pk, 0))
                                inst = None
                                for j in range(4):
                                    inst = te.transpose(te_ps[t % 2][:, j * 128:(j + 1) * 128],
                                                        he_bf[t % 2][:, j * 128:(j + 1) * 128],
                                                        id_bf[:, :])
                                m1_tp_marks[(gi2, t)] = PE.inc(inst)

                            @block.vector
                            def _(v, gi2=gi2, sl=sl, t=t):
                                v.wait_ge(PE.h, m1_tp_marks[(gi2, t)])
                                if t == 0 and hes_tok.get(sl):
                                    HES.wait(v, hes_tok[sl])
                                dst = stg[sl][:, :].rearrange("p (f t c) -> p f t c",
                                                              f=4, t=T_OWN)[:, :, t, :]
                                inst = v.tensor_copy(dst, te_ps[t % 2][:, 0:512])
                                m1_stg_ve[(gi2, t)] = VE.inc(inst)

                        if t >= 1:
                            m1_tp(gi2, sl, t - 1)
                        if t == T_OWN - 1:
                            m1_tp(gi2, sl, t)

                    @block.sync
                    def _(sy, e=e, qq=qq, gi2=gi2, sl=sl):
                        sy.wait_ge(VE.h, m1_stg_ve[(gi2, T_OWN - 1)])
                        dst = scr_heT[e][qq * 512:(qq + 1) * 512, :].rearrange(
                            "(f p) c -> p f c", p=128)
                        hes_tok[sl] = HES.load(
                            sy, sl, dst,
                            stg[sl][:, :].rearrange("p (f c) -> p f c", f=4))
                    block.flush()
        HT_scope.close()
        scopes.remove(HT_scope)

        if phases < 8:
            return _finish_early(nc, dbg, block, out_own, g, OST, SYO, MIR, GDO, mir_total, scopes)
        m2w = Slots(nc, top, "m2w", 1)
        m2h = Slots(nc, top, "m2h", 2)
        m2x = Slots(nc, top, "m2x", 4)
        GST = Slots(nc, top, "gst", 2)
        with ExitStack() as ph:
            e0 = ph.enter_context
            wd_sb = [e0(nc.sbuf_tensor(f"m2_wd{i}", [128, 8 * D], BF16))
                     for i in range(4)]
            het = [e0(nc.sbuf_tensor(f"m2_het{i}", [128, 4 * 8 * 128], BF16))
                   for i in range(2)]
            hld_sb = [e0(nc.sbuf_tensor(f"m2_h{i}", [128, D], F32)) for i in range(2)]
            m1ld = [e0(nc.sbuf_tensor(f"m2_m1{i}", [128, D], F32)) for i in range(2)]
            g_sb = [e0(nc.sbuf_tensor(f"m2_g{i}", [128, D], F32)) for i in range(2)]
            acc_ps = [e0(nc.psum_tensor(f"m2_acc{i}", [128, 512], F32))
                      for i in range(8)]

            m2_pe = {}
            m2_ve = {}
            m2_se = {}
            mo_tok = {}
            g2_tok = {}
            wd_tok = {}
            het_tok = {}
            hm_tok = {}
            for half in range(2):
                @block.gpsimd
                def _(gp, half=half):
                    if half == 0:
                        gp.wait_ge(PE.h, PE.n)
                        gp.wait_ge(VE.h, VE.n)
                        gp.wait_ge(SE.h, SE.n)
                        for s_ in HES.sems:
                            gp.wait_ge(s_.h, s_.n)
                        for s_ in HST.sems:
                            gp.wait_ge(s_.h, s_.n)
                    else:
                        gp.wait_ge(PE.h, m2_pe[(0, T_OWN - 1)])
                    s_ = m2w.sems[0]
                    for i in range(4):
                        e_ = half * 4 + i
                        gp.dma_start(out=wd_sb[i][:, :].rearrange(
                            "p (k c) -> p k c", k=8),
                            in_=g["m_down"][e_].rearrange("(k p) c -> p k c", p=128)).then_inc(s_.h, 16)
                        s_.n += 16
                    wd_tok[half] = (0, s_.n)

                for t in range(T_OWN):
                    @block.sync
                    def _(sy, half=half, t=t):
                        sl = t % 2
                        s_ = m2h.sems[sl]
                        if half == 0 and t == 0:
                            for s2 in HES.sems:
                                sy.wait_ge(s2.h, s2.n)
                        if t >= 2:
                            sy.wait_ge(PE.h, m2_pe[(half, t - 2)])
                        for i in range(4):
                            e_ = half * 4 + i
                            src = scr_heT[e_][:, t * 128:(t + 1) * 128].rearrange(
                                "(k p) c -> p k c", p=128)
                            sy.dma_start(
                                out=het[sl][:, :].rearrange(
                                    "p (e k c) -> p e k c", e=4, k=8)[:, i, :, :],
                                in_=src).then_inc(s_.h, 16)
                            s_.n += 16
                        het_tok[(half, t)] = (sl, s_.n)
                        if half == 1:
                            if t >= 2:
                                sy.wait_ge(SE.h, m2_se[t - 2])
                            s2 = m2x.sems[t % 2]
                            sy.dma_start(out=hld_sb[t % 2][:, :],
                                         in_=scr_h[t * 128:(t + 1) * 128, :]).then_inc(s2.h, 16)
                            s2.n += 16
                            s3 = m2x.sems[2 + t % 2]
                            sy.dma_start(out=m1ld[t % 2][:, :],
                                         in_=scr_moe1[t * 128:(t + 1) * 128, :]).then_inc(s3.h, 16)
                            s3.n += 16
                            hm_tok[t] = ((t % 2, s2.n), (2 + t % 2, s3.n))

                    @block.tensor
                    def _(te, half=half, t=t):
                        sl = t % 2
                        if t == 0:
                            te.wait_ge(m2w.sems[0].h, wd_tok[half][1])
                        m2h.wait(te, het_tok[(half, t)])
                        te.wait_ge(VE.h, m2_ve.get((half, t - 2), 0))
                        inst = None
                        for nch in range(4):
                            for i in range(4):
                                for k in range(8):
                                    inst = te.matmul(
                                        acc_ps[(t % 2) * 4 + nch][:, :],
                                        het[sl][:, (i * 8 + k) * 128:(i * 8 + k) * 128 + 128],
                                        wd_sb[i][:, k * D + nch * 512:k * D + nch * 512 + 512],
                                        start=(i == 0 and k == 0),
                                        stop=(i == 3 and k == 7))
                        m2_pe[(half, t)] = PE.inc(inst)

                    if half == 0:
                        @block.vector
                        def _(v, t=t):
                            v.wait_ge(PE.h, m2_pe[(0, t)])
                            if t >= 2:
                                GST.wait(v, mo_tok[t - 2])
                            inst = None
                            for nch in range(4):
                                inst = v.tensor_copy(
                                    g_sb[t % 2][:, nch * 512:(nch + 1) * 512],
                                    acc_ps[(t % 2) * 4 + nch][:, :])
                            m2_ve[(0, t)] = VE.inc(inst)

                        @block.sync
                        def _(sy, t=t):
                            sy.wait_ge(VE.h, m2_ve[(0, t)])
                            mo_tok[t] = GST.load(
                                sy, t, scr_moe1[t * 128:(t + 1) * 128, :],
                                g_sb[t % 2][:, :])
                        if t % 4 == 3:
                            block.flush()
                    else:
                        @block.vector
                        def _(v, t=t):
                            v.wait_ge(PE.h, m2_pe[(1, t)])
                            m2x.wait(v, hm_tok[t][0])
                            m2x.wait(v, hm_tok[t][1])
                            if t >= 2:
                                GST.wait(v, g2_tok[t - 2])
                            else:
                                GST.wait(v, mo_tok[14 + t])
                            inst = None
                            for nch in range(4):
                                sl4 = (t % 2) * 4 + nch
                                cs_ = slice(nch * 512, (nch + 1) * 512)
                                ch(v, VE, v.tensor_add(g_sb[t % 2][:, cs_],
                                                       acc_ps[sl4][:, :],
                                                       m1ld[t % 2][:, cs_]))
                                inst = v.tensor_add(g_sb[t % 2][:, cs_],
                                                    g_sb[t % 2][:, cs_],
                                                    hld_sb[t % 2][:, cs_])
                                VE.inc(inst)
                            m2_ve[(1, t)] = VE.n

                        @block.scalar
                        def _(s, t=t):
                            s.wait_ge(VE.h, m2_ve[(1, t)])
                            inst = s.activation(g_sb[t % 2][:, :], g_sb[t % 2][:, :],
                                                AF.Silu)
                            m2_se[t] = SE.inc(inst)

                        @block.sync
                        def _(sy, t=t):
                            sy.wait_ge(SE.h, m2_se[t])
                            g2_tok[t] = GST.load(
                                sy, t, scr_g[t * 128:(t + 1) * 128, :],
                                g_sb[t % 2][:, :])
                    if t % 4 == 3:
                        block.flush()

            block.flush()

        if phases < 9:
            return _finish_early(nc, dbg, block, out_own, g, OST, SYO, MIR, GDO, mir_total, scopes)
        a2bw = Slots(nc, top, "a2bw", 1)
        a2bx = Slots(nc, top, "a2bx", 6)
        with ExitStack() as ph:
            e0 = ph.enter_context
            xnT3 = e0(nc.sbuf_tensor("a2b_xnT", [128, KB * T_OWN * 128], BF16))
            wg_bf = e0(nc.sbuf_tensor("a2b_wg", [128, KB * D], BF16))
            bg_bf = e0(nc.sbuf_tensor("a2b_bg", [1, D], BF16))
            xfb = [e0(nc.sbuf_tensor(f"a2b_x{i}", [128, D], F32)) for i in range(2)]
            mhb = [e0(nc.sbuf_tensor(f"a2b_mh{i}", [128, D], F32)) for i in range(2)]
            ggb = [e0(nc.sbuf_tensor(f"a2b_g{i}", [128, D], F32)) for i in range(2)]
            xnb = e0(nc.sbuf_tensor("a2b_xn", [128, D], F32))
            gate_sb = e0(nc.sbuf_tensor("a2b_gate", [128, D], F32))
            gp_ps = [e0(nc.psum_tensor(f"a2b_gp{i}", [128, 512], F32)) for i in range(2)]

            @block.gpsimd
            def _(gp):
                gp.wait_ge(VE.h, VE.n)
                gp.wait_ge(SE.h, SE.n)
                for s_ in GST.sems:
                    gp.wait_ge(s_.h, s_.n)
                for s_ in m2h.sems + m2x.sems + m2w.sems:
                    pass
                sem = a2bw.sems[0]
                x3v = xnT3[:, :].rearrange("p (k t c) -> p k t c", t=T_OWN, c=128)
                for t_ in range(T_OWN):
                    gp.dma_start(out=x3v[:, :, t_, :],
                                 in_=scr_xnT[t_ * 128:(t_ + 1) * 128, :].rearrange(
                                     "p (k c) -> p k c", c=128)).then_inc(sem.h, 16)
                gp.dma_start(out=wg_bf[:, :].rearrange("p (k c) -> p k c", k=KB),
                             in_=g["wgT"].rearrange("(k p) c -> p k c", p=128)).then_inc(sem.h, 16)
                gp.dma_start(out=bg_bf[:, :], in_=g["bg"]).then_inc(sem.h, 16)
                sem.n += 18 * 16
            a2b_tok = (0, a2bw.sems[0].n)

            gb_pe = {}
            gb_read = [0, 0]
            ob_ve = {}
            ob_tok = {}
            ld_toks = {}
            for t in range(T_OWN):
                @block.sync
                def _(sy, t=t):
                    if t == 0:
                        sy.wait_ge(PE.h, PE.n)
                        sy.wait_ge(VE.h, VE.n)
                        sy.wait_ge(SE.h, SE.n)
                        for s_ in GST.sems:
                            sy.wait_ge(s_.h, s_.n)
                        for s_ in MHS.sems:
                            sy.wait_ge(s_.h, s_.n)
                    if t >= 2:
                        sy.wait_ge(VE.h, ob_ve[t - 2])
                        OST.wait(sy, ob_tok[t - 2])
                    toks = []
                    for i, (buf, src) in enumerate((
                            (xfb[t % 2], g["x_own"][t * 128:(t + 1) * 128, :]),
                            (mhb[t % 2], scr_mh[t * 128:(t + 1) * 128, :]),
                            (ggb[t % 2], scr_g[t * 128:(t + 1) * 128, :]))):
                        s_ = a2bx.sems[i * 2 + t % 2]
                        sy.dma_start(out=buf[:, :], in_=src).then_inc(s_.h, 16)
                        s_.n += 16
                        toks.append((i * 2 + t % 2, s_.n))
                    ld_toks[t] = toks

                for cq in range(4):
                    @block.tensor
                    def _(te, t=t, cq=cq):
                        if t == 0 and cq == 0:
                            te.wait_ge(a2bw.sems[0].h, a2b_tok[1])
                        te.wait_ge(SE.h, gb_read[cq % 2])
                        inst = None
                        for k in range(KB):
                            inst = te.matmul(
                                gp_ps[cq % 2][:, :],
                                xnT3[:, (k * T_OWN + t) * 128:(k * T_OWN + t) * 128 + 128],
                                wg_bf[:, k * D + cq * 512:k * D + cq * 512 + 512],
                                start=(k == 0), stop=False)
                        inst = te.matmul(gp_ps[cq % 2][:, :], ones_r[:, :],
                                         bg_bf[:, cq * 512:(cq + 1) * 512],
                                         start=False, stop=True)
                        gb_pe[(t, cq)] = PE.inc(inst)

                    @block.scalar
                    def _(s, t=t, cq=cq):
                        s.wait_ge(PE.h, gb_pe[(t, cq)])
                        if cq == 0:
                            s.wait_ge(VE.h, ob_ve.get(t - 1, 0))
                        inst = s.activation(gate_sb[:, cq * 512:(cq + 1) * 512],
                                            gp_ps[cq % 2][:, :], AF.Sigmoid)
                        gb_read[cq % 2] = SE.inc(inst)
                gate_se = SE.n

                @block.scalar
                def _(s, t=t):
                    for tok in ld_toks[t]:
                        a2bx.wait(s, tok)
                    SE.inc(s.activation(xnb[:, :], xfb[t % 2][:, :], AF.Identity,
                                        scale=rstd_own[:, t:t + 1],
                                        bias=nmr_own[:, t:t + 1]))
                xn_se2 = SE.n

                @block.vector
                def _(v, t=t):
                    v.wait_ge(SE.h, xn_se2)
                    o = mhb[t % 2]
                    ch(v, VE, v.tensor_mul(o[:, :], gate_sb[:, :], o[:, :]))
                    ch(v, VE, v.tensor_add(o[:, :], o[:, :], xnb[:, :]))
                    ch(v, VE, v.tensor_mul(o[:, :], o[:, :], ggb[t % 2][:, :]))
                    inst = v.tensor_add(o[:, :], o[:, :], xfb[t % 2][:, :])
                    ob_ve[t] = VE.inc(inst)

                @block.sync
                def _(sy, t=t):
                    sy.wait_ge(VE.h, ob_ve[t])
                    ob_tok[t] = OST.load(sy, t, out_own[t * 128:(t + 1) * 128, :],
                                         mhb[t % 2][:, :])
                if t % 4 == 3:
                    block.flush()
            block.flush()

        @block.sync
        def _(sy):
            for s_ in OST.sems:
                sy.wait_ge(s_.h, s_.n)
            sy.wait_ge(SYO.h, SYO.n)
            sy.wait_ge(MIR.h, mir_total)

        @block.gpsimd
        def _(gp):
            gp.wait_ge(GDO.h, GDO.n)
        block.flush()

    return nc, dbg



def kernel(**inputs):
    in_maps = _prepare(inputs)
    nc, dbg = _build(debug=_DEBUG)
    res = run_bass_kernel_spmd(nc, in_maps, core_ids=list(range(8)))
    out = np.empty((4, S, D), np.float32)
    for c in range(8):
        b, half = c // 2, c % 2
        out[b, half * TOK:(half + 1) * TOK] = np.asarray(res.results[c]["out_own"],
                                                         np.float32)
    if _DEBUG:
        taps = [{k: np.asarray(res.results[c][k]) for k in dbg} for c in range(8)]
        return out, taps
    return out
